# version 01
# baseline (speedup 1.0000x reference)
import contextlib
import numpy as np
import ml_dtypes
import concourse.bass as bass
import concourse.mybir as mybir
from concourse.bass_utils import run_bass_kernel_spmd

F32 = mybir.dt.float32
BF16 = mybir.dt.bfloat16
I32 = mybir.dt.int32
AF = mybir.ActivationFunctionType
ALU = mybir.AluOpType
AX = mybir.AxisListType

B, L, D = 4, 2048, 2048
NCH = D // 128
TKV = 1152
TQ = 1024
QOFF = 64
NH, DH, NS = 32, 64, 16
UP = 4096
UPCH = UP // 128
QKV_OC = 3 * NCH
NEG = -1.0e30
NT_KV = ((0, 512), (512, 512), (1024, 128))
NT_Q = ((64, 512), (576, 512))

ENGS = ("T", "S", "V", "P", "Q")


class Trk:
    def __init__(self):
        self.prog = {e: [] for e in ENGS}
        self.cnt = {e: 0 for e in ENGS}
        self.qcnt = {}
        self.writer = {}
        self.readers = {}
        self.waited = {e: {} for e in ENGS}
        self.pfence = {}
        self.sems = {}
        import os
        self.log = open("/tmp/trk.log", "w") if os.environ.get("TRKLOG") else None

    def _add_wait(self, e, oe, val):
        if val <= 0:
            return
        if self.waited[e].get(oe, 0) >= val:
            return
        self.waited[e][oe] = val
        sems = self.sems
        self.prog[e].append(lambda eng, oe=oe, val=val: eng.wait_ge(sems[oe], val))

    def emit(self, e, fn, reads=(), writes=(), tag="", qsem=None):
        need = {}
        for r in set(reads) | set(writes):
            p = r.split("/", 1)[0]
            marks = self.pfence.get(p)
            if marks:
                for (oe, c) in marks:
                    if oe != e:
                        need[oe] = max(need.get(oe, 0),
                                       self.qcnt[oe] if oe in self.qcnt else c)
        skipT = (e == "T")

        def mval(oe, c):
            return self.qcnt[oe] if oe in self.qcnt else c

        for r in reads:
            w = self.writer.get(r)
            if w is not None and not (skipT and w[0] == "T"):
                need[w[0]] = max(need.get(w[0], 0), mval(*w))
        for r in writes:
            w = self.writer.get(r)
            if w is not None and not (skipT and w[0] == "T"):
                need[w[0]] = max(need.get(w[0], 0), mval(*w))
            for oe, c in self.readers.get(r, {}).items():
                if not (skipT and oe == "T"):
                    need[oe] = max(need.get(oe, 0), mval(oe, c))
        for oe, v in need.items():
            self._add_wait(e, oe, v)
        if e == "Q":
            assert qsem is not None
            self.qcnt[qsem] = self.qcnt.get(qsem, 0) + 16
            me = self.qcnt[qsem]
            mark = (qsem, me)
            sems = self.sems
            self.prog["Q"].append(lambda eng, fn=fn, qsem=qsem: fn(eng).then_inc(sems[qsem], 16))
        else:
            self.cnt[e] += 1
            me = self.cnt[e]
            mark = (e, me)
            sems = self.sems
            self.prog[e].append(lambda eng, fn=fn, e=e: fn(eng).then_inc(sems[e], 1))
        if self.log is not None:
            self.log.write(f"{e}/{qsem} cnt={me} waits={need} reads={reads} writes={writes} {tag}\n")
        for r in writes:
            self.writer[r] = mark
            self.readers[r] = {}
        for r in reads:
            self.readers.setdefault(r, {})[mark[0]] = me
        return mark

    def settle(self, qsem, regions):
        total = self.qcnt.get(qsem, 0)
        for r in regions:
            w = self.writer.get(r)
            if w is not None and w[0] == qsem:
                self.writer[r] = (qsem, total)

    def marks_for(self, prefix):
        out = []
        for r, w in list(self.writer.items()):
            if r == prefix or r.startswith(prefix + "/"):
                if w is not None:
                    out.append(w)
                for oe, c in self.readers.get(r, {}).items():
                    out.append((oe, c))
        return out

    def clear_prefix(self, prefix):
        self.pfence.pop(prefix, None)
        for r in list(self.writer.keys()):
            if r == prefix or r.startswith(prefix + "/"):
                del self.writer[r]
                self.readers.pop(r, None)

    def fence(self, region, marks):
        self.pfence[region] = list(marks)


class Mem:
    def __init__(self, nc, trk):
        self.nc = nc
        self.trk = trk
        self.stack = {"sl": [], "sr": [], "p": []}
        self.serial = 0

    def push(self, which, name, shape, dtype, fence_marks=None):
        self.serial += 1
        tname = f"{name}__{self.serial}"
        if which == "p":
            g = self.nc.psum_tensor(tname, shape, dtype)
        else:
            g = self.nc.sbuf_tensor(tname, shape, dtype,
                                    side=("left" if which == "sl" else "right"))
        h = g.__enter__()
        self.stack[which].append((name, g))
        if fence_marks:
            self.trk.fence(name, fence_marks)
        return h

    def pop(self, which, name):
        top, g = self.stack[which].pop()
        assert top == name, f"LIFO violation: pop {name} but top is {top}"
        marks = self.trk.marks_for(name)
        self.trk.clear_prefix(name)
        g.__exit__(None, None, None)
        return marks


def _hadamard128():
    i = np.arange(128)
    pc = np.bitwise_count(i[:, None] & i[None, :]) if hasattr(np, "bitwise_count") else \
        np.array([bin(a & b).count("1") for a in i for b in i]).reshape(128, 128)
    return np.where(pc % 2 == 0, np.float32(1.0), np.float32(-1.0)).astype(np.float32)


def _pswap():
    p = np.zeros((128, 128), np.float32)
    for m in range(128):
        k = m + 32 if (m % 64) < 32 else m - 32
        p[k, m] = 1.0
    return p


def _trig_tables(qstart):
    import jax
    import jax.numpy as jnp
    cpu = jax.devices("cpu")[0]
    with jax.default_device(cpu):
        half = 32
        inv_freq = 1.0 / 10000.0 ** (jnp.arange(half, dtype=jnp.float32) / half)

        def tables(p0, n, scale):
            pos = jnp.arange(p0, p0 + n, dtype=jnp.float32)
            ang = pos[None, :] * inv_freq[:, None]
            c = np.asarray(jnp.cos(ang), np.float32)
            s = np.asarray(jnp.sin(ang), np.float32)
            C = np.tile(c, (4, 1)) * scale
            S = np.tile(s, (4, 1)) * scale
            sign = np.where((np.arange(128) % 64) < 32, np.float32(-1.0), np.float32(1.0))
            return np.ascontiguousarray(C), np.ascontiguousarray(S * sign[:, None])

        ck, sk = tables(qstart - 64, TKV, np.float32(1.0))
        cq, sq = tables(qstart, TQ, np.float32(0.125))
    return cq, sq, ck, sk


def _masks(qstart):
    out = []
    for g in range(2):
        m = np.zeros((128, 4, 272), np.float32)
        for i in range(4):
            qb = g * 4 + i
            q = np.arange(128)[:, None]
            jw = np.arange(256)[None, :]
            valid = (jw >= q) & (jw <= q + 128)
            kpos = qstart - 64 + qb * 128 + jw
            valid = valid & (kpos >= 0) & (kpos < L)
            m[:, i, 16:] = np.where(valid, np.float32(0.0), np.float32(NEG))
        out.append(np.ascontiguousarray(m.reshape(128, 4 * 272)).astype(ml_dtypes.bfloat16))
    return out


def _strips(w, n_oc, n_kc):
    return np.ascontiguousarray(
        w.reshape(n_oc, 128, n_kc, 128).transpose(0, 3, 2, 1).reshape(n_oc, 128, n_kc * 128))


class _Stop(Exception):
    pass


def build_program(taps=(), stop_after=None):
    nc = bass.Bass("TRN2", target_bir_lowering=False)
    trk = Trk()
    mem = Mem(nc, trk)
    E = trk.emit
    Q = "Q"

    dI = lambda n, s, dt=F32: nc.dram_tensor(n, s, dt, kind="ExternalInput").ap()
    dO = lambda n, s, dt=F32: nc.dram_tensor(n, s, dt, kind="ExternalOutput").ap()

    u_T = dI("u_T", [D, TKV])
    wqkv = dI("wqkv", [QKV_OC, 128, NCH * 128])
    bqkv = dI("bqkv_t", [128, QKV_OC])
    wo = dI("wo", [NCH, 128, NCH * 128])
    bo_in = dI("bo_t", [128, NCH])
    wup = dI("wup", [UPCH, 128, NCH * 128])
    wdn = dI("wdn", [NCH, 128, UPCH * 128])
    sinkT_in = dI("sinkT", [128, NCH, NS])
    sinkV_in = dI("sinkV", [NS, NH * DH])
    cq_in = dI("cq", [128, TQ])
    sq_in = dI("sq", [128, TQ])
    ck_in = dI("ck", [128, TKV])
    sk_in = dI("sk", [128, TKV])
    mask0_in = dI("mask0", [128, 4 * 272], BF16)
    mask1_in = dI("mask1", [128, 4 * 272], BF16)
    h128_in = dI("h128bf", [128, 128], BF16)
    h128f_in = dI("h128f", [128, 128])
    iden_in = dI("idenbf", [128, 128], BF16)
    idenf_in = dI("idenf", [128, 128])
    pswap_in = dI("pswap", [128, 128], BF16)
    consts_in = dI("consts", [128, 8])
    onesf_in = dI("onesf", [1, 128])
    onesc_in = dI("onesc", [128, 1])

    out_T = dO("out_T", [D, TQ])

    wup_cache = nc.dram_tensor("wup_cache", [UPCH, 128, NCH * 128], BF16, kind="Internal").ap()
    wdn_cache = nc.dram_tensor("wdn_cache", [NCH, 128, UPCH * 128], BF16, kind="Internal").ap()

    tapset = set(taps)
    tap_out = {}
    for nm, shape, dt in [("xn", [128, NCH, TKV], BF16), ("q", [128, NCH, TQ], BF16),
                          ("k", [128, NCH, TKV], BF16), ("v", [128, 9, NH * DH], BF16),
                          ("y", [128, NCH, TQ], BF16), ("accwo", [128, NCH, TQ], F32),
                          ("xq", [128, NCH, 512], BF16), ("h", [128, UPCH, 512], BF16),
                          ("hq", [128, UPCH, 512], BF16), ("sw", [128, 2], F32),
                          ("pe", [128, 4, 272], F32), ("pe2", [128, 4, 272], F32),
                          ("pb", [128, 4, 272], BF16), ("pt", [128, 4, 384], BF16),
                          ("ysu", [64, 512], BF16),
                          ("sr0", [128, 16], F32), ("mj", [128, 8], F32),
                          ("rows", [1, TKV * 2], F32)]:
        if nm in tapset:
            tap_out[nm] = dO("tap_" + nm, shape, dt)

    def ckpt(ph):
        if stop_after == ph:
            raise _Stop()

    P = {}
    for nm, shape, dt in [
        ("consts", [128, 8], F32), ("onesc", [128, 1], F32),
        ("onesbf", [128, 1], BF16), ("idenbf", [128, 128], BF16), ("idenf", [128, 128], F32),
        ("pswap", [128, 128], BF16), ("h128bf", [128, 128], BF16), ("h128f", [128, 128], F32),
        ("bq", [128, QKV_OC], F32), ("bo", [128, NCH], F32),
        ("rsc2048", [128, 1], F32),
        ("swup", [128, 1], F32), ("thrup", [128, 1], F32), ("nthrup", [128, 1], F32),
        ("swdn", [128, 1], F32), ("thrdn", [128, 1], F32), ("nthrdn", [128, 1], F32),
        ("wacc", [128, 2], F32), ("wsums", [128, 64], F32),
        ("onesf", [1, 128], F32), ("mrow", [1, TKV], F32), ("rrow", [1, TKV], F32),
        ("sc2048", [1, 1], F32), ("ln2048", [1, 1], F32),
    ]:
        P[nm] = mem.push("sl", "sb_" + nm, shape, dt)

    for val, col in [(0.0, 0), (1.0, 1), (1e-5, 2), (-0.5, 3), (2048.0, 4), (0.015625, 5), (0.5, 6)]:
        nc.const_aps.register(F32, val, P["consts"][:, col:col + 1])

    try:
        for nm, src in [("consts", consts_in), ("onesf", onesf_in), ("onesc", onesc_in),
                        ("idenbf", iden_in), ("idenf", idenf_in), ("pswap", pswap_in),
                        ("h128bf", h128_in), ("h128f", h128f_in), ("bq", bqkv), ("bo", bo_in)]:
            E(Q, lambda g, d=P[nm], s=src: g.dma_start(out=d[:], in_=s[:, :]), writes=(nm,), qsem="qx")
        trk.settle("qx", ("consts", "onesf", "onesc", "idenbf", "idenf", "pswap",
                          "h128bf", "h128f", "bq", "bo"))
        E("V", lambda g: g.tensor_copy(out=P["onesbf"][:], in_=P["onesc"][:]),
          reads=("onesc",), writes=("onesbf",))
        E("S", lambda g: g.activation(out=P["ln2048"][:], in_=P["consts"][0:1, 4:5], func=AF.Ln,
                                      bias=0.0, scale=1.0), reads=("consts",), writes=("ln2048",))
        E("S", lambda g: g.activation(out=P["sc2048"][:], in_=P["ln2048"][:], func=AF.Exp,
                                      bias=0.0, scale=0.5), reads=("ln2048",), writes=("sc2048",))

        xn = mem.push("sl", "xn", [128, NCH, TKV], BF16)

        ubuf = mem.push("sr", "ubuf", [128, 2, TKV], F32)
        sqb = mem.push("sr", "sqb", [128, 2, TKV], BF16)
        ubfb = mem.push("sr", "ubfb", [128, 2, TKV], BF16)
        arow = mem.push("sr", "arow", [1, 4 * TKV], F32)
        RB = mem.push("sr", "RB", [128, 2, TKV], F32)

        ps_st1 = mem.push("p", "ps_st1", [1, TKV], F32)
        ps_st2 = mem.push("p", "ps_st2", [1, TKV], F32)

        for c in range(NCH):
            bi = c % 2
            E(Q, lambda g, c=c, bi=bi: g.dma_start(out=ubuf[:, bi, :], in_=u_T[c * 128:(c + 1) * 128, :]),
              writes=(f"ubuf/{bi}",), qsem=f"qu{bi}")
            E("V", lambda g, bi=bi: g.tensor_copy(out=ubfb[:, bi, :], in_=ubuf[:, bi, :]),
              reads=(f"ubuf/{bi}",), writes=(f"ubfb/{bi}",))
            E("P", lambda g, bi=bi: g.tensor_mul(out=sqb[:, bi, :], in0=ubfb[:, bi, :], in1=ubfb[:, bi, :]),
              reads=(f"ubfb/{bi}",), writes=(f"sqb/{bi}",))
            for (o, n) in NT_KV:
                E("T", lambda g, c=c, bi=bi, o=o, n=n: g.matmul(
                    ps_st1[:, o:o + n], P["onesbf"][:], ubfb[:, bi, o:o + n],
                    start=(c == 0), stop=(c == NCH - 1)),
                  reads=(f"ubfb/{bi}", "onesbf"), writes=(f"ps_st1/{o}",))
                E("T", lambda g, c=c, bi=bi, o=o, n=n: g.matmul(
                    ps_st2[:, o:o + n], P["onesbf"][:], sqb[:, bi, o:o + n],
                    start=(c == 0), stop=(c == NCH - 1)),
                  reads=(f"sqb/{bi}", "onesbf"), writes=(f"ps_st2/{o}",))

        mr, rr = P["mrow"], P["rrow"]
        e2 = arow[0:1, 0:TKV]
        var = arow[0:1, TKV:2 * TKV]
        lnv = arow[0:1, 2 * TKV:3 * TKV]
        brow = arow[0:1, 3 * TKV:4 * TKV]
        st1keys = tuple(f"ps_st1/{o}" for (o, n) in NT_KV)
        st2keys = tuple(f"ps_st2/{o}" for (o, n) in NT_KV)
        E("V", lambda g: g.tensor_scalar(out=mr[:], in0=ps_st1[:, :], scalar1=1.0 / 2048.0,
                                         scalar2=None, op0=ALU.mult), reads=st1keys, writes=("mrow",))
        E("V", lambda g: g.tensor_scalar(out=e2, in0=ps_st2[:, :], scalar1=1.0 / 2048.0,
                                         scalar2=None, op0=ALU.mult), reads=st2keys, writes=("arow/e2",))
        E("V", lambda g: g.tensor_mul(out=var, in0=mr[:], in1=mr[:]), reads=("mrow",), writes=("arow/var",))
        E("V", lambda g: g.tensor_sub(out=var, in0=e2, in1=var),
          reads=("arow/e2", "arow/var"), writes=("arow/var",))
        E("S", lambda g: g.activation(out=lnv, in_=var, func=AF.Ln, bias=1e-5, scale=1.0),
          reads=("arow/var",), writes=("arow/lnv",))
        E("S", lambda g: g.activation(out=rr[:], in_=lnv, func=AF.Exp, bias=0.0, scale=-0.5),
          reads=("arow/lnv",), writes=("rrow",))
        E("V", lambda g: g.tensor_mul(out=brow, in0=mr[:], in1=rr[:]),
          reads=("mrow", "rrow"), writes=("arow/brow",))
        E("V", lambda g: g.tensor_scalar(out=brow, in0=brow, scalar1=-1.0, scalar2=None, op0=ALU.mult),
          reads=("arow/brow",), writes=("arow/brow",))
        if "rows" in tap_out:
            E(Q, lambda g: g.dma_start(out=tap_out["rows"][0:1, 0:TKV], in_=mr[:]), reads=("mrow",), qsem="qout")
            E(Q, lambda g: g.dma_start(out=tap_out["rows"][0:1, TKV:2 * TKV], in_=rr[:]), reads=("rrow",), qsem="qout")
        E("S", lambda g: g.activation(out=arow[0:1, 0:1], in_=P["ln2048"][:], func=AF.Exp,
                                      bias=0.0, scale=-0.5),
          reads=("ln2048",), writes=("arow/rsc",))

        f_st2 = mem.pop("p", "ps_st2")
        f_st1 = mem.pop("p", "ps_st1")
        ps_rb = mem.push("p", "ps_rb", [128, 2, 1536], F32, fence_marks=f_st1 + f_st2)
        for (o, n) in NT_KV:
            E("T", lambda g, o=o, n=n: g.matmul(ps_rb[:, 0, o:o + n], P["onesf"][:], rr[0:1, o:o + n],
                                                start=True, stop=True),
              reads=("rrow", "onesf"), writes=(f"ps_rb/0/{o}",))
            E("T", lambda g, o=o, n=n: g.matmul(ps_rb[:, 1, o:o + n], P["onesf"][:],
                                                arow[0:1, 3 * TKV + o:3 * TKV + o + n],
                                                start=True, stop=True),
              reads=("arow/brow", "onesf"), writes=(f"ps_rb/1/{o}",))
        E("T", lambda g: g.matmul(ps_rb[:, 0, TKV:TKV + 1], P["onesf"][:], arow[0:1, 0:1],
                                  start=True, stop=True),
          reads=("arow/rsc", "onesf"), writes=("ps_rb/rsc",))
        E("V", lambda g: g.tensor_copy(out=P["rsc2048"][:], in_=ps_rb[:, 0, TKV:TKV + 1]),
          reads=("ps_rb/rsc",), writes=("rsc2048",))
        rbk0 = tuple(f"ps_rb/0/{o}" for (o, n) in NT_KV)
        rbk1 = tuple(f"ps_rb/1/{o}" for (o, n) in NT_KV)
        E("V", lambda g: g.tensor_copy(out=RB[:, 0, :], in_=ps_rb[:, 0, 0:TKV]), reads=rbk0, writes=("RB/0",))
        E("V", lambda g: g.tensor_copy(out=RB[:, 1, :], in_=ps_rb[:, 1, 0:TKV]), reads=rbk1, writes=("RB/1",))

        for c in range(NCH):
            bi = c % 2
            E(Q, lambda g, c=c, bi=bi: g.dma_start(out=ubuf[:, bi, :], in_=u_T[c * 128:(c + 1) * 128, :]),
              writes=(f"ubuf/{bi}",), qsem=f"qu{bi}")
            eng = "V" if c % 2 == 0 else "P"
            E(eng, lambda g, bi=bi: g.tensor_mul(out=ubuf[:, bi, :], in0=ubuf[:, bi, :], in1=RB[:, 0, :]),
              reads=(f"ubuf/{bi}", "RB/0"), writes=(f"ubuf/{bi}",))
            E(eng, lambda g, c=c, bi=bi: g.tensor_add(out=xn[:, c, :], in0=ubuf[:, bi, :], in1=RB[:, 1, :]),
              reads=(f"ubuf/{bi}", "RB/1"), writes=(f"xn/{c}",))

        xnkeys = tuple(f"xn/{c}" for c in range(NCH))
        if "xn" in tap_out:
            E(Q, lambda g: g.dma_start(out=tap_out["xn"][:, :, :], in_=xn[:, :, :]), reads=xnkeys, qsem="qout")
        ckpt("A")

        f_rb = mem.pop("p", "ps_rb")
        fA = []
        for nm in ("RB", "arow", "ubfb", "sqb", "ubuf"):
            fA += mem.pop("sr", nm)

        q_sb = mem.push("sr", "q", [128, NCH, TQ], BF16, fence_marks=fA)
        k_sb = mem.push("sr", "k", [128, NCH, TKV], BF16, fence_marks=fA)
        v_tm = mem.push("sr", "V", [128, 9, NH * DH], BF16, fence_marks=fA)
        wst = mem.push("sr", "wst", [128, NCH * 128], F32, fence_marks=fA)
        wbf = mem.push("sr", "wbf", [128, 2, NCH * 128], BF16, fence_marks=fA)
        vbf = mem.push("sr", "vbf", [128, 2, 512], BF16, fence_marks=fA)

        trig = mem.push("sl", "trig", [128, 2, TKV], F32)
        rsc = mem.push("sl", "rsc", [128, 2, 512], F32)

        ps_qkv = mem.push("p", "ps_qkv", [128, 2, 3, 512], F32, fence_marks=f_rb)
        ps_sw = mem.push("p", "ps_sw", [128, 512], F32, fence_marks=f_rb)
        ps_vt = mem.push("p", "ps_vt", [128, 1024], BF16, fence_marks=f_rb)

        def cvt(e, dst, src, rkey, wkey):
            if e == "S":
                E(e, lambda g: g.activation(out=dst, in_=src, func=AF.Copy, bias=0.0, scale=1.0),
                  reads=(rkey,) if isinstance(rkey, str) else rkey,
                  writes=(wkey,) if isinstance(wkey, str) else wkey)
            else:
                E(e, lambda g: g.tensor_copy(out=dst, in_=src),
                  reads=(rkey,) if isinstance(rkey, str) else rkey,
                  writes=(wkey,) if isinstance(wkey, str) else wkey)

        E(Q, lambda g: g.dma_start(out=trig[:, 0, 0:TQ], in_=cq_in[:, :]), writes=("trig/c",), qsem="qtr")
        E(Q, lambda g: g.dma_start(out=trig[:, 1, 0:TQ], in_=sq_in[:, :]), writes=("trig/s",), qsem="qtr")
        trk.settle("qtr", ("trig/c", "trig/s"))

        def rope_chunk(c, kind):
            tens = q_sb if kind == "q" else k_sb
            tiles = NT_Q if kind == "q" else NT_KV
            toff = QOFF if kind == "q" else 0
            for (o, n) in tiles:
                co = o - toff
                E("T", lambda g, c=c, o=o, n=n, co=co, tens=tens: g.matmul(
                    ps_sw[:, 0:n], P["pswap"][:], tens[:, c, co:co + n], start=True, stop=True),
                  reads=(f"{kind}/n/{c}", "pswap"), writes=("ps_sw",))
                E("V", lambda g, n=n, co=co: g.tensor_tensor(
                    out=rsc[:, 0, 0:n], in0=ps_sw[:, 0:n], in1=trig[:, 1, co:co + n], op=ALU.mult),
                  reads=("ps_sw", "trig/s"), writes=("rsc/m",))
                E("P", lambda g, c=c, n=n, co=co, tens=tens: g.tensor_tensor(
                    out=rsc[:, 1, 0:n], in0=tens[:, c, co:co + n], in1=trig[:, 0, co:co + n], op=ALU.mult),
                  reads=(f"{kind}/n/{c}", "trig/c"), writes=("rsc/t1",))
                E("V", lambda g, c=c, n=n, co=co, tens=tens: g.tensor_add(
                    out=tens[:, c, co:co + n], in0=rsc[:, 1, 0:n], in1=rsc[:, 0, 0:n]),
                  reads=("rsc/t1", "rsc/m"), writes=(f"{kind}/{c}",))

        conv_eng = ("S", "V", "P")
        for oc in range(QKV_OC):
            wb = oc % 2
            E(Q, lambda g, oc=oc: g.dma_start(out=wst[:, :], in_=wqkv[oc, :, :]), writes=("wst",), qsem="qw")
            cvt(conv_eng[oc % 3], wbf[:, wb, :], wst[:, :], "wst", f"wbf/{wb}")
            kind = "q" if oc < NCH else ("k" if oc < 2 * NCH else "v")
            c = oc % NCH
            tiles = NT_Q if kind == "q" else NT_KV
            pp = oc % 2
            for ni, (o, n) in enumerate(tiles):
                for kc in range(NCH):
                    E("T", lambda g, wb=wb, kc=kc, pp=pp, ni=ni, o=o, n=n: g.matmul(
                        ps_qkv[:, pp, ni, 0:n], wbf[:, wb, kc * 128:(kc + 1) * 128],
                        xn[:, kc, o:o + n], start=(kc == 0), stop=(kc == NCH - 1)),
                      reads=(f"wbf/{wb}", f"xn/{kc}"), writes=(f"ps_qkv/{pp}/{ni}",))
            if kind in ("q", "k"):
                tens = q_sb if kind == "q" else k_sb
                toff = QOFF if kind == "q" else 0
                for ni, (o, n) in enumerate(tiles):
                    E("S", lambda g, oc=oc, c=c, pp=pp, ni=ni, o=o, n=n, toff=toff, tens=tens: g.activation(
                        out=tens[:, c, o - toff:o - toff + n], in_=ps_qkv[:, pp, ni, 0:n],
                        func=AF.Identity, bias=P["bq"][:, oc:oc + 1], scale=1.0),
                      reads=(f"ps_qkv/{pp}/{ni}", "bq"), writes=(f"{kind}/n/{c}",))
                rope_chunk(c, kind)
                if kind == "q" and c == NCH - 1:
                    E(Q, lambda g: g.dma_start(out=trig[:, 0, :], in_=ck_in[:, :]), writes=("trig/c",), qsem="qtr")
                    E(Q, lambda g: g.dma_start(out=trig[:, 1, :], in_=sk_in[:, :]), writes=("trig/s",), qsem="qtr")
                    trk.settle("qtr", ("trig/c", "trig/s"))
            else:
                vc = oc - 2 * NCH
                for ni, (o, n) in enumerate(tiles):
                    vb = ni % 2
                    E("S", lambda g, oc=oc, pp=pp, ni=ni, n=n, vb=vb: g.activation(
                        out=vbf[:, vb, 0:n], in_=ps_qkv[:, pp, ni, 0:n],
                        func=AF.Identity, bias=P["bq"][:, oc:oc + 1], scale=1.0),
                      reads=(f"ps_qkv/{pp}/{ni}", "bq"), writes=(f"vbf/{vb}",))
                    ntp = n // 128
                    for j in range(ntp):
                        E("T", lambda g, vb=vb, j=j: g.transpose(
                            ps_vt[:, j * 128:(j + 1) * 128], vbf[:, vb, j * 128:(j + 1) * 128],
                            P["idenbf"][:]),
                          reads=(f"vbf/{vb}", "idenbf"), writes=(f"ps_vt/{j}",))
                    jt0 = o // 128
                    E("V", lambda g, vc=vc, jt0=jt0, n=n, ntp=ntp: g.tensor_copy(
                        out=v_tm[:, jt0:jt0 + ntp, vc * 128:(vc + 1) * 128],
                        in_=ps_vt[:, 0:n].rearrange("p (j f) -> p j f", f=128)),
                      reads=tuple(f"ps_vt/{j}" for j in range(ntp)), writes=(f"V/{vc}",))

        qkeys = tuple(f"q/{c}" for c in range(NCH))
        kkeys = tuple(f"k/{c}" for c in range(NCH))
        vkeys = tuple(f"V/{vc}" for vc in range(NCH))
        if "q" in tap_out:
            E(Q, lambda g: g.dma_start(out=tap_out["q"][:, :, :], in_=q_sb[:, :, :]), reads=qkeys, qsem="qout")
        if "k" in tap_out:
            E(Q, lambda g: g.dma_start(out=tap_out["k"][:, :, :], in_=k_sb[:, :, :]), reads=kkeys, qsem="qout")
        if "v" in tap_out:
            E(Q, lambda g: g.dma_start(out=tap_out["v"][:, :, :], in_=v_tm[:, :, :]), reads=vkeys, qsem="qout")
        ckpt("B")

        fB_ps = []
        for nm in ("ps_vt", "ps_sw", "ps_qkv"):
            fB_ps += mem.pop("p", nm)
        fB_l = mem.pop("sl", "rsc") + mem.pop("sl", "trig")
        fB_l += mem.pop("sl", "xn")
        y_sb = mem.push("sl", "y", [128, NCH, TQ], BF16, fence_marks=fB_l)
        fB_r = []
        for nm in ("vbf", "wbf", "wst"):
            fB_r += mem.pop("sr", nm)

        masks = mem.push("sr", "masks", [128, 2, 4 * 272], BF16, fence_marks=fB_r)
        sbf = mem.push("sr", "sbf", [128, NCH, NS], BF16, fence_marks=fB_r)
        svbf = mem.push("sr", "svbf", [NS, NH * DH], BF16, fence_marks=fB_r)
        stf = mem.push("sr", "stf", [128, NCH, NS], F32, fence_marks=fB_r)
        pe_sc = mem.push("sr", "pe", [128, 2, 4, 272], F32, fence_marks=fB_r)
        pe2_sc = mem.push("sr", "pe2", [128, 2, 4, 272], F32, fence_marks=fB_r)
        pb_sc = mem.push("sr", "pb", [128, 2, 4, 272], BF16, fence_marks=fB_r)
        pt_sb = mem.push("sr", "pt", [128, 2, 4, 384], BF16, fence_marks=fB_r)
        srow = mem.push("sr", "srow", [128, 2, 16], F32, fence_marks=fB_r)
        mst = mem.push("sr", "mst", [128, 2, 2048], F32, fence_marks=fB_r)

        E(Q, lambda g: g.dma_start(out=masks[:, 0, :], in_=mask0_in[:, :]), writes=("masks/0",), qsem="qx2")
        E(Q, lambda g: g.dma_start(out=masks[:, 1, :], in_=mask1_in[:, :]), writes=("masks/1",), qsem="qx2")
        E(Q, lambda g: g.dma_start(out=stf[:, :, :], in_=sinkT_in[:, :, :]), writes=("stf",), qsem="qx2")
        E(Q, lambda g: g.dma_start(out=mst[0:NS, 0, :], in_=sinkV_in[:, 0:2048]), writes=("mst/0",), qsem="qx2")
        trk.settle("qx2", ("masks/0", "masks/1", "stf", "mst/0"))
        E("V", lambda g: g.tensor_copy(out=sbf[:, :, :], in_=stf[:, :, :]), reads=("stf",), writes=("sbf",))
        E("V", lambda g: g.tensor_copy(out=svbf[:, :], in_=mst[0:NS, 0, :]), reads=("mst/0",), writes=("svbf",))

        ps_S = mem.push("p", "ps_S", [128, 4, 512], F32, fence_marks=fB_ps)
        ps_PT = mem.push("p", "ps_PT", [128, 4, 512], BF16, fence_marks=fB_ps)
        E("S", lambda g: g.memzero(ps_PT[:, :, :]),
          writes=tuple(f"ps_PT/{i}" for i in range(4)))
        ps_Y = mem.push("p", "ps_Y", [64, 2, 512], F32, fence_marks=fB_ps)

        mean_jobs = [("up", oc, 0) for oc in range(UPCH)] + \
                    [("dn", oc, off) for oc in range(NCH) for off in (0, 2048)]

        def emit_mean(idx):
            which, oc, off = mean_jobs[idx]
            bi = idx % 2
            src = wup[oc, :, :] if which == "up" else wdn[oc, :, off:off + 2048]
            wcol = 0 if which == "up" else 1
            E(Q, lambda g, src=src, bi=bi: g.dma_start(out=mst[:, bi, :], in_=src), writes=(f"mst/{bi}",), qsem=f"qm{bi}")
            E("V", lambda g, bi=bi, idx=idx: g.reduce_sum(out=P["wsums"][:, idx:idx + 1],
                                                          in_=mst[:, bi, :], axis=AX.X,
                                                          apply_absolute_value=True),
              reads=(f"mst/{bi}",), writes=(f"wsums/{idx}",))

        def su(h, g2):
            hp = (h % 2) * 64
            ch = h // 2
            vch = (h * DH) // 128
            pp = (h * 2 + g2) % 2
            for i in range(4):
                qb = g2 * 4 + i
                E("T", lambda g, hp=hp, ch=ch, qb=qb, i=i: g.matmul(
                    ps_S[:, i, 0:16], q_sb[hp:hp + 64, ch, qb * 128:(qb + 1) * 128],
                    sbf[hp:hp + 64, ch, :], start=True, stop=True),
                  reads=(f"q/{ch}", "sbf"), writes=(f"ps_S/{i}",))
                E("T", lambda g, hp=hp, ch=ch, qb=qb, i=i: g.matmul(
                    ps_S[:, i, 16:272], q_sb[hp:hp + 64, ch, qb * 128:(qb + 1) * 128],
                    k_sb[hp:hp + 64, ch, qb * 128:qb * 128 + 256], start=True, stop=True),
                  reads=(f"q/{ch}", f"k/{ch}"), writes=(f"ps_S/{i}",))
                E("V", lambda g, i=i, g2=g2, pp=pp: g.scalar_tensor_tensor(
                    out=pe_sc[:, pp, i, :], in0=ps_S[:, i, 0:272], scalar=1.0,
                    in1=masks[:, g2, i * 272:(i + 1) * 272], op0=ALU.mult, op1=ALU.add),
                  reads=(f"ps_S/{i}", f"masks/{g2}"), writes=(f"pe/{pp}/{i}",))
            E("V", lambda g, pp=pp: g.reduce_max(out=srow[:, pp, 0:4], in_=pe_sc[:, pp, :, :],
                                                 axis=AX.X, negate=True),
              reads=tuple(f"pe/{pp}/{i}" for i in range(4)), writes=(f"srow/{pp}/negmx",))
            for i in range(4):
                E("S", lambda g, pp=pp, i=i: g.activation(
                    out=pe_sc[:, pp, i, :], in_=pe_sc[:, pp, i, :], func=AF.Exp,
                    bias=srow[:, pp, i:i + 1], scale=1.0, accum_out=srow[:, pp, 4 + i:5 + i]),
                  reads=(f"pe/{pp}/{i}", f"srow/{pp}/negmx"),
                  writes=(f"pe/{pp}/{i}", f"srow/{pp}/sum",))
            E("V", lambda g, pp=pp: g.reciprocal(out=srow[:, pp, 8:12], in_=srow[:, pp, 4:8]),
              reads=(f"srow/{pp}/sum",), writes=(f"srow/{pp}/rec",))
            for i in range(4):
                eng = "P" if i % 2 == 0 else "S"
                if eng == "S":
                    E(eng, lambda g, pp=pp, i=i: g.activation(
                        out=pb_sc[:, pp, i, :], in_=pe_sc[:, pp, i, :], func=AF.Identity,
                        bias=0.0, scale=srow[:, pp, 8 + i:9 + i]),
                      reads=(f"pe/{pp}/{i}", f"srow/{pp}/rec"), writes=(f"pb/{pp}/{i}",))
                else:
                    E(eng, lambda g, pp=pp, i=i: g.tensor_scalar(
                        out=pb_sc[:, pp, i, :], in0=pe_sc[:, pp, i, :],
                        scalar1=srow[:, pp, 8 + i:9 + i], scalar2=None, op0=ALU.mult),
                      reads=(f"pe/{pp}/{i}", f"srow/{pp}/rec"), writes=(f"pb/{pp}/{i}",))
                E("T", lambda g, pp=pp, i=i: g.transpose(
                    ps_PT[:, i, 0:128], pb_sc[:, pp, i, 16:144], P["idenbf"][:]),
                  reads=(f"pb/{pp}/{i}", "idenbf"), writes=(f"ps_PT/{i}",))
                E("T", lambda g, pp=pp, i=i: g.transpose(
                    ps_PT[:, i, 128:256], pb_sc[:, pp, i, 144:272], P["idenbf"][:]),
                  reads=(f"pb/{pp}/{i}",), writes=(f"ps_PT/{i}",))
                E("T", lambda g, pp=pp, i=i: g.transpose(
                    ps_PT[0:16, i, 256:384], pb_sc[:, pp, i, 0:16], P["idenbf"][:]),
                  reads=(f"pb/{pp}/{i}",), writes=(f"ps_PT/{i}",))
            for half in range(2):
                E("V", lambda g, pp=pp, half=half: g.tensor_copy(
                    out=pt_sb[:, pp, 2 * half:2 * half + 2, :],
                    in_=ps_PT[:, 2 * half:2 * half + 2, 0:384]),
                  reads=(f"ps_PT/{2 * half}", f"ps_PT/{2 * half + 1}"),
                  writes=(f"pt/{pp}/{2 * half}", f"pt/{pp}/{2 * half + 1}"))
            yp = (h * 2 + g2) % 2
            for i in range(4):
                qb = g2 * 4 + i
                E("T", lambda g, pp=pp, i=i, h=h: g.matmul(
                    ps_Y[:, yp, i * 128:(i + 1) * 128], svbf[:, h * 64:(h + 1) * 64],
                    pt_sb[0:16, pp, i, 256:384], start=True, stop=False),
                  reads=(f"pt/{pp}/{i}", "svbf"), writes=(f"ps_Y/{yp}",))
                E("T", lambda g, pp=pp, i=i, h=h, qb=qb: g.matmul(
                    ps_Y[:, yp, i * 128:(i + 1) * 128], v_tm[:, qb, h * 64:(h + 1) * 64],
                    pt_sb[:, pp, i, 0:128], start=False, stop=False),
                  reads=(f"pt/{pp}/{i}", f"V/{vch}"), writes=(f"ps_Y/{yp}",))
                E("T", lambda g, pp=pp, i=i, h=h, qb=qb: g.matmul(
                    ps_Y[:, yp, i * 128:(i + 1) * 128], v_tm[:, qb + 1, h * 64:(h + 1) * 64],
                    pt_sb[:, pp, i, 128:256], start=False, stop=True),
                  reads=(f"pt/{pp}/{i}", f"V/{vch}"), writes=(f"ps_Y/{yp}",))
            E("S", lambda g, h=h, g2=g2, yp=yp, hp=hp, ch=ch: g.activation(
                out=y_sb[hp:hp + 64, ch, g2 * 512:(g2 + 1) * 512], in_=ps_Y[:, yp, :],
                func=AF.Copy, bias=0.0, scale=1.0),
              reads=(f"ps_Y/{yp}",), writes=(f"y/{h}/{g2}",))

        import os as _os
        _dsu = int(_os.environ.get("DSU", "0"))
        tap_wacc_final = None
        if "pe" in tap_out:
            su(_dsu, 0)
            E(Q, lambda g: g.dma_start(out=tap_out["pe"][:, :, :], in_=pe_sc[:, 0, :, :]),
              reads=tuple(f"pe/0/{i}" for i in range(4)), qsem="qout")
            E(Q, lambda g: g.dma_start(out=tap_out["pe2"][:, :, :], in_=pe2_sc[:, 0, :, :]),
              reads=tuple(f"pe2/0/{i}" for i in range(4)), qsem="qout")
            E(Q, lambda g: g.dma_start(out=tap_out["sr0"][:, :], in_=srow[:, 0, :]),
              reads=("srow/0/negmx", "srow/0/sum", "srow/0/rec"), qsem="qout")
            if "pb" in tap_out:
                E(Q, lambda g: g.dma_start(out=tap_out["pb"][:, :, :], in_=pb_sc[:, 0, :, :]),
                  reads=tuple(f"pb/0/{i}" for i in range(4)), qsem="qout")
                E(Q, lambda g: g.dma_start(out=tap_out["pt"][:, :, :], in_=pt_sb[:, 0, :, :]),
                  reads=tuple(f"pt/0/{i}" for i in range(4)), qsem="qout")
                E(Q, lambda g, hp=(_dsu % 2) * 64, ch=_dsu // 2: g.dma_start(
                    out=tap_out["ysu"][:, :], in_=y_sb[hp:hp + 64, ch, 0:512]),
                  reads=(f"y/{_dsu}/0",), qsem="qout")
        if "mj" in tap_out:
            _mjdone = [False]
            def _tap_wacc_final():
                E(Q, lambda g: g.dma_start(out=tap_out["mj"][:, 2:4], in_=P["wacc"][:, 0:2]),
                  reads=("wacc",), qsem="qout")
            tap_wacc_final = _tap_wacc_final
        else:
            tap_wacc_final = None
        _nosu = bool(_os.environ.get("DBG_NOSU"))
        mean_i = 0
        for h in range(NH):
            for g2 in range(2):
                if not _nosu and not ("pe" in tap_out and h == _dsu and g2 == 0):
                    su(h, g2)
                want = ((h * 2 + g2 + 1) * len(mean_jobs)) // 64
                while mean_i < min(want, len(mean_jobs)):
                    emit_mean(mean_i)
                    mean_i += 1
        while mean_i < len(mean_jobs):
            emit_mean(mean_i)
            mean_i += 1

        ykeys = tuple(f"y/{h}/{g2}" for h in range(NH) for g2 in range(2))
        if "y" in tap_out:
            E(Q, lambda g: g.dma_start(out=tap_out["y"][:, :, :], in_=y_sb[:, :, :]), reads=ykeys, qsem="qout")

        fC_ps = []
        for nm in ("ps_Y", "ps_PT", "ps_S"):
            fC_ps += mem.pop("p", nm)
        E("V", lambda g: g.reduce_sum(out=P["wacc"][:, 0:1], in_=P["wsums"][:, 0:32], axis=AX.X),
          reads=tuple(f"wsums/{i}" for i in range(32)), writes=("wacc",))
        E("V", lambda g: g.reduce_sum(out=P["wacc"][:, 1:2], in_=P["wsums"][:, 32:64], axis=AX.X),
          reads=tuple(f"wsums/{i}" for i in range(32, 64)), writes=("wacc",))
        ps_w = mem.push("p", "ps_w", [1, 2], F32, fence_marks=fC_ps)
        E("T", lambda g: g.matmul(ps_w[:, 0:1], P["wacc"][:, 0:1], P["onesc"][:], start=True, stop=True),
          reads=("wacc", "onesc"), writes=("ps_w/0",))
        E("T", lambda g: g.matmul(ps_w[:, 1:2], P["wacc"][:, 1:2], P["onesc"][:], start=True, stop=True),
          reads=("wacc", "onesc"), writes=("ps_w/1",))
        E("V", lambda g: g.tensor_copy(out=srow[0:1, 0, 0:2], in_=ps_w[:, 0:2]),
          reads=("ps_w/0", "ps_w/1"), writes=("srow/swr",))
        ps_wb = mem.push("p", "ps_wb", [128, 4], F32)
        E("T", lambda g: g.matmul(ps_wb[:, 0:2], P["onesf"][:], srow[0:1, 0, 0:2], start=True, stop=True),
          reads=("srow/swr", "onesf"), writes=("ps_wb",))
        for nm, col in (("swup", 0), ("swdn", 1)):
            E("V", lambda g, nm=nm, col=col: g.tensor_scalar(
                out=P[nm][:], in0=ps_wb[:, col:col + 1], scalar1=1.0 / (UP * D), scalar2=1e-5,
                op0=ALU.mult, op1=ALU.max),
              reads=("ps_wb",), writes=(nm,))
        for nm, sw, f in (("thrup", "swup", 0.5), ("thrdn", "swdn", 0.5),
                          ("nthrup", "swup", -0.5), ("nthrdn", "swdn", -0.5)):
            E("V", lambda g, nm=nm, sw=sw, f=f: g.tensor_scalar(out=P[nm][:], in0=P[sw][:], scalar1=f,
                                                                scalar2=None, op0=ALU.mult),
              reads=(sw,), writes=(nm,))
        if tap_wacc_final is not None:
            E(Q, lambda g: g.dma_start(out=tap_out["mj"][:, 0:2], in_=P["wsums"][:, 0:2]),
              reads=("wsums/0", "wsums/1"), qsem="qout")
            E(Q, lambda g: g.dma_start(out=tap_out["mj"][:, 2:4], in_=P["wacc"][:, 0:2]),
              reads=("wacc",), qsem="qout")
            E(Q, lambda g: g.dma_start(out=tap_out["mj"][0:1, 4:6], in_=srow[0:1, 0, 0:2]),
              reads=("srow/swr",), qsem="qout")
        if "sw" in tap_out:
            E("V", lambda g: g.tensor_copy(out=P["wacc"][:, 0:1], in_=P["swup"][:, :]),
              reads=("swup",), writes=("wacc",))
            E("V", lambda g: g.tensor_copy(out=P["wacc"][:, 1:2], in_=P["swdn"][:, :]),
              reads=("swdn",), writes=("wacc",))
            E(Q, lambda g: g.dma_start(out=tap_out["sw"][:, :], in_=P["wacc"][:, :]),
              reads=("wacc",), qsem="qout")
        fC_ps += mem.pop("p", "ps_wb")
        fC_ps += mem.pop("p", "ps_w")
        ckpt("C")

        fC = []
        for nm in ("mst", "srow", "pt", "pb", "pe2", "pe", "stf", "svbf", "sbf", "masks",
                   "V", "k", "q"):
            fC += mem.pop("sr", nm)

        acc = mem.push("sr", "acc", [128, NCH, TQ], F32, fence_marks=fC)
        wst2 = mem.push("sr", "wst2", [128, NCH * 128], F32, fence_marks=fC)
        wbf2 = mem.push("sr", "wbf2", [128, 2, NCH * 128], BF16, fence_marks=fC)
        ps_wo = mem.push("p", "ps_wo", [128, 2, 2, 512], F32, fence_marks=fC_ps)

        for c in range(NCH):
            E(Q, lambda g, c=c: g.dma_start(out=acc[:, c, :], in_=u_T[c * 128:(c + 1) * 128, QOFF:QOFF + TQ]),
              writes=(f"acc/{c}",), qsem="qa")
        trk.settle("qa", tuple(f"acc/{c}" for c in range(NCH)))

        for oc in range(NCH):
            wb = oc % 2
            E(Q, lambda g, oc=oc: g.dma_start(out=wst2[:, :], in_=wo[oc, :, :]), writes=("wst2",), qsem="qw")
            cvt(conv_eng[oc % 3], wbf2[:, wb, :], wst2[:, :], "wst2", f"wbf2/{wb}")
            pp = oc % 2
            for ni in range(2):
                for kc in range(NCH):
                    E("T", lambda g, wb=wb, kc=kc, ni=ni, pp=pp: g.matmul(
                        ps_wo[:, pp, ni, :], wbf2[:, wb, kc * 128:(kc + 1) * 128],
                        y_sb[:, kc, ni * 512:(ni + 1) * 512], start=(kc == 0), stop=(kc == NCH - 1)),
                      reads=(f"wbf2/{wb}",) + ykeys, writes=(f"ps_wo/{pp}/{ni}",))
                E("V", lambda g, oc=oc, ni=ni, pp=pp: g.scalar_tensor_tensor(
                    out=acc[:, oc, ni * 512:(ni + 1) * 512], in0=ps_wo[:, pp, ni, :],
                    scalar=P["bo"][:, oc:oc + 1], in1=acc[:, oc, ni * 512:(ni + 1) * 512],
                    op0=ALU.add, op1=ALU.add),
                  reads=(f"ps_wo/{pp}/{ni}", "bo", f"acc/{oc}"), writes=(f"acc/{oc}",))

        acckeys = tuple(f"acc/{c}" for c in range(NCH))
        if "accwo" in tap_out:
            E(Q, lambda g: g.dma_start(out=tap_out["accwo"][:, :, :], in_=acc[:, :, :]), reads=acckeys, qsem="qout")
        ckpt("D")

        fD_ps = mem.pop("p", "ps_wo")
        fD = mem.pop("sr", "wbf2") + mem.pop("sr", "wst2")
        fy = mem.pop("sl", "y")

        hq = mem.push("sl", "hq", [128, UPCH, 512], BF16, fence_marks=fy)
        hbf = mem.push("sl", "hbf", [128, UPCH, 512], BF16, fence_marks=fy)

        xq = mem.push("sr", "xq", [128, NCH, 512], BF16, fence_marks=fD)
        sxdq = mem.push("sr", "sxdq", [128, 6, 256], F32, fence_marks=fD)
        qrow = mem.push("sr", "qrow", [128, 16], F32, fence_marks=fD)
        qr2 = mem.push("sr", "qr2", [1, 1024], F32, fence_marks=fD)

        def quant_tile(xht, nch, slots, spare, key, width, psq, pk, dq_rstd_q0, out_fn, dbg=False):
            E("V", lambda g, s0=slots[0]: g.tensor_copy(out=sxdq[:, 2, 0:width], in_=xht[:, s0, :]),
              reads=(f"{key}/{slots[0]}",), writes=("sxdq/2",))
            E("V", lambda g, s0=slots[0]: g.tensor_copy(out=sxdq[:, 4, 0:width], in_=xht[:, s0, :]),
              reads=(f"{key}/{slots[0]}",), writes=("sxdq/4",))
            for c in range(1, nch):
                s_ = slots[c]
                E("V", lambda g, s_=s_: g.tensor_tensor(
                    out=sxdq[:, 2, 0:width], in0=sxdq[:, 2, 0:width], in1=xht[:, s_, :], op=ALU.max),
                  reads=(f"{key}/{s_}", "sxdq/2"), writes=("sxdq/2",))
                E("V", lambda g, s_=s_: g.tensor_tensor(
                    out=sxdq[:, 4, 0:width], in0=sxdq[:, 4, 0:width], in1=xht[:, s_, :], op=ALU.min),
                  reads=(f"{key}/{s_}", "sxdq/4"), writes=("sxdq/4",))
            E("V", lambda g: g.tensor_scalar(out=sxdq[:, 4, 0:width], in0=sxdq[:, 4, 0:width],
                                             scalar1=-1.0, scalar2=None, op0=ALU.mult),
              reads=("sxdq/4",), writes=("sxdq/4",))
            E("V", lambda g: g.tensor_tensor(out=sxdq[:, 2, 0:width], in0=sxdq[:, 2, 0:width],
                                             in1=sxdq[:, 4, 0:width], op=ALU.max),
              reads=("sxdq/2", "sxdq/4"), writes=("sxdq/2",))
            ntp = width // 128
            for j in range(ntp):
                E("T", lambda g, j=j: g.transpose(psq[:, 0, j * 128:(j + 1) * 128],
                                                  sxdq[:, 2, j * 128:(j + 1) * 128], P["idenf"][:]),
                  reads=("sxdq/2", "idenf"), writes=(f"{pk}/a{j}",))
            E("V", lambda g, ntp=ntp: g.reduce_max(
                out=qrow[:, 0:ntp], in_=psq[:, 0, 0:ntp * 128].rearrange("p (j t) -> p j t", j=ntp),
                axis=AX.X),
              reads=tuple(f"{pk}/a{j}" for j in range(ntp)), writes=("qrow/mx",))
            if dbg:
                ckpt("E0c")
            if dq_rstd_q0 is not None:
                for j in range(ntp):
                    E("T", lambda g, j=j, q0=dq_rstd_q0: g.transpose(
                        psq[:, 1, j * 128:j * 128 + 1],
                        P["rrow"][0:1, q0 + j * 128:q0 + (j + 1) * 128], P["idenf"][0:1, 0:1]),
                      reads=("rrow", "idenf"), writes=(f"{pk}/b{j}",))
                    E("V", lambda g, j=j: g.tensor_copy(out=qrow[:, 8 + j:9 + j],
                                                        in_=psq[:, 1, j * 128:j * 128 + 1]),
                      reads=(f"{pk}/b{j}",), writes=("qrow/rst",))
                E("V", lambda g, ntp=ntp: g.reciprocal(out=qrow[:, 12:12 + ntp], in_=qrow[:, 8:8 + ntp]),
                  reads=("qrow/rst",), writes=("qrow/irst",))
                E("V", lambda g, ntp=ntp: g.tensor_scalar(out=qrow[:, 12:12 + ntp],
                                                          in0=qrow[:, 12:12 + ntp], scalar1=1e-5,
                                                          scalar2=None, op0=ALU.mult),
                  reads=("qrow/irst",), writes=("qrow/irst",))
                E("V", lambda g, ntp=ntp: g.tensor_tensor(out=qrow[:, 0:ntp], in0=qrow[:, 0:ntp],
                                                          in1=qrow[:, 12:12 + ntp], op=ALU.max),
                  reads=("qrow/mx", "qrow/irst"), writes=("qrow/mx",))
            else:
                E("V", lambda g, ntp=ntp: g.tensor_scalar(out=qrow[:, 0:ntp], in0=qrow[:, 0:ntp],
                                                          scalar1=1e-5, scalar2=None, op0=ALU.max),
                  reads=("qrow/mx",), writes=("qrow/mx",))
            E("V", lambda g, ntp=ntp: g.reciprocal(out=qrow[:, 4:4 + ntp], in_=qrow[:, 0:ntp]),
              reads=("qrow/mx",), writes=("qrow/sx",))
            E("V", lambda g, ntp=ntp: g.tensor_tensor(out=qrow[:, 12:12 + ntp],
                                                      in0=qrow[:, 4:4 + ntp], in1=qrow[:, 0:ntp],
                                                      op=ALU.mult),
              reads=("qrow/sx", "qrow/mx"), writes=("qrow/nt",))
            E("V", lambda g, ntp=ntp: g.tensor_scalar(out=qrow[:, 12:12 + ntp],
                                                      in0=qrow[:, 12:12 + ntp], scalar1=-1.0,
                                                      scalar2=2.0, op0=ALU.mult, op1=ALU.add),
              reads=("qrow/nt",), writes=("qrow/nt",))
            E("V", lambda g, ntp=ntp: g.tensor_tensor(out=qrow[:, 4:4 + ntp],
                                                      in0=qrow[:, 4:4 + ntp],
                                                      in1=qrow[:, 12:12 + ntp], op=ALU.mult),
              reads=("qrow/sx", "qrow/nt"), writes=("qrow/sx",))
            E("V", lambda g, ntp=ntp: g.tensor_scalar(out=qrow[:, 4:4 + ntp], in0=qrow[:, 4:4 + ntp],
                                                      scalar1=127.0, scalar2=None, op0=ALU.mult),
              reads=("qrow/sx",), writes=("qrow/sx",))
            sw = P["swup"] if dq_rstd_q0 is not None else P["swdn"]
            swkey = "swup" if dq_rstd_q0 is not None else "swdn"
            E("V", lambda g, ntp=ntp, sw=sw: g.tensor_scalar(out=qrow[:, 12:12 + ntp],
                                                             in0=qrow[:, 0:ntp],
                                                             scalar1=1.0 / 127.0,
                                                             scalar2=sw[:, 0:1],
                                                             op0=ALU.mult, op1=ALU.mult),
              reads=("qrow/mx", swkey), writes=("qrow/dq",))
            if dq_rstd_q0 is not None:
                E("V", lambda g, ntp=ntp: g.tensor_tensor(out=qrow[:, 12:12 + ntp],
                                                          in0=qrow[:, 12:12 + ntp],
                                                          in1=qrow[:, 8:8 + ntp], op=ALU.mult),
                  reads=("qrow/dq", "qrow/rst"), writes=("qrow/dq",))
            for j in range(ntp):
                E("T", lambda g, j=j: g.transpose(psq[0:1, 2, j * 128:(j + 1) * 128],
                                                  qrow[:, 4 + j:5 + j], P["idenf"][:]),
                  reads=("qrow/sx", "idenf"), writes=(f"{pk}/s{j}",))
                E("T", lambda g, j=j: g.transpose(psq[0:1, 3, j * 128:(j + 1) * 128],
                                                  qrow[:, 12 + j:13 + j], P["idenf"][:]),
                  reads=("qrow/dq", "idenf"), writes=(f"{pk}/d{j}",))
            E("V", lambda g: g.tensor_copy(out=qr2[0:1, 0:width], in_=psq[0:1, 2, 0:width]),
              reads=tuple(f"{pk}/s{j}" for j in range(ntp)), writes=("qr2/sx",))
            E("V", lambda g: g.tensor_copy(out=qr2[0:1, 512:512 + width], in_=psq[0:1, 3, 0:width]),
              reads=tuple(f"{pk}/d{j}" for j in range(ntp)), writes=("qr2/dq",))
            for j in range(ntp):
                E("T", lambda g, j=j: g.matmul(psq[:, 0, j * 128:(j + 1) * 128], P["onesf"][:],
                                               qr2[0:1, j * 128:(j + 1) * 128], start=True, stop=True),
                  reads=("qr2/sx", "onesf"), writes=(f"{pk}/a{j}",))
                E("T", lambda g, j=j: g.matmul(psq[:, 1, 128 + j * 128:128 + (j + 1) * 128],
                                               P["onesf"][:],
                                               qr2[0:1, 512 + j * 128:512 + (j + 1) * 128],
                                               start=True, stop=True),
                  reads=("qr2/dq", "onesf"), writes=(f"{pk}/e{j}",))
            E("V", lambda g: g.tensor_copy(out=sxdq[:, 0, 0:width], in_=psq[:, 0, 0:width]),
              reads=tuple(f"{pk}/a{j}" for j in range(ntp)), writes=("sxdq/0",))
            E("V", lambda g: g.tensor_copy(out=sxdq[:, 1, 0:width], in_=psq[:, 1, 128:128 + width]),
              reads=tuple(f"{pk}/e{j}" for j in range(ntp)), writes=("sxdq/1",))
            if dbg:
                ckpt("E0d")
            for c in range(nch):
                out_fn(c, slots[c])

        def butterflies(xht, nch, slots, spare, key, width):
            s = 1
            while s < nch:
                pairs = [c for c in range(nch) if not (c & s)]
                for pi, c in enumerate(pairs):
                    c2 = c + s
                    eng = "V" if pi % 8 < 3 else "P"
                    spi = 0 if eng == "V" else 1
                    sp = spare[spi]
                    a, b = slots[c], slots[c2]
                    E(eng, lambda g, a=a, b=b, sp=sp: g.tensor_add(
                        out=xht[:, sp, :], in0=xht[:, a, :], in1=xht[:, b, :]),
                      reads=(f"{key}/{a}", f"{key}/{b}"), writes=(f"{key}/{sp}",))
                    E(eng, lambda g, a=a, b=b: g.tensor_sub(
                        out=xht[:, b, :], in0=xht[:, a, :], in1=xht[:, b, :]),
                      reads=(f"{key}/{a}", f"{key}/{b}"), writes=(f"{key}/{b}",))
                    spare[spi] = a
                    slots[c] = sp
                s *= 2

        last_dn_ps = fD_ps
        last_dn_r = []
        xqkeys = tuple(f"xq/{c}" for c in range(NCH))
        hqkeys = tuple(f"hq/{c}/{qo}" for c in range(UPCH) for qo in (0, 256))
        hbkeys = tuple(f"hbf/{c}" for c in range(UPCH))

        def emit_ffn_half(hb, last_dn_ps, last_dn_r):
            CH0 = QOFF + hb * 512

            ub2 = mem.push("sr", "ub2", [128, 2, 256], F32, fence_marks=last_dn_r)
            xh = mem.push("sr", "xh", [128, 18, 256], F32, fence_marks=last_dn_r)
            ps_x1 = mem.push("p", "ps_x1", [128, 2, 512], F32, fence_marks=last_dn_ps)
            ps_tp = mem.push("p", "ps_tp", [128, 4, 512], F32, fence_marks=last_dn_ps)

            for ql in range(2):
                T0 = CH0 + ql * 256
                slots = list(range(16))
                spare = [16, 17]
                if ql == 0 and hb == 0:
                    ckpt("D2")
                for c in range(NCH):
                    bi = c % 2
                    E(Q, lambda g, c=c, bi=bi, T0=T0: g.dma_start(
                        out=ub2[:, bi, :], in_=u_T[c * 128:(c + 1) * 128, T0:T0 + 256]),
                      writes=(f"ub2/{bi}",), qsem=f"qu{bi}")
                    if c == 0 and ql == 0 and hb == 0:
                        ckpt("D3")
                    E("T", lambda g, bi=bi: g.matmul(ps_x1[:, bi, 0:256], P["h128f"][:], ub2[:, bi, :],
                                                     start=True, stop=True),
                      reads=(f"ub2/{bi}", "h128f"), writes=(f"ps_x1/{bi}",))
                    if c == 0 and ql == 0 and hb == 0:
                        ckpt("D4")
                    E("S", lambda g, c=c, bi=bi: g.activation(out=xh[:, c, :], in_=ps_x1[:, bi, 0:256],
                                                              func=AF.Identity, bias=0.0,
                                                              scale=P["rsc2048"][:, 0:1]),
                      reads=(f"ps_x1/{bi}", "rsc2048"), writes=(f"xh/{c}",))
                    if c == 1 and ql == 0 and hb == 0:
                        ckpt("D5")
                    if c == 3 and ql == 0 and hb == 0:
                        ckpt("D6")
                    if c == 7 and ql == 0 and hb == 0:
                        ckpt("D7")
                    if c == 15 and ql == 0 and hb == 0:
                        ckpt("D8")
                if ql == 0 and hb == 0:
                    ckpt("E0a")
                butterflies(xh, 16, slots, spare, "xh", 256)

                E("V", lambda g, T0=T0: g.tensor_tensor(
                    out=qr2[0:1, 512:768], in0=P["mrow"][0:1, T0:T0 + 256],
                    in1=P["sc2048"][0:1, 0:1].broadcast_to([1, 256]), op=ALU.mult),
                  reads=("mrow", "sc2048"), writes=("qr2/corr",))
                E("V", lambda g, s0=slots[0], xh=xh: g.tensor_sub(
                    out=xh[0:1, s0, :], in0=xh[0:1, s0, :], in1=qr2[0:1, 512:768]),
                  reads=(f"xh/{slots[0]}", "qr2/corr"), writes=(f"xh/{slots[0]}",))
                if ql == 0 and hb == 0:
                    ckpt("E0b")

                def out_q(c, sl, ql=ql):
                    eng = "V"
                    sp = spare[0]
                    E(eng, lambda g, sl=sl, sp=sp: g.tensor_tensor(
                        out=xh[:, sp, :].bitcast(I32), in0=xh[:, sl, :], in1=sxdq[:, 0, :],
                        op=ALU.mult),
                      reads=(f"xh/{sl}", "sxdq/0"), writes=(f"xh/{sp}",))
                    E(eng, lambda g, c=c, sp=sp, ql=ql: g.tensor_tensor(
                        out=xq[:, c, ql * 256:(ql + 1) * 256], in0=xh[:, sp, :].bitcast(I32),
                        in1=sxdq[:, 1, :], op=ALU.mult),
                      reads=(f"xh/{sp}", "sxdq/1"), writes=(f"xq/{c}",))

                quant_tile(xh, 16, slots, spare, "xh", 256, ps_tp, "ps_tp", T0, out_q, dbg=(ql == 0 and hb == 0))
                if ql == 0 and hb == 0:
                    ckpt("E1")

            if hb == 0:
                ckpt("EA")
            if "xq" in tap_out and hb == 0:
                E(Q, lambda g: g.dma_start(out=tap_out["xq"][:, :, :], in_=xq[:, :, :]), reads=xqkeys, qsem="qout")

            f_tp = mem.pop("p", "ps_tp")
            f_x1 = mem.pop("p", "ps_x1")
            f_xh = mem.pop("sr", "xh") + mem.pop("sr", "ub2")

            wst3 = mem.push("sr", "wst3", [128, NCH * 128], F32, fence_marks=f_xh)
            wtb = mem.push("sr", "wtb", [128, 2, NCH * 128], BF16, fence_marks=f_xh)
            sgn = mem.push("sr", "sgn", [128, NCH * 128], BF16, fence_marks=f_xh)
            rsc2 = mem.push("sr", "rsc2", [128, 2, 512], F32, fence_marks=f_xh)
            ps_up = mem.push("p", "ps_up", [128, 2, 512], F32, fence_marks=f_x1 + f_tp)

            for oc in range(UPCH):
                wb = oc % 2
                if hb == 0:
                    E(Q, lambda g, oc=oc: g.dma_start(out=wst3[:, :], in_=wup[oc, :, :]), writes=("wst3",), qsem="qw")
                    E("P", lambda g: g.tensor_scalar(out=sgn[:, :], in0=wst3[:, :],
                                                     scalar1=P["nthrup"][:, 0:1], scalar2=None,
                                                     op0=ALU.is_lt),
                      reads=("wst3", "nthrup"), writes=("sgn",))
                    E("V", lambda g, wb=wb: g.tensor_scalar(out=wtb[:, wb, :], in0=wst3[:, :],
                                                            scalar1=P["thrup"][:, 0:1], scalar2=None,
                                                            op0=ALU.is_gt),
                      reads=("wst3", "thrup"), writes=(f"wtb/{wb}",))
                    E("P", lambda g, wb=wb: g.tensor_sub(out=wtb[:, wb, :], in0=wtb[:, wb, :],
                                                         in1=sgn[:, :]),
                      reads=("sgn", f"wtb/{wb}"), writes=(f"wtb/{wb}",))
                    E(Q, lambda g, oc=oc, wb=wb: g.dma_start(out=wup_cache[oc, :, :], in_=wtb[:, wb, :]),
                      reads=(f"wtb/{wb}",), writes=(f"wupc/{oc}",), qsem="qc")
                else:
                    E(Q, lambda g, oc=oc, wb=wb: g.dma_start(out=wtb[:, wb, :], in_=wup_cache[oc, :, :]),
                      reads=(f"wupc/{oc}",), writes=(f"wtb/{wb}",), qsem=f"qt{wb}")
                pp = oc % 2
                for kc in range(NCH):
                    E("T", lambda g, wb=wb, kc=kc, pp=pp: g.matmul(
                        ps_up[:, pp, :], wtb[:, wb, kc * 128:(kc + 1) * 128], xq[:, kc, :],
                        start=(kc == 0), stop=(kc == NCH - 1)),
                      reads=(f"wtb/{wb}", f"xq/{kc}"), writes=(f"ps_up/{pp}",))
                E("S", lambda g, pp=pp: g.activation(out=rsc2[:, pp, :], in_=ps_up[:, pp, :],
                                                     func=AF.Relu, bias=0.0, scale=1.0),
                  reads=(f"ps_up/{pp}",), writes=(f"rsc2/{pp}",))
                E("V", lambda g, oc=oc, pp=pp: g.tensor_mul(out=hbf[:, oc, :], in0=rsc2[:, pp, :],
                                                            in1=rsc2[:, pp, :]),
                  reads=(f"rsc2/{pp}",), writes=(f"hbf/{oc}",))

            if hb == 0:
                trk.settle("qc", tuple(f"wupc/{oc}" for oc in range(UPCH)))
                ckpt("EB")
            if "h" in tap_out and hb == 0:
                E(Q, lambda g: g.dma_start(out=tap_out["h"][:, :, :], in_=hbf[:, :, :]), reads=hbkeys, qsem="qout")

            f_up = mem.pop("p", "ps_up")
            f_s2 = mem.pop("sr", "rsc2") + mem.pop("sr", "sgn") + mem.pop("sr", "wtb") + \
                mem.pop("sr", "wst3")

            xh2 = mem.push("sr", "xh2", [128, 34, 256], F32, fence_marks=f_s2)
            ps_x2 = mem.push("p", "ps_x2", [128, 2, 512], F32, fence_marks=f_up)
            ps_t2 = mem.push("p", "ps_t2", [128, 4, 512], F32, fence_marks=f_up)

            for ql in range(2):
                QO = ql * 256
                slots2 = list(range(32))
                spare2 = [32, 33]
                for c in range(UPCH):
                    bi = c % 2
                    E("T", lambda g, c=c, bi=bi, QO=QO: g.matmul(
                        ps_x2[:, bi, 0:256], P["h128bf"][:], hbf[:, c, QO:QO + 256], start=True, stop=True),
                      reads=(f"hbf/{c}", "h128bf"), writes=(f"ps_x2/{bi}",))
                    E("S", lambda g, c=c, bi=bi: g.activation(out=xh2[:, c, :], in_=ps_x2[:, bi, 0:256],
                                                              func=AF.Copy, bias=0.0, scale=0.015625),
                      reads=(f"ps_x2/{bi}",), writes=(f"xh2/{c}",))

                butterflies(xh2, 32, slots2, spare2, "xh2", 256)

                def out_h(c, sl, ql=ql):
                    eng = "V"
                    sp = spare2[0]
                    E(eng, lambda g, sl=sl, sp=sp: g.tensor_tensor(
                        out=xh2[:, sp, :].bitcast(I32), in0=xh2[:, sl, :], in1=sxdq[:, 0, :],
                        op=ALU.mult),
                      reads=(f"xh2/{sl}", "sxdq/0"), writes=(f"xh2/{sp}",))
                    E(eng, lambda g, c=c, sp=sp, ql=ql: g.tensor_tensor(
                        out=hq[:, c, ql * 256:(ql + 1) * 256], in0=xh2[:, sp, :].bitcast(I32),
                        in1=sxdq[:, 1, :], op=ALU.mult),
                      reads=(f"xh2/{sp}", "sxdq/1"), writes=(f"hq/{c}/{ql * 256}",))

                quant_tile(xh2, 32, slots2, spare2, "xh2", 256, ps_t2, "ps_t2", None, out_h)
                if ql == 0 and hb == 0:
                    ckpt("E2")

            if hb == 0:
                ckpt("EC")
            if "hq" in tap_out and hb == 0:
                E(Q, lambda g: g.dma_start(out=tap_out["hq"][:, :, :], in_=hq[:, :, :]), reads=hqkeys, qsem="qout")

            f_t2 = mem.pop("p", "ps_t2")
            f_x2 = mem.pop("p", "ps_x2")
            f_xh2 = mem.pop("sr", "xh2")

            wst4 = mem.push("sr", "wst4", [128, UPCH * 128], F32, fence_marks=f_xh2)
            wtb2 = mem.push("sr", "wtb2", [128, 2, UPCH * 128], BF16, fence_marks=f_xh2)
            sgn2 = mem.push("sr", "sgn2", [128, UPCH * 128], BF16, fence_marks=f_xh2)
            ps_dn = mem.push("p", "ps_dn", [128, 2, 512], F32, fence_marks=f_x2 + f_t2)

            for oc in range(NCH):
                wb = oc % 2
                if hb == 0:
                    E(Q, lambda g, oc=oc: g.dma_start(out=wst4[:, :], in_=wdn[oc, :, :]), writes=("wst4",), qsem="qw")
                    E("P", lambda g: g.tensor_scalar(out=sgn2[:, :], in0=wst4[:, :],
                                                     scalar1=P["nthrdn"][:, 0:1], scalar2=None,
                                                     op0=ALU.is_lt),
                      reads=("wst4", "nthrdn"), writes=("sgn2",))
                    E("V", lambda g, wb=wb: g.tensor_scalar(out=wtb2[:, wb, :], in0=wst4[:, :],
                                                            scalar1=P["thrdn"][:, 0:1], scalar2=None,
                                                            op0=ALU.is_gt),
                      reads=("wst4", "thrdn"), writes=(f"wtb2/{wb}",))
                    E("P", lambda g, wb=wb: g.tensor_sub(out=wtb2[:, wb, :], in0=wtb2[:, wb, :],
                                                         in1=sgn2[:, :]),
                      reads=("sgn2", f"wtb2/{wb}"), writes=(f"wtb2/{wb}",))
                    E(Q, lambda g, oc=oc, wb=wb: g.dma_start(out=wdn_cache[oc, :, :], in_=wtb2[:, wb, :]),
                      reads=(f"wtb2/{wb}",), writes=(f"wdnc/{oc}",), qsem="qc")
                else:
                    E(Q, lambda g, oc=oc, wb=wb: g.dma_start(out=wtb2[:, wb, :], in_=wdn_cache[oc, :, :]),
                      reads=(f"wdnc/{oc}",), writes=(f"wtb2/{wb}",), qsem=f"qt{wb}")
                pp = oc % 2
                for kc in range(UPCH):
                    E("T", lambda g, wb=wb, kc=kc, pp=pp: g.matmul(
                        ps_dn[:, pp, :], wtb2[:, wb, kc * 128:(kc + 1) * 128], hq[:, kc, :],
                        start=(kc == 0), stop=(kc == UPCH - 1)),
                      reads=(f"wtb2/{wb}", f"hq/{kc}/0", f"hq/{kc}/256"), writes=(f"ps_dn/{pp}",))
                E("V", lambda g, oc=oc, pp=pp, hb=hb: g.tensor_add(
                    out=acc[:, oc, hb * 512:(hb + 1) * 512], in0=acc[:, oc, hb * 512:(hb + 1) * 512],
                    in1=ps_dn[:, pp, :]),
                  reads=(f"ps_dn/{pp}", f"acc/{oc}"), writes=(f"acc/{oc}", f"accf/{oc}/{hb}"))

            if hb == 0:
                trk.settle("qc", tuple(f"wdnc/{oc}" for oc in range(NCH)))
                ckpt("ED")
            last_dn_ps = mem.pop("p", "ps_dn")
            last_dn_r = mem.pop("sr", "sgn2") + mem.pop("sr", "wtb2") + mem.pop("sr", "wst4")
            return last_dn_ps, last_dn_r

        for hb in range(2):
            last_dn_ps, last_dn_r = emit_ffn_half(hb, last_dn_ps, last_dn_r)

        for c in range(NCH):
            E(Q, lambda g, c=c: g.dma_start(out=out_T[c * 128:(c + 1) * 128, :], in_=acc[:, c, :]),
              reads=(f"acc/{c}", f"accf/{c}/0", f"accf/{c}/1"), qsem="qout")


    except _Stop:
        pass


    es = contextlib.ExitStack()
    for e in ENGS:
        trk.sems[e] = es.enter_context(nc.semaphore(f"sem{e}"))
    for qn in trk.qcnt:
        trk.sems[qn] = es.enter_context(nc.semaphore(f"sem_{qn}"))
    qout_total = trk.qcnt.get("qout", 0)
    with es, nc.Block() as block:
        @block.sync
        def _(eng):
            for f in trk.prog["Q"]:
                f(eng)
            if qout_total:
                eng.wait_ge(trk.sems["qout"], qout_total)

        @block.tensor
        def _(eng):
            for f in trk.prog["T"]:
                f(eng)

        @block.scalar
        def _(eng):
            for f in trk.prog["S"]:
                f(eng)

        @block.vector
        def _(eng):
            for f in trk.prog["V"]:
                f(eng)

        @block.gpsimd
        def _(eng):
            for f in trk.prog["P"]:
                f(eng)

    return nc, tap_out


_CACHE = {}


def _get_program(taps=()):
    key = tuple(sorted(taps))
    if key not in _CACHE:
        _CACHE[key] = build_program(taps)
    return _CACHE[key]


def host_inputs(L0_attn_u, L0_attn_Wqkv, L0_attn_bqkv, L0_attn_Wo, L0_attn_bo,
                L0_attn_sinks, L0_ffn_w_up, L0_ffn_w_down):
    wqkv_s = _strips(np.asarray(L0_attn_Wqkv, np.float32), QKV_OC, NCH)
    wo_s = _strips(np.asarray(L0_attn_Wo, np.float32), NCH, NCH)
    wup_s = _strips(np.asarray(L0_ffn_w_up, np.float32), UPCH, NCH)
    wdn_s = _strips(np.asarray(L0_ffn_w_down, np.float32), NCH, UPCH)
    bq_t = np.ascontiguousarray(np.asarray(L0_attn_bqkv, np.float32).reshape(QKV_OC, 128).T)
    bo_t = np.ascontiguousarray(np.asarray(L0_attn_bo, np.float32).reshape(NCH, 128).T)
    sinks = np.asarray(L0_attn_sinks, np.float32)
    s4 = sinks.reshape(16, 2, NS, DH)
    sinkT = np.ascontiguousarray(s4.transpose(1, 3, 0, 2).reshape(128, 16, NS))
    sinkV = np.ascontiguousarray(sinks.transpose(1, 0, 2).reshape(NS, NH * DH))

    h128 = _hadamard128()
    consts = np.zeros((128, 8), np.float32)
    for j, v in enumerate([0.0, 1.0, 1e-5, -0.5, 2048.0, 0.015625]):
        consts[:, j] = v

    shared = dict(
        wqkv=wqkv_s, bqkv_t=bq_t, wo=wo_s, bo_t=bo_t, wup=wup_s, wdn=wdn_s,
        sinkT=sinkT, sinkV=sinkV,
        h128bf=h128.astype(ml_dtypes.bfloat16), h128f=h128,
        idenbf=np.eye(128).astype(ml_dtypes.bfloat16), idenf=np.eye(128, dtype=np.float32),
        pswap=_pswap().astype(ml_dtypes.bfloat16),
        consts=consts, onesf=np.ones((1, 128), np.float32), onesc=np.ones((128, 1), np.float32),
    )

    u = np.asarray(L0_attn_u, np.float32)
    in_maps = []
    for core in range(8):
        b, half = core // 2, core % 2
        qstart = half * 1024
        u_Tc = np.zeros((D, TKV), np.float32)
        lo, hi = qstart - 64, qstart - 64 + TKV
        slo, shi = max(lo, 0), min(hi, L)
        u_Tc[:, slo - lo:shi - lo] = u[b, slo:shi, :].T
        cq, sq, ck, sk = _trig_tables(qstart)
        m0, m1 = _masks(qstart)
        m = dict(shared)
        m.update(u_T=np.ascontiguousarray(u_Tc), cq=cq, sq=sq, ck=ck, sk=sk,
                 mask0=m0, mask1=m1)
        in_maps.append(m)
    return in_maps


def kernel(**inputs):
    nc, _ = _get_program()
    in_maps = host_inputs(**inputs)
    res = run_bass_kernel_spmd(nc, in_maps, core_ids=list(range(8)))
    out = np.empty((B, L, D), np.float32)
    for core in range(8):
        b, half = core // 2, core % 2
        out[b, half * 1024:(half + 1) * 1024, :] = np.asarray(res.results[core]["out_T"]).T
    return out


# version 03
# speedup vs baseline: 1.6996x; 1.6996x over previous
import contextlib
import numpy as np
import ml_dtypes
import concourse.bass as bass
import concourse.mybir as mybir
from concourse.bass_utils import run_bass_kernel_spmd

F32 = mybir.dt.float32
BF16 = mybir.dt.bfloat16
I32 = mybir.dt.int32
AF = mybir.ActivationFunctionType
ALU = mybir.AluOpType
AX = mybir.AxisListType

B, L, D = 4, 2048, 2048
NCH = D // 128
TKV = 1152
TQ = 1024
QOFF = 64
NH, DH, NS = 32, 64, 16
UP = 4096
UPCH = UP // 128
QKV_OC = 3 * NCH
NEG = -1.0e30
NT_KV = ((0, 512), (512, 512), (1024, 128))
NT_Q = ((64, 512), (576, 512))

ENGS = ("T", "S", "V", "P", "Q")


class Trk:
    def __init__(self):
        self.prog = {e: [] for e in ENGS}
        self.cnt = {e: 0 for e in ENGS}
        self.qcnt = {}
        self.writer = {}
        self.readers = {}
        self.waited = {e: {} for e in ENGS}
        self.pfence = {}
        self.sems = {}
        import os
        self.log = open("/tmp/trk.log", "w") if os.environ.get("TRKLOG") else None

    def _add_wait(self, e, oe, val):
        if val <= 0:
            return
        if self.waited[e].get(oe, 0) >= val:
            return
        self.waited[e][oe] = val
        sems = self.sems
        self.prog[e].append(lambda eng, oe=oe, val=val: eng.wait_ge(sems[oe], val))

    def emit(self, e, fn, reads=(), writes=(), tag="", qsem=None):
        need = {}
        for r in set(reads) | set(writes):
            p = r.split("/", 1)[0]
            marks = self.pfence.get(p)
            if marks:
                for (oe, c) in marks:
                    if oe != e:
                        need[oe] = max(need.get(oe, 0),
                                       self.qcnt[oe] if oe in self.qcnt else c)
        skipT = (e == "T")

        def mval(oe, c):
            return self.qcnt[oe] if oe in self.qcnt else c

        for r in reads:
            w = self.writer.get(r)
            if w is not None and not (skipT and w[0] == "T"):
                need[w[0]] = max(need.get(w[0], 0), mval(*w))
        for r in writes:
            w = self.writer.get(r)
            if w is not None and not (skipT and w[0] == "T"):
                need[w[0]] = max(need.get(w[0], 0), mval(*w))
            for oe, c in self.readers.get(r, {}).items():
                if not (skipT and oe == "T"):
                    need[oe] = max(need.get(oe, 0), mval(oe, c))
        for oe, v in need.items():
            self._add_wait(e, oe, v)
        if e == "Q":
            assert qsem is not None
            self.qcnt[qsem] = self.qcnt.get(qsem, 0) + 16
            me = self.qcnt[qsem]
            mark = (qsem, me)
            sems = self.sems
            self.prog["Q"].append(lambda eng, fn=fn, qsem=qsem: fn(eng).then_inc(sems[qsem], 16))
        else:
            self.cnt[e] += 1
            me = self.cnt[e]
            mark = (e, me)
            sems = self.sems
            self.prog[e].append(lambda eng, fn=fn, e=e: fn(eng).then_inc(sems[e], 1))
        if self.log is not None:
            self.log.write(f"{e}/{qsem} cnt={me} waits={need} reads={reads} writes={writes} {tag}\n")
        for r in writes:
            self.writer[r] = mark
            self.readers[r] = {}
        for r in reads:
            self.readers.setdefault(r, {})[mark[0]] = me
        return mark

    def settle(self, qsem, regions):
        total = self.qcnt.get(qsem, 0)
        for r in regions:
            w = self.writer.get(r)
            if w is not None and w[0] == qsem:
                self.writer[r] = (qsem, total)

    def marks_for(self, prefix):
        out = []
        for r, w in list(self.writer.items()):
            if r == prefix or r.startswith(prefix + "/"):
                if w is not None:
                    out.append(w)
                for oe, c in self.readers.get(r, {}).items():
                    out.append((oe, c))
        return out

    def clear_prefix(self, prefix):
        self.pfence.pop(prefix, None)
        for r in list(self.writer.keys()):
            if r == prefix or r.startswith(prefix + "/"):
                del self.writer[r]
                self.readers.pop(r, None)

    def fence(self, region, marks):
        self.pfence[region] = list(marks)


class Mem:
    def __init__(self, nc, trk):
        self.nc = nc
        self.trk = trk
        self.stack = {"sl": [], "sr": [], "p": []}
        self.serial = 0

    def push(self, which, name, shape, dtype, fence_marks=None):
        self.serial += 1
        tname = f"{name}__{self.serial}"
        if which == "p":
            g = self.nc.psum_tensor(tname, shape, dtype)
        else:
            g = self.nc.sbuf_tensor(tname, shape, dtype,
                                    side=("left" if which == "sl" else "right"))
        h = g.__enter__()
        self.stack[which].append((name, g))
        if fence_marks:
            self.trk.fence(name, fence_marks)
        return h

    def pop(self, which, name):
        top, g = self.stack[which].pop()
        assert top == name, f"LIFO violation: pop {name} but top is {top}"
        marks = self.trk.marks_for(name)
        self.trk.clear_prefix(name)
        g.__exit__(None, None, None)
        return marks


def _hadamard128():
    i = np.arange(128)
    pc = np.bitwise_count(i[:, None] & i[None, :]) if hasattr(np, "bitwise_count") else \
        np.array([bin(a & b).count("1") for a in i for b in i]).reshape(128, 128)
    return np.where(pc % 2 == 0, np.float32(1.0), np.float32(-1.0)).astype(np.float32)


def _pswap():
    p = np.zeros((128, 128), np.float32)
    for m in range(128):
        k = m + 32 if (m % 64) < 32 else m - 32
        p[k, m] = 1.0
    return p


def _trig_tables(qstart):
    import jax
    import jax.numpy as jnp
    cpu = jax.devices("cpu")[0]
    with jax.default_device(cpu):
        half = 32
        inv_freq = 1.0 / 10000.0 ** (jnp.arange(half, dtype=jnp.float32) / half)

        def tables(p0, n, scale):
            pos = jnp.arange(p0, p0 + n, dtype=jnp.float32)
            ang = pos[None, :] * inv_freq[:, None]
            c = np.asarray(jnp.cos(ang), np.float32)
            s = np.asarray(jnp.sin(ang), np.float32)
            C = np.tile(c, (4, 1)) * scale
            S = np.tile(s, (4, 1)) * scale
            sign = np.where((np.arange(128) % 64) < 32, np.float32(-1.0), np.float32(1.0))
            return np.ascontiguousarray(C), np.ascontiguousarray(S * sign[:, None])

        ck, sk = tables(qstart - 64, TKV, np.float32(1.0))
        cq, sq = tables(qstart, TQ, np.float32(0.125))
    return cq, sq, ck, sk


def _masks(qstart):
    out = []
    for g in range(2):
        m = np.zeros((128, 4, 272), np.float32)
        for i in range(4):
            qb = g * 4 + i
            q = np.arange(128)[:, None]
            jw = np.arange(256)[None, :]
            valid = (jw >= q) & (jw <= q + 128)
            kpos = qstart - 64 + qb * 128 + jw
            valid = valid & (kpos >= 0) & (kpos < L)
            m[:, i, 16:] = np.where(valid, np.float32(0.0), np.float32(NEG))
        out.append(np.ascontiguousarray(m.reshape(128, 4 * 272)).astype(ml_dtypes.bfloat16))
    return out


def _strips(w, n_oc, n_kc):
    return np.ascontiguousarray(
        w.reshape(n_oc, 128, n_kc, 128).transpose(0, 3, 2, 1).reshape(n_oc, 128, n_kc * 128))


class _Stop(Exception):
    pass


def build_program(taps=(), stop_after=None):
    nc = bass.Bass("TRN2", target_bir_lowering=False)
    trk = Trk()
    mem = Mem(nc, trk)
    E = trk.emit
    Q = "Q"

    dI = lambda n, s, dt=F32: nc.dram_tensor(n, s, dt, kind="ExternalInput").ap()
    dO = lambda n, s, dt=F32: nc.dram_tensor(n, s, dt, kind="ExternalOutput").ap()

    u_T = dI("u_T", [D, TKV])
    wqkv = dI("wqkv", [QKV_OC, 128, NCH * 128])
    bqkv = dI("bqkv_t", [128, QKV_OC])
    wo = dI("wo", [NCH, 128, NCH * 128])
    bo_in = dI("bo_t", [128, NCH])
    wup = dI("wup", [UPCH, 128, NCH * 128])
    wdn = dI("wdn", [NCH, 128, UPCH * 128])
    sinkT_in = dI("sinkT", [128, NCH, NS])
    sinkV_in = dI("sinkV", [NS, NH * DH])
    cq_in = dI("cq", [128, TQ])
    sq_in = dI("sq", [128, TQ])
    ck_in = dI("ck", [128, TKV])
    sk_in = dI("sk", [128, TKV])
    mask0_in = dI("mask0", [128, 4 * 272], BF16)
    mask1_in = dI("mask1", [128, 4 * 272], BF16)
    h128_in = dI("h128bf", [128, 128], BF16)
    h128f_in = dI("h128f", [128, 128])
    iden_in = dI("idenbf", [128, 128], BF16)
    idenf_in = dI("idenf", [128, 128])
    pswap_in = dI("pswap", [128, 128], BF16)
    consts_in = dI("consts", [128, 8])
    onesf_in = dI("onesf", [1, 128])
    onesc_in = dI("onesc", [128, 1])

    out_T = dO("out_T", [D, TQ])

    wup_cache = nc.dram_tensor("wup_cache", [UPCH, 128, NCH * 128], BF16, kind="Internal").ap()
    wdn_cache = nc.dram_tensor("wdn_cache", [NCH, 128, UPCH * 128], BF16, kind="Internal").ap()

    tapset = set(taps)
    tap_out = {}
    for nm, shape, dt in [("xn", [128, NCH, TKV], BF16), ("q", [128, NCH, TQ], BF16),
                          ("k", [128, NCH, TKV], BF16), ("v", [128, 9, NH * DH], BF16),
                          ("y", [128, NCH, TQ], BF16), ("accwo", [128, NCH, TQ], F32),
                          ("xq", [128, NCH, 512], BF16), ("h", [128, UPCH, 512], BF16),
                          ("hq", [128, UPCH, 512], BF16), ("sw", [128, 2], F32),
                          ("pe", [128, 4, 272], F32), ("pe2", [128, 4, 272], F32),
                          ("pb", [128, 4, 272], BF16), ("pt", [128, 4, 384], BF16),
                          ("ysu", [64, 512], BF16),
                          ("sr0", [128, 16], F32), ("mj", [128, 8], F32),
                          ("rows", [1, TKV * 2], F32)]:
        if nm in tapset:
            tap_out[nm] = dO("tap_" + nm, shape, dt)

    def ckpt(ph):
        if stop_after == ph:
            raise _Stop()

    P = {}
    for nm, shape, dt in [
        ("consts", [128, 8], F32), ("onesc", [128, 1], F32),
        ("onesbf", [128, 1], BF16), ("idenbf", [128, 128], BF16), ("idenf", [128, 128], F32),
        ("pswap", [128, 128], BF16), ("h128bf", [128, 128], BF16), ("h128f", [128, 128], F32),
        ("bq", [128, QKV_OC], F32), ("bo", [128, NCH], F32),
        ("rsc2048", [128, 1], F32),
        ("swup", [128, 1], F32), ("thrup", [128, 1], F32), ("nthrup", [128, 1], F32),
        ("swdn", [128, 1], F32), ("thrdn", [128, 1], F32), ("nthrdn", [128, 1], F32),
        ("wacc", [128, 2], F32), ("wsums", [128, 64], F32),
        ("onesf", [1, 128], F32), ("mrow", [1, TKV], F32), ("rrow", [1, TKV], F32),
        ("sc2048", [1, 1], F32), ("ln2048", [1, 1], F32),
    ]:
        P[nm] = mem.push("sl", "sb_" + nm, shape, dt)

    for val, col in [(0.0, 0), (1.0, 1), (1e-5, 2), (-0.5, 3), (2048.0, 4), (0.015625, 5), (0.5, 6)]:
        nc.const_aps.register(F32, val, P["consts"][:, col:col + 1])

    try:
        for nm, src in [("consts", consts_in), ("onesf", onesf_in), ("onesc", onesc_in),
                        ("idenbf", iden_in), ("idenf", idenf_in), ("pswap", pswap_in),
                        ("h128bf", h128_in), ("h128f", h128f_in), ("bq", bqkv), ("bo", bo_in)]:
            E(Q, lambda g, d=P[nm], s=src: g.dma_start(out=d[:], in_=s[:, :]), writes=(nm,), qsem="qx")
        trk.settle("qx", ("consts", "onesf", "onesc", "idenbf", "idenf", "pswap",
                          "h128bf", "h128f", "bq", "bo"))
        E("V", lambda g: g.tensor_copy(out=P["onesbf"][:], in_=P["onesc"][:]),
          reads=("onesc",), writes=("onesbf",))
        E("S", lambda g: g.activation(out=P["ln2048"][:], in_=P["consts"][0:1, 4:5], func=AF.Ln,
                                      bias=0.0, scale=1.0), reads=("consts",), writes=("ln2048",))
        E("S", lambda g: g.activation(out=P["sc2048"][:], in_=P["ln2048"][:], func=AF.Exp,
                                      bias=0.0, scale=0.5), reads=("ln2048",), writes=("sc2048",))

        xn = mem.push("sl", "xn", [128, NCH, TKV], BF16)

        ubuf = mem.push("sr", "ubuf", [128, 2, TKV], F32)
        sqb = mem.push("sr", "sqb", [128, 2, TKV], BF16)
        ubfb = mem.push("sr", "ubfb", [128, 2, TKV], BF16)
        arow = mem.push("sr", "arow", [1, 4 * TKV], F32)
        RB = mem.push("sr", "RB", [128, 2, TKV], F32)

        ps_st1 = mem.push("p", "ps_st1", [1, TKV], F32)
        ps_st2 = mem.push("p", "ps_st2", [1, TKV], F32)

        for c in range(NCH):
            bi = c % 2
            E(Q, lambda g, c=c, bi=bi: g.dma_start(out=ubuf[:, bi, :], in_=u_T[c * 128:(c + 1) * 128, :]),
              writes=(f"ubuf/{bi}",), qsem=f"qu{bi}")
            E("V", lambda g, bi=bi: g.tensor_copy(out=ubfb[:, bi, :], in_=ubuf[:, bi, :]),
              reads=(f"ubuf/{bi}",), writes=(f"ubfb/{bi}",))
            E("V", lambda g, bi=bi: g.tensor_mul(out=sqb[:, bi, :], in0=ubfb[:, bi, :], in1=ubfb[:, bi, :]),
              reads=(f"ubfb/{bi}",), writes=(f"sqb/{bi}",))
            for (o, n) in NT_KV:
                E("T", lambda g, c=c, bi=bi, o=o, n=n: g.matmul(
                    ps_st1[:, o:o + n], P["onesbf"][:], ubfb[:, bi, o:o + n],
                    start=(c == 0), stop=(c == NCH - 1)),
                  reads=(f"ubfb/{bi}", "onesbf"), writes=(f"ps_st1/{o}",))
                E("T", lambda g, c=c, bi=bi, o=o, n=n: g.matmul(
                    ps_st2[:, o:o + n], P["onesbf"][:], sqb[:, bi, o:o + n],
                    start=(c == 0), stop=(c == NCH - 1)),
                  reads=(f"sqb/{bi}", "onesbf"), writes=(f"ps_st2/{o}",))

        mr, rr = P["mrow"], P["rrow"]
        e2 = arow[0:1, 0:TKV]
        var = arow[0:1, TKV:2 * TKV]
        lnv = arow[0:1, 2 * TKV:3 * TKV]
        brow = arow[0:1, 3 * TKV:4 * TKV]
        st1keys = tuple(f"ps_st1/{o}" for (o, n) in NT_KV)
        st2keys = tuple(f"ps_st2/{o}" for (o, n) in NT_KV)
        E("V", lambda g: g.tensor_scalar(out=mr[:], in0=ps_st1[:, :], scalar1=1.0 / 2048.0,
                                         scalar2=None, op0=ALU.mult), reads=st1keys, writes=("mrow",))
        E("V", lambda g: g.tensor_scalar(out=e2, in0=ps_st2[:, :], scalar1=1.0 / 2048.0,
                                         scalar2=None, op0=ALU.mult), reads=st2keys, writes=("arow/e2",))
        E("V", lambda g: g.tensor_mul(out=var, in0=mr[:], in1=mr[:]), reads=("mrow",), writes=("arow/var",))
        E("V", lambda g: g.tensor_sub(out=var, in0=e2, in1=var),
          reads=("arow/e2", "arow/var"), writes=("arow/var",))
        E("S", lambda g: g.activation(out=lnv, in_=var, func=AF.Ln, bias=1e-5, scale=1.0),
          reads=("arow/var",), writes=("arow/lnv",))
        E("S", lambda g: g.activation(out=rr[:], in_=lnv, func=AF.Exp, bias=0.0, scale=-0.5),
          reads=("arow/lnv",), writes=("rrow",))
        E("V", lambda g: g.tensor_mul(out=brow, in0=mr[:], in1=rr[:]),
          reads=("mrow", "rrow"), writes=("arow/brow",))
        E("V", lambda g: g.tensor_scalar(out=brow, in0=brow, scalar1=-1.0, scalar2=None, op0=ALU.mult),
          reads=("arow/brow",), writes=("arow/brow",))
        if "rows" in tap_out:
            E(Q, lambda g: g.dma_start(out=tap_out["rows"][0:1, 0:TKV], in_=mr[:]), reads=("mrow",), qsem="qout")
            E(Q, lambda g: g.dma_start(out=tap_out["rows"][0:1, TKV:2 * TKV], in_=rr[:]), reads=("rrow",), qsem="qout")
        E("S", lambda g: g.activation(out=arow[0:1, 0:1], in_=P["ln2048"][:], func=AF.Exp,
                                      bias=0.0, scale=-0.5),
          reads=("ln2048",), writes=("arow/rsc",))

        f_st2 = mem.pop("p", "ps_st2")
        f_st1 = mem.pop("p", "ps_st1")
        ps_rb = mem.push("p", "ps_rb", [128, 2, 1536], F32, fence_marks=f_st1 + f_st2)
        for (o, n) in NT_KV:
            E("T", lambda g, o=o, n=n: g.matmul(ps_rb[:, 0, o:o + n], P["onesf"][:], rr[0:1, o:o + n],
                                                start=True, stop=True),
              reads=("rrow", "onesf"), writes=(f"ps_rb/0/{o}",))
            E("T", lambda g, o=o, n=n: g.matmul(ps_rb[:, 1, o:o + n], P["onesf"][:],
                                                arow[0:1, 3 * TKV + o:3 * TKV + o + n],
                                                start=True, stop=True),
              reads=("arow/brow", "onesf"), writes=(f"ps_rb/1/{o}",))
        E("T", lambda g: g.matmul(ps_rb[:, 0, TKV:TKV + 1], P["onesf"][:], arow[0:1, 0:1],
                                  start=True, stop=True),
          reads=("arow/rsc", "onesf"), writes=("ps_rb/rsc",))
        E("V", lambda g: g.tensor_copy(out=P["rsc2048"][:], in_=ps_rb[:, 0, TKV:TKV + 1]),
          reads=("ps_rb/rsc",), writes=("rsc2048",))
        rbk0 = tuple(f"ps_rb/0/{o}" for (o, n) in NT_KV)
        rbk1 = tuple(f"ps_rb/1/{o}" for (o, n) in NT_KV)
        E("V", lambda g: g.tensor_copy(out=RB[:, 0, :], in_=ps_rb[:, 0, 0:TKV]), reads=rbk0, writes=("RB/0",))
        E("V", lambda g: g.tensor_copy(out=RB[:, 1, :], in_=ps_rb[:, 1, 0:TKV]), reads=rbk1, writes=("RB/1",))

        for c in range(NCH):
            bi = c % 2
            E(Q, lambda g, c=c, bi=bi: g.dma_start(out=ubuf[:, bi, :], in_=u_T[c * 128:(c + 1) * 128, :]),
              writes=(f"ubuf/{bi}",), qsem=f"qu{bi}")
            eng = "V"
            E(eng, lambda g, bi=bi: g.tensor_mul(out=ubuf[:, bi, :], in0=ubuf[:, bi, :], in1=RB[:, 0, :]),
              reads=(f"ubuf/{bi}", "RB/0"), writes=(f"ubuf/{bi}",))
            E(eng, lambda g, c=c, bi=bi: g.tensor_add(out=xn[:, c, :], in0=ubuf[:, bi, :], in1=RB[:, 1, :]),
              reads=(f"ubuf/{bi}", "RB/1"), writes=(f"xn/{c}",))

        xnkeys = tuple(f"xn/{c}" for c in range(NCH))
        if "xn" in tap_out:
            E(Q, lambda g: g.dma_start(out=tap_out["xn"][:, :, :], in_=xn[:, :, :]), reads=xnkeys, qsem="qout")
        ckpt("A")

        f_rb = mem.pop("p", "ps_rb")
        fA = []
        for nm in ("RB", "arow", "ubfb", "sqb", "ubuf"):
            fA += mem.pop("sr", nm)

        q_sb = mem.push("sr", "q", [128, NCH, TQ], BF16, fence_marks=fA)
        k_sb = mem.push("sr", "k", [128, NCH, TKV], BF16, fence_marks=fA)
        v_tm = mem.push("sr", "V", [128, 9, NH * DH], BF16, fence_marks=fA)
        wst = mem.push("sr", "wst", [128, NCH * 128], F32, fence_marks=fA)
        wbf = mem.push("sr", "wbf", [128, 2, NCH * 128], BF16, fence_marks=fA)
        vbf = mem.push("sr", "vbf", [128, 2, 512], BF16, fence_marks=fA)

        trig = mem.push("sl", "trig", [128, 2, TKV], F32)
        rsc = mem.push("sl", "rsc", [128, 2, 512], F32)

        ps_qkv = mem.push("p", "ps_qkv", [128, 2, 3, 512], F32, fence_marks=f_rb)
        ps_sw = mem.push("p", "ps_sw", [128, 512], F32, fence_marks=f_rb)
        ps_vt = mem.push("p", "ps_vt", [128, 1024], BF16, fence_marks=f_rb)

        def cvt(e, dst, src, rkey, wkey):
            if e == "S":
                E(e, lambda g: g.activation(out=dst, in_=src, func=AF.Copy, bias=0.0, scale=1.0),
                  reads=(rkey,) if isinstance(rkey, str) else rkey,
                  writes=(wkey,) if isinstance(wkey, str) else wkey)
            else:
                E(e, lambda g: g.tensor_copy(out=dst, in_=src),
                  reads=(rkey,) if isinstance(rkey, str) else rkey,
                  writes=(wkey,) if isinstance(wkey, str) else wkey)

        E(Q, lambda g: g.dma_start(out=trig[:, 0, 0:TQ], in_=cq_in[:, :]), writes=("trig/c",), qsem="qtr")
        E(Q, lambda g: g.dma_start(out=trig[:, 1, 0:TQ], in_=sq_in[:, :]), writes=("trig/s",), qsem="qtr")
        trk.settle("qtr", ("trig/c", "trig/s"))

        def rope_chunk(c, kind):
            tens = q_sb if kind == "q" else k_sb
            tiles = NT_Q if kind == "q" else NT_KV
            toff = QOFF if kind == "q" else 0
            for (o, n) in tiles:
                co = o - toff
                E("T", lambda g, c=c, o=o, n=n, co=co, tens=tens: g.matmul(
                    ps_sw[:, 0:n], P["pswap"][:], tens[:, c, co:co + n], start=True, stop=True),
                  reads=(f"{kind}/n/{c}", "pswap"), writes=("ps_sw",))
                E("V", lambda g, n=n, co=co: g.tensor_tensor(
                    out=rsc[:, 0, 0:n], in0=ps_sw[:, 0:n], in1=trig[:, 1, co:co + n], op=ALU.mult),
                  reads=("ps_sw", "trig/s"), writes=("rsc/m",))
                E("V", lambda g, c=c, n=n, co=co, tens=tens: g.tensor_tensor(
                    out=rsc[:, 1, 0:n], in0=tens[:, c, co:co + n], in1=trig[:, 0, co:co + n], op=ALU.mult),
                  reads=(f"{kind}/n/{c}", "trig/c"), writes=("rsc/t1",))
                E("V", lambda g, c=c, n=n, co=co, tens=tens: g.tensor_add(
                    out=tens[:, c, co:co + n], in0=rsc[:, 1, 0:n], in1=rsc[:, 0, 0:n]),
                  reads=("rsc/t1", "rsc/m"), writes=(f"{kind}/{c}",))

        conv_eng = ("S", "V", "S")
        for oc in range(QKV_OC):
            wb = oc % 2
            E(Q, lambda g, oc=oc: g.dma_start(out=wst[:, :], in_=wqkv[oc, :, :]), writes=("wst",), qsem="qw")
            cvt(conv_eng[oc % 3], wbf[:, wb, :], wst[:, :], "wst", f"wbf/{wb}")
            kind = "q" if oc < NCH else ("k" if oc < 2 * NCH else "v")
            c = oc % NCH
            tiles = NT_Q if kind == "q" else NT_KV
            pp = oc % 2
            for ni, (o, n) in enumerate(tiles):
                for kc in range(NCH):
                    E("T", lambda g, wb=wb, kc=kc, pp=pp, ni=ni, o=o, n=n: g.matmul(
                        ps_qkv[:, pp, ni, 0:n], wbf[:, wb, kc * 128:(kc + 1) * 128],
                        xn[:, kc, o:o + n], start=(kc == 0), stop=(kc == NCH - 1)),
                      reads=(f"wbf/{wb}", f"xn/{kc}"), writes=(f"ps_qkv/{pp}/{ni}",))
            if kind in ("q", "k"):
                tens = q_sb if kind == "q" else k_sb
                toff = QOFF if kind == "q" else 0
                for ni, (o, n) in enumerate(tiles):
                    E("S", lambda g, oc=oc, c=c, pp=pp, ni=ni, o=o, n=n, toff=toff, tens=tens: g.activation(
                        out=tens[:, c, o - toff:o - toff + n], in_=ps_qkv[:, pp, ni, 0:n],
                        func=AF.Identity, bias=P["bq"][:, oc:oc + 1], scale=1.0),
                      reads=(f"ps_qkv/{pp}/{ni}", "bq"), writes=(f"{kind}/n/{c}",))
                rope_chunk(c, kind)
                if kind == "q" and c == NCH - 1:
                    E(Q, lambda g: g.dma_start(out=trig[:, 0, :], in_=ck_in[:, :]), writes=("trig/c",), qsem="qtr")
                    E(Q, lambda g: g.dma_start(out=trig[:, 1, :], in_=sk_in[:, :]), writes=("trig/s",), qsem="qtr")
                    trk.settle("qtr", ("trig/c", "trig/s"))
            else:
                vc = oc - 2 * NCH
                for ni, (o, n) in enumerate(tiles):
                    vb = ni % 2
                    E("S", lambda g, oc=oc, pp=pp, ni=ni, n=n, vb=vb: g.activation(
                        out=vbf[:, vb, 0:n], in_=ps_qkv[:, pp, ni, 0:n],
                        func=AF.Identity, bias=P["bq"][:, oc:oc + 1], scale=1.0),
                      reads=(f"ps_qkv/{pp}/{ni}", "bq"), writes=(f"vbf/{vb}",))
                    ntp = n // 128
                    for j in range(ntp):
                        E("T", lambda g, vb=vb, j=j: g.transpose(
                            ps_vt[:, j * 128:(j + 1) * 128], vbf[:, vb, j * 128:(j + 1) * 128],
                            P["idenbf"][:]),
                          reads=(f"vbf/{vb}", "idenbf"), writes=(f"ps_vt/{j}",))
                    jt0 = o // 128
                    E("V", lambda g, vc=vc, jt0=jt0, n=n, ntp=ntp: g.tensor_copy(
                        out=v_tm[:, jt0:jt0 + ntp, vc * 128:(vc + 1) * 128],
                        in_=ps_vt[:, 0:n].rearrange("p (j f) -> p j f", f=128)),
                      reads=tuple(f"ps_vt/{j}" for j in range(ntp)), writes=(f"V/{vc}",))

        qkeys = tuple(f"q/{c}" for c in range(NCH))
        kkeys = tuple(f"k/{c}" for c in range(NCH))
        vkeys = tuple(f"V/{vc}" for vc in range(NCH))
        if "q" in tap_out:
            E(Q, lambda g: g.dma_start(out=tap_out["q"][:, :, :], in_=q_sb[:, :, :]), reads=qkeys, qsem="qout")
        if "k" in tap_out:
            E(Q, lambda g: g.dma_start(out=tap_out["k"][:, :, :], in_=k_sb[:, :, :]), reads=kkeys, qsem="qout")
        if "v" in tap_out:
            E(Q, lambda g: g.dma_start(out=tap_out["v"][:, :, :], in_=v_tm[:, :, :]), reads=vkeys, qsem="qout")
        ckpt("B")

        fB_ps = []
        for nm in ("ps_vt", "ps_sw", "ps_qkv"):
            fB_ps += mem.pop("p", nm)
        fB_l = mem.pop("sl", "rsc") + mem.pop("sl", "trig")
        fB_l += mem.pop("sl", "xn")
        y_sb = mem.push("sl", "y", [128, NCH, TQ], BF16, fence_marks=fB_l)
        fB_r = []
        for nm in ("vbf", "wbf", "wst"):
            fB_r += mem.pop("sr", nm)

        masks = mem.push("sr", "masks", [128, 2, 4 * 272], BF16, fence_marks=fB_r)
        sbf = mem.push("sr", "sbf", [128, NCH, NS], BF16, fence_marks=fB_r)
        svbf = mem.push("sr", "svbf", [NS, NH * DH], BF16, fence_marks=fB_r)
        stf = mem.push("sr", "stf", [128, NCH, NS], F32, fence_marks=fB_r)
        pe_sc = mem.push("sr", "pe", [128, 2, 4, 272], F32, fence_marks=fB_r)
        pe2_sc = mem.push("sr", "pe2", [128, 2, 4, 272], F32, fence_marks=fB_r)
        pb_sc = mem.push("sr", "pb", [128, 2, 4, 272], BF16, fence_marks=fB_r)
        pt_sb = mem.push("sr", "pt", [128, 2, 4, 384], BF16, fence_marks=fB_r)
        srow = mem.push("sr", "srow", [128, 2, 16], F32, fence_marks=fB_r)
        mst = mem.push("sr", "mst", [128, 2, 2048], F32, fence_marks=fB_r)

        E(Q, lambda g: g.dma_start(out=masks[:, 0, :], in_=mask0_in[:, :]), writes=("masks/0",), qsem="qx2")
        E(Q, lambda g: g.dma_start(out=masks[:, 1, :], in_=mask1_in[:, :]), writes=("masks/1",), qsem="qx2")
        E(Q, lambda g: g.dma_start(out=stf[:, :, :], in_=sinkT_in[:, :, :]), writes=("stf",), qsem="qx2")
        E(Q, lambda g: g.dma_start(out=mst[0:NS, 0, :], in_=sinkV_in[:, 0:2048]), writes=("mst/0",), qsem="qx2")
        trk.settle("qx2", ("masks/0", "masks/1", "stf", "mst/0"))
        E("V", lambda g: g.tensor_copy(out=sbf[:, :, :], in_=stf[:, :, :]), reads=("stf",), writes=("sbf",))
        E("V", lambda g: g.tensor_copy(out=svbf[:, :], in_=mst[0:NS, 0, :]), reads=("mst/0",), writes=("svbf",))

        ps_S = mem.push("p", "ps_S", [128, 4, 512], F32, fence_marks=fB_ps)
        ps_PT = mem.push("p", "ps_PT", [128, 4, 512], BF16, fence_marks=fB_ps)
        E("S", lambda g: g.memzero(ps_PT[:, :, :]),
          writes=tuple(f"ps_PT/{i}" for i in range(4)))
        ps_Y = mem.push("p", "ps_Y", [64, 2, 512], F32, fence_marks=fB_ps)

        mean_jobs = [("up", oc, 0) for oc in range(UPCH)] + \
                    [("dn", oc, off) for oc in range(NCH) for off in (0, 2048)]

        def emit_mean(idx):
            which, oc, off = mean_jobs[idx]
            bi = idx % 2
            src = wup[oc, :, :] if which == "up" else wdn[oc, :, off:off + 2048]
            wcol = 0 if which == "up" else 1
            E(Q, lambda g, src=src, bi=bi: g.dma_start(out=mst[:, bi, :], in_=src), writes=(f"mst/{bi}",), qsem=f"qm{bi}")
            E("V", lambda g, bi=bi, idx=idx: g.reduce_sum(out=P["wsums"][:, idx:idx + 1],
                                                          in_=mst[:, bi, :], axis=AX.X,
                                                          apply_absolute_value=True),
              reads=(f"mst/{bi}",), writes=(f"wsums/{idx}",))

        def su(h, g2):
            hp = (h % 2) * 64
            ch = h // 2
            vch = (h * DH) // 128
            pp = (h * 2 + g2) % 2
            for i in range(4):
                qb = g2 * 4 + i
                E("T", lambda g, hp=hp, ch=ch, qb=qb, i=i: g.matmul(
                    ps_S[:, i, 0:16], q_sb[hp:hp + 64, ch, qb * 128:(qb + 1) * 128],
                    sbf[hp:hp + 64, ch, :], start=True, stop=True),
                  reads=(f"q/{ch}", "sbf"), writes=(f"ps_S/{i}",))
                E("T", lambda g, hp=hp, ch=ch, qb=qb, i=i: g.matmul(
                    ps_S[:, i, 16:272], q_sb[hp:hp + 64, ch, qb * 128:(qb + 1) * 128],
                    k_sb[hp:hp + 64, ch, qb * 128:qb * 128 + 256], start=True, stop=True),
                  reads=(f"q/{ch}", f"k/{ch}"), writes=(f"ps_S/{i}",))
                E("V", lambda g, i=i, g2=g2, pp=pp: g.scalar_tensor_tensor(
                    out=pe_sc[:, pp, i, :], in0=ps_S[:, i, 0:272], scalar=1.0,
                    in1=masks[:, g2, i * 272:(i + 1) * 272], op0=ALU.mult, op1=ALU.add),
                  reads=(f"ps_S/{i}", f"masks/{g2}"), writes=(f"pe/{pp}/{i}",))
            E("V", lambda g, pp=pp: g.reduce_max(out=srow[:, pp, 0:4], in_=pe_sc[:, pp, :, :],
                                                 axis=AX.X, negate=True),
              reads=tuple(f"pe/{pp}/{i}" for i in range(4)), writes=(f"srow/{pp}/negmx",))
            for i in range(4):
                E("S", lambda g, pp=pp, i=i: g.activation(
                    out=pe_sc[:, pp, i, :], in_=pe_sc[:, pp, i, :], func=AF.Exp,
                    bias=srow[:, pp, i:i + 1], scale=1.0, accum_out=srow[:, pp, 4 + i:5 + i]),
                  reads=(f"pe/{pp}/{i}", f"srow/{pp}/negmx"),
                  writes=(f"pe/{pp}/{i}", f"srow/{pp}/sum",))
            E("V", lambda g, pp=pp: g.reciprocal(out=srow[:, pp, 8:12], in_=srow[:, pp, 4:8]),
              reads=(f"srow/{pp}/sum",), writes=(f"srow/{pp}/rec",))
            for i in range(4):
                eng = "S"
                if eng == "S":
                    E(eng, lambda g, pp=pp, i=i: g.activation(
                        out=pb_sc[:, pp, i, :], in_=pe_sc[:, pp, i, :], func=AF.Identity,
                        bias=0.0, scale=srow[:, pp, 8 + i:9 + i]),
                      reads=(f"pe/{pp}/{i}", f"srow/{pp}/rec"), writes=(f"pb/{pp}/{i}",))
                else:
                    E(eng, lambda g, pp=pp, i=i: g.tensor_scalar(
                        out=pb_sc[:, pp, i, :], in0=pe_sc[:, pp, i, :],
                        scalar1=srow[:, pp, 8 + i:9 + i], scalar2=None, op0=ALU.mult),
                      reads=(f"pe/{pp}/{i}", f"srow/{pp}/rec"), writes=(f"pb/{pp}/{i}",))
                E("T", lambda g, pp=pp, i=i: g.transpose(
                    ps_PT[:, i, 0:128], pb_sc[:, pp, i, 16:144], P["idenbf"][:]),
                  reads=(f"pb/{pp}/{i}", "idenbf"), writes=(f"ps_PT/{i}",))
                E("T", lambda g, pp=pp, i=i: g.transpose(
                    ps_PT[:, i, 128:256], pb_sc[:, pp, i, 144:272], P["idenbf"][:]),
                  reads=(f"pb/{pp}/{i}",), writes=(f"ps_PT/{i}",))
                E("T", lambda g, pp=pp, i=i: g.transpose(
                    ps_PT[0:16, i, 256:384], pb_sc[:, pp, i, 0:16], P["idenbf"][:]),
                  reads=(f"pb/{pp}/{i}",), writes=(f"ps_PT/{i}",))
            for half in range(2):
                E("V", lambda g, pp=pp, half=half: g.tensor_copy(
                    out=pt_sb[:, pp, 2 * half:2 * half + 2, :],
                    in_=ps_PT[:, 2 * half:2 * half + 2, 0:384]),
                  reads=(f"ps_PT/{2 * half}", f"ps_PT/{2 * half + 1}"),
                  writes=(f"pt/{pp}/{2 * half}", f"pt/{pp}/{2 * half + 1}"))
            yp = (h * 2 + g2) % 2
            for i in range(4):
                qb = g2 * 4 + i
                E("T", lambda g, pp=pp, i=i, h=h: g.matmul(
                    ps_Y[:, yp, i * 128:(i + 1) * 128], svbf[:, h * 64:(h + 1) * 64],
                    pt_sb[0:16, pp, i, 256:384], start=True, stop=False),
                  reads=(f"pt/{pp}/{i}", "svbf"), writes=(f"ps_Y/{yp}",))
                E("T", lambda g, pp=pp, i=i, h=h, qb=qb: g.matmul(
                    ps_Y[:, yp, i * 128:(i + 1) * 128], v_tm[:, qb, h * 64:(h + 1) * 64],
                    pt_sb[:, pp, i, 0:128], start=False, stop=False),
                  reads=(f"pt/{pp}/{i}", f"V/{vch}"), writes=(f"ps_Y/{yp}",))
                E("T", lambda g, pp=pp, i=i, h=h, qb=qb: g.matmul(
                    ps_Y[:, yp, i * 128:(i + 1) * 128], v_tm[:, qb + 1, h * 64:(h + 1) * 64],
                    pt_sb[:, pp, i, 128:256], start=False, stop=True),
                  reads=(f"pt/{pp}/{i}", f"V/{vch}"), writes=(f"ps_Y/{yp}",))
            E("S", lambda g, h=h, g2=g2, yp=yp, hp=hp, ch=ch: g.activation(
                out=y_sb[hp:hp + 64, ch, g2 * 512:(g2 + 1) * 512], in_=ps_Y[:, yp, :],
                func=AF.Copy, bias=0.0, scale=1.0),
              reads=(f"ps_Y/{yp}",), writes=(f"y/{h}/{g2}",))

        import os as _os
        _dsu = int(_os.environ.get("DSU", "0"))
        tap_wacc_final = None
        if "pe" in tap_out:
            su(_dsu, 0)
            E(Q, lambda g: g.dma_start(out=tap_out["pe"][:, :, :], in_=pe_sc[:, 0, :, :]),
              reads=tuple(f"pe/0/{i}" for i in range(4)), qsem="qout")
            E(Q, lambda g: g.dma_start(out=tap_out["pe2"][:, :, :], in_=pe2_sc[:, 0, :, :]),
              reads=tuple(f"pe2/0/{i}" for i in range(4)), qsem="qout")
            E(Q, lambda g: g.dma_start(out=tap_out["sr0"][:, :], in_=srow[:, 0, :]),
              reads=("srow/0/negmx", "srow/0/sum", "srow/0/rec"), qsem="qout")
            if "pb" in tap_out:
                E(Q, lambda g: g.dma_start(out=tap_out["pb"][:, :, :], in_=pb_sc[:, 0, :, :]),
                  reads=tuple(f"pb/0/{i}" for i in range(4)), qsem="qout")
                E(Q, lambda g: g.dma_start(out=tap_out["pt"][:, :, :], in_=pt_sb[:, 0, :, :]),
                  reads=tuple(f"pt/0/{i}" for i in range(4)), qsem="qout")
                E(Q, lambda g, hp=(_dsu % 2) * 64, ch=_dsu // 2: g.dma_start(
                    out=tap_out["ysu"][:, :], in_=y_sb[hp:hp + 64, ch, 0:512]),
                  reads=(f"y/{_dsu}/0",), qsem="qout")
        if "mj" in tap_out:
            _mjdone = [False]
            def _tap_wacc_final():
                E(Q, lambda g: g.dma_start(out=tap_out["mj"][:, 2:4], in_=P["wacc"][:, 0:2]),
                  reads=("wacc",), qsem="qout")
            tap_wacc_final = _tap_wacc_final
        else:
            tap_wacc_final = None
        _nosu = bool(_os.environ.get("DBG_NOSU"))
        mean_i = 0
        for h in range(NH):
            for g2 in range(2):
                if not _nosu and not ("pe" in tap_out and h == _dsu and g2 == 0):
                    su(h, g2)
                want = ((h * 2 + g2 + 1) * len(mean_jobs)) // 64
                while mean_i < min(want, len(mean_jobs)):
                    emit_mean(mean_i)
                    mean_i += 1
        while mean_i < len(mean_jobs):
            emit_mean(mean_i)
            mean_i += 1

        ykeys = tuple(f"y/{h}/{g2}" for h in range(NH) for g2 in range(2))
        if "y" in tap_out:
            E(Q, lambda g: g.dma_start(out=tap_out["y"][:, :, :], in_=y_sb[:, :, :]), reads=ykeys, qsem="qout")

        fC_ps = []
        for nm in ("ps_Y", "ps_PT", "ps_S"):
            fC_ps += mem.pop("p", nm)
        E("V", lambda g: g.reduce_sum(out=P["wacc"][:, 0:1], in_=P["wsums"][:, 0:32], axis=AX.X),
          reads=tuple(f"wsums/{i}" for i in range(32)), writes=("wacc",))
        E("V", lambda g: g.reduce_sum(out=P["wacc"][:, 1:2], in_=P["wsums"][:, 32:64], axis=AX.X),
          reads=tuple(f"wsums/{i}" for i in range(32, 64)), writes=("wacc",))
        ps_w = mem.push("p", "ps_w", [1, 2], F32, fence_marks=fC_ps)
        E("T", lambda g: g.matmul(ps_w[:, 0:1], P["wacc"][:, 0:1], P["onesc"][:], start=True, stop=True),
          reads=("wacc", "onesc"), writes=("ps_w/0",))
        E("T", lambda g: g.matmul(ps_w[:, 1:2], P["wacc"][:, 1:2], P["onesc"][:], start=True, stop=True),
          reads=("wacc", "onesc"), writes=("ps_w/1",))
        E("V", lambda g: g.tensor_copy(out=srow[0:1, 0, 0:2], in_=ps_w[:, 0:2]),
          reads=("ps_w/0", "ps_w/1"), writes=("srow/swr",))
        ps_wb = mem.push("p", "ps_wb", [128, 4], F32)
        E("T", lambda g: g.matmul(ps_wb[:, 0:2], P["onesf"][:], srow[0:1, 0, 0:2], start=True, stop=True),
          reads=("srow/swr", "onesf"), writes=("ps_wb",))
        for nm, col in (("swup", 0), ("swdn", 1)):
            E("V", lambda g, nm=nm, col=col: g.tensor_scalar(
                out=P[nm][:], in0=ps_wb[:, col:col + 1], scalar1=1.0 / (UP * D), scalar2=1e-5,
                op0=ALU.mult, op1=ALU.max),
              reads=("ps_wb",), writes=(nm,))
        for nm, sw, f in (("thrup", "swup", 0.5), ("thrdn", "swdn", 0.5),
                          ("nthrup", "swup", -0.5), ("nthrdn", "swdn", -0.5)):
            E("V", lambda g, nm=nm, sw=sw, f=f: g.tensor_scalar(out=P[nm][:], in0=P[sw][:], scalar1=f,
                                                                scalar2=None, op0=ALU.mult),
              reads=(sw,), writes=(nm,))
        if tap_wacc_final is not None:
            E(Q, lambda g: g.dma_start(out=tap_out["mj"][:, 0:2], in_=P["wsums"][:, 0:2]),
              reads=("wsums/0", "wsums/1"), qsem="qout")
            E(Q, lambda g: g.dma_start(out=tap_out["mj"][:, 2:4], in_=P["wacc"][:, 0:2]),
              reads=("wacc",), qsem="qout")
            E(Q, lambda g: g.dma_start(out=tap_out["mj"][0:1, 4:6], in_=srow[0:1, 0, 0:2]),
              reads=("srow/swr",), qsem="qout")
        if "sw" in tap_out:
            E("V", lambda g: g.tensor_copy(out=P["wacc"][:, 0:1], in_=P["swup"][:, :]),
              reads=("swup",), writes=("wacc",))
            E("V", lambda g: g.tensor_copy(out=P["wacc"][:, 1:2], in_=P["swdn"][:, :]),
              reads=("swdn",), writes=("wacc",))
            E(Q, lambda g: g.dma_start(out=tap_out["sw"][:, :], in_=P["wacc"][:, :]),
              reads=("wacc",), qsem="qout")
        fC_ps += mem.pop("p", "ps_wb")
        fC_ps += mem.pop("p", "ps_w")
        ckpt("C")

        fC = []
        for nm in ("mst", "srow", "pt", "pb", "pe2", "pe", "stf", "svbf", "sbf", "masks",
                   "V", "k", "q"):
            fC += mem.pop("sr", nm)

        acc = mem.push("sr", "acc", [128, NCH, TQ], F32, fence_marks=fC)
        wst2 = mem.push("sr", "wst2", [128, NCH * 128], F32, fence_marks=fC)
        wbf2 = mem.push("sr", "wbf2", [128, 2, NCH * 128], BF16, fence_marks=fC)
        ps_wo = mem.push("p", "ps_wo", [128, 2, 2, 512], F32, fence_marks=fC_ps)

        for c in range(NCH):
            E(Q, lambda g, c=c: g.dma_start(out=acc[:, c, :], in_=u_T[c * 128:(c + 1) * 128, QOFF:QOFF + TQ]),
              writes=(f"acc/{c}",), qsem="qa")
        trk.settle("qa", tuple(f"acc/{c}" for c in range(NCH)))

        for oc in range(NCH):
            wb = oc % 2
            E(Q, lambda g, oc=oc: g.dma_start(out=wst2[:, :], in_=wo[oc, :, :]), writes=("wst2",), qsem="qw")
            cvt(conv_eng[oc % 3], wbf2[:, wb, :], wst2[:, :], "wst2", f"wbf2/{wb}")
            pp = oc % 2
            for ni in range(2):
                for kc in range(NCH):
                    E("T", lambda g, wb=wb, kc=kc, ni=ni, pp=pp: g.matmul(
                        ps_wo[:, pp, ni, :], wbf2[:, wb, kc * 128:(kc + 1) * 128],
                        y_sb[:, kc, ni * 512:(ni + 1) * 512], start=(kc == 0), stop=(kc == NCH - 1)),
                      reads=(f"wbf2/{wb}",) + ykeys, writes=(f"ps_wo/{pp}/{ni}",))
                E("V", lambda g, oc=oc, ni=ni, pp=pp: g.scalar_tensor_tensor(
                    out=acc[:, oc, ni * 512:(ni + 1) * 512], in0=ps_wo[:, pp, ni, :],
                    scalar=P["bo"][:, oc:oc + 1], in1=acc[:, oc, ni * 512:(ni + 1) * 512],
                    op0=ALU.add, op1=ALU.add),
                  reads=(f"ps_wo/{pp}/{ni}", "bo", f"acc/{oc}"), writes=(f"acc/{oc}",))

        acckeys = tuple(f"acc/{c}" for c in range(NCH))
        if "accwo" in tap_out:
            E(Q, lambda g: g.dma_start(out=tap_out["accwo"][:, :, :], in_=acc[:, :, :]), reads=acckeys, qsem="qout")
        ckpt("D")

        fD_ps = mem.pop("p", "ps_wo")
        fD = mem.pop("sr", "wbf2") + mem.pop("sr", "wst2")
        fy = mem.pop("sl", "y")

        hq = mem.push("sl", "hq", [128, UPCH, 512], BF16, fence_marks=fy)
        hbf = mem.push("sl", "hbf", [128, UPCH, 512], BF16, fence_marks=fy)

        xq = mem.push("sr", "xq", [128, NCH, 512], BF16, fence_marks=fD)
        sxdq = mem.push("sr", "sxdq", [128, 6, 256], F32, fence_marks=fD)
        qrow = mem.push("sr", "qrow", [128, 16], F32, fence_marks=fD)
        qr2 = mem.push("sr", "qr2", [1, 1024], F32, fence_marks=fD)

        def quant_tile(xht, nch, slots, spare, key, width, psq, pk, dq_rstd_q0, out_fn, dbg=False):
            E("V", lambda g, s0=slots[0]: g.tensor_copy(out=sxdq[:, 2, 0:width], in_=xht[:, s0, :]),
              reads=(f"{key}/{slots[0]}",), writes=("sxdq/2",))
            E("V", lambda g, s0=slots[0]: g.tensor_copy(out=sxdq[:, 4, 0:width], in_=xht[:, s0, :]),
              reads=(f"{key}/{slots[0]}",), writes=("sxdq/4",))
            for c in range(1, nch):
                s_ = slots[c]
                E("V", lambda g, s_=s_: g.tensor_tensor(
                    out=sxdq[:, 2, 0:width], in0=sxdq[:, 2, 0:width], in1=xht[:, s_, :], op=ALU.max),
                  reads=(f"{key}/{s_}", "sxdq/2"), writes=("sxdq/2",))
                E("V", lambda g, s_=s_: g.tensor_tensor(
                    out=sxdq[:, 4, 0:width], in0=sxdq[:, 4, 0:width], in1=xht[:, s_, :], op=ALU.min),
                  reads=(f"{key}/{s_}", "sxdq/4"), writes=("sxdq/4",))
            E("V", lambda g: g.tensor_scalar(out=sxdq[:, 4, 0:width], in0=sxdq[:, 4, 0:width],
                                             scalar1=-1.0, scalar2=None, op0=ALU.mult),
              reads=("sxdq/4",), writes=("sxdq/4",))
            E("V", lambda g: g.tensor_tensor(out=sxdq[:, 2, 0:width], in0=sxdq[:, 2, 0:width],
                                             in1=sxdq[:, 4, 0:width], op=ALU.max),
              reads=("sxdq/2", "sxdq/4"), writes=("sxdq/2",))
            ntp = width // 128
            for j in range(ntp):
                E("T", lambda g, j=j: g.transpose(psq[:, 0, j * 128:(j + 1) * 128],
                                                  sxdq[:, 2, j * 128:(j + 1) * 128], P["idenf"][:]),
                  reads=("sxdq/2", "idenf"), writes=(f"{pk}/a{j}",))
            E("V", lambda g, ntp=ntp: g.reduce_max(
                out=qrow[:, 0:ntp], in_=psq[:, 0, 0:ntp * 128].rearrange("p (j t) -> p j t", j=ntp),
                axis=AX.X),
              reads=tuple(f"{pk}/a{j}" for j in range(ntp)), writes=("qrow/mx",))
            if dbg:
                ckpt("E0c")
            if dq_rstd_q0 is not None:
                for j in range(ntp):
                    E("T", lambda g, j=j, q0=dq_rstd_q0: g.transpose(
                        psq[:, 1, j * 128:j * 128 + 1],
                        P["rrow"][0:1, q0 + j * 128:q0 + (j + 1) * 128], P["idenf"][0:1, 0:1]),
                      reads=("rrow", "idenf"), writes=(f"{pk}/b{j}",))
                    E("V", lambda g, j=j: g.tensor_copy(out=qrow[:, 8 + j:9 + j],
                                                        in_=psq[:, 1, j * 128:j * 128 + 1]),
                      reads=(f"{pk}/b{j}",), writes=("qrow/rst",))
                E("V", lambda g, ntp=ntp: g.reciprocal(out=qrow[:, 12:12 + ntp], in_=qrow[:, 8:8 + ntp]),
                  reads=("qrow/rst",), writes=("qrow/irst",))
                E("V", lambda g, ntp=ntp: g.tensor_scalar(out=qrow[:, 12:12 + ntp],
                                                          in0=qrow[:, 12:12 + ntp], scalar1=1e-5,
                                                          scalar2=None, op0=ALU.mult),
                  reads=("qrow/irst",), writes=("qrow/irst",))
                E("V", lambda g, ntp=ntp: g.tensor_tensor(out=qrow[:, 0:ntp], in0=qrow[:, 0:ntp],
                                                          in1=qrow[:, 12:12 + ntp], op=ALU.max),
                  reads=("qrow/mx", "qrow/irst"), writes=("qrow/mx",))
            else:
                E("V", lambda g, ntp=ntp: g.tensor_scalar(out=qrow[:, 0:ntp], in0=qrow[:, 0:ntp],
                                                          scalar1=1e-5, scalar2=None, op0=ALU.max),
                  reads=("qrow/mx",), writes=("qrow/mx",))
            E("V", lambda g, ntp=ntp: g.reciprocal(out=qrow[:, 4:4 + ntp], in_=qrow[:, 0:ntp]),
              reads=("qrow/mx",), writes=("qrow/sx",))
            E("V", lambda g, ntp=ntp: g.tensor_tensor(out=qrow[:, 12:12 + ntp],
                                                      in0=qrow[:, 4:4 + ntp], in1=qrow[:, 0:ntp],
                                                      op=ALU.mult),
              reads=("qrow/sx", "qrow/mx"), writes=("qrow/nt",))
            E("V", lambda g, ntp=ntp: g.tensor_scalar(out=qrow[:, 12:12 + ntp],
                                                      in0=qrow[:, 12:12 + ntp], scalar1=-1.0,
                                                      scalar2=2.0, op0=ALU.mult, op1=ALU.add),
              reads=("qrow/nt",), writes=("qrow/nt",))
            E("V", lambda g, ntp=ntp: g.tensor_tensor(out=qrow[:, 4:4 + ntp],
                                                      in0=qrow[:, 4:4 + ntp],
                                                      in1=qrow[:, 12:12 + ntp], op=ALU.mult),
              reads=("qrow/sx", "qrow/nt"), writes=("qrow/sx",))
            E("V", lambda g, ntp=ntp: g.tensor_scalar(out=qrow[:, 4:4 + ntp], in0=qrow[:, 4:4 + ntp],
                                                      scalar1=127.0, scalar2=None, op0=ALU.mult),
              reads=("qrow/sx",), writes=("qrow/sx",))
            sw = P["swup"] if dq_rstd_q0 is not None else P["swdn"]
            swkey = "swup" if dq_rstd_q0 is not None else "swdn"
            E("V", lambda g, ntp=ntp, sw=sw: g.tensor_scalar(out=qrow[:, 12:12 + ntp],
                                                             in0=qrow[:, 0:ntp],
                                                             scalar1=1.0 / 127.0,
                                                             scalar2=sw[:, 0:1],
                                                             op0=ALU.mult, op1=ALU.mult),
              reads=("qrow/mx", swkey), writes=("qrow/dq",))
            if dq_rstd_q0 is not None:
                E("V", lambda g, ntp=ntp: g.tensor_tensor(out=qrow[:, 12:12 + ntp],
                                                          in0=qrow[:, 12:12 + ntp],
                                                          in1=qrow[:, 8:8 + ntp], op=ALU.mult),
                  reads=("qrow/dq", "qrow/rst"), writes=("qrow/dq",))
            for j in range(ntp):
                E("T", lambda g, j=j: g.transpose(psq[0:1, 2, j * 128:(j + 1) * 128],
                                                  qrow[:, 4 + j:5 + j], P["idenf"][:]),
                  reads=("qrow/sx", "idenf"), writes=(f"{pk}/s{j}",))
                E("T", lambda g, j=j: g.transpose(psq[0:1, 3, j * 128:(j + 1) * 128],
                                                  qrow[:, 12 + j:13 + j], P["idenf"][:]),
                  reads=("qrow/dq", "idenf"), writes=(f"{pk}/d{j}",))
            E("V", lambda g: g.tensor_copy(out=qr2[0:1, 0:width], in_=psq[0:1, 2, 0:width]),
              reads=tuple(f"{pk}/s{j}" for j in range(ntp)), writes=("qr2/sx",))
            E("V", lambda g: g.tensor_copy(out=qr2[0:1, 512:512 + width], in_=psq[0:1, 3, 0:width]),
              reads=tuple(f"{pk}/d{j}" for j in range(ntp)), writes=("qr2/dq",))
            for j in range(ntp):
                E("T", lambda g, j=j: g.matmul(psq[:, 0, j * 128:(j + 1) * 128], P["onesf"][:],
                                               qr2[0:1, j * 128:(j + 1) * 128], start=True, stop=True),
                  reads=("qr2/sx", "onesf"), writes=(f"{pk}/a{j}",))
                E("T", lambda g, j=j: g.matmul(psq[:, 1, 128 + j * 128:128 + (j + 1) * 128],
                                               P["onesf"][:],
                                               qr2[0:1, 512 + j * 128:512 + (j + 1) * 128],
                                               start=True, stop=True),
                  reads=("qr2/dq", "onesf"), writes=(f"{pk}/e{j}",))
            E("V", lambda g: g.tensor_copy(out=sxdq[:, 0, 0:width], in_=psq[:, 0, 0:width]),
              reads=tuple(f"{pk}/a{j}" for j in range(ntp)), writes=("sxdq/0",))
            E("V", lambda g: g.tensor_copy(out=sxdq[:, 1, 0:width], in_=psq[:, 1, 128:128 + width]),
              reads=tuple(f"{pk}/e{j}" for j in range(ntp)), writes=("sxdq/1",))
            if dbg:
                ckpt("E0d")
            for c in range(nch):
                out_fn(c, slots[c])

        def butterflies(xht, nch, slots, spare, key, width):
            s = 1
            while s < nch:
                pairs = [c for c in range(nch) if not (c & s)]
                for pi, c in enumerate(pairs):
                    c2 = c + s
                    eng = "V"
                    spi = 0 if eng == "V" else 1
                    sp = spare[spi]
                    a, b = slots[c], slots[c2]
                    E(eng, lambda g, a=a, b=b, sp=sp: g.tensor_add(
                        out=xht[:, sp, :], in0=xht[:, a, :], in1=xht[:, b, :]),
                      reads=(f"{key}/{a}", f"{key}/{b}"), writes=(f"{key}/{sp}",))
                    E(eng, lambda g, a=a, b=b: g.tensor_sub(
                        out=xht[:, b, :], in0=xht[:, a, :], in1=xht[:, b, :]),
                      reads=(f"{key}/{a}", f"{key}/{b}"), writes=(f"{key}/{b}",))
                    spare[spi] = a
                    slots[c] = sp
                s *= 2

        last_dn_ps = fD_ps
        last_dn_r = []
        xqkeys = tuple(f"xq/{c}/{ql}" for c in range(NCH) for ql in range(2))
        hqkeys = tuple(f"hq/{c}/{qo}" for c in range(UPCH) for qo in (0, 256))
        hbkeys = tuple(f"hbf/{c}" for c in range(UPCH))

        def emit_ffn_half(hb, last_dn_ps, last_dn_r):
            CH0 = QOFF + hb * 512

            ub2 = mem.push("sr", "ub2", [128, 2, 256], F32, fence_marks=last_dn_r)
            xh = mem.push("sr", "xh", [128, 18, 256], F32, fence_marks=last_dn_r)
            ps_x1 = mem.push("p", "ps_x1", [128, 2, 512], F32, fence_marks=last_dn_ps)
            ps_tp = mem.push("p", "ps_tp", [128, 4, 512], F32, fence_marks=last_dn_ps)

            for ql in range(2):
                T0 = CH0 + ql * 256
                slots = list(range(16))
                spare = [16, 17]
                if ql == 0 and hb == 0:
                    ckpt("D2")
                for c in range(NCH):
                    bi = c % 2
                    E(Q, lambda g, c=c, bi=bi, T0=T0: g.dma_start(
                        out=ub2[:, bi, :], in_=u_T[c * 128:(c + 1) * 128, T0:T0 + 256]),
                      writes=(f"ub2/{bi}",), qsem=f"qu{bi}")
                    if c == 0 and ql == 0 and hb == 0:
                        ckpt("D3")
                    E("T", lambda g, bi=bi: g.matmul(ps_x1[:, bi, 0:256], P["h128f"][:], ub2[:, bi, :],
                                                     start=True, stop=True),
                      reads=(f"ub2/{bi}", "h128f"), writes=(f"ps_x1/{bi}",))
                    if c == 0 and ql == 0 and hb == 0:
                        ckpt("D4")
                    E("S", lambda g, c=c, bi=bi: g.activation(out=xh[:, c, :], in_=ps_x1[:, bi, 0:256],
                                                              func=AF.Identity, bias=0.0,
                                                              scale=P["rsc2048"][:, 0:1]),
                      reads=(f"ps_x1/{bi}", "rsc2048"), writes=(f"xh/{c}",))
                    if c == 1 and ql == 0 and hb == 0:
                        ckpt("D5")
                    if c == 3 and ql == 0 and hb == 0:
                        ckpt("D6")
                    if c == 7 and ql == 0 and hb == 0:
                        ckpt("D7")
                    if c == 15 and ql == 0 and hb == 0:
                        ckpt("D8")
                if ql == 0 and hb == 0:
                    ckpt("E0a")
                butterflies(xh, 16, slots, spare, "xh", 256)

                E("V", lambda g, T0=T0: g.tensor_tensor(
                    out=qr2[0:1, 512:768], in0=P["mrow"][0:1, T0:T0 + 256],
                    in1=P["sc2048"][0:1, 0:1].broadcast_to([1, 256]), op=ALU.mult),
                  reads=("mrow", "sc2048"), writes=("qr2/corr",))
                E("V", lambda g, s0=slots[0], xh=xh: g.tensor_sub(
                    out=xh[0:1, s0, :], in0=xh[0:1, s0, :], in1=qr2[0:1, 512:768]),
                  reads=(f"xh/{slots[0]}", "qr2/corr"), writes=(f"xh/{slots[0]}",))
                if ql == 0 and hb == 0:
                    ckpt("E0b")

                def out_q(c, sl, ql=ql):
                    eng = "V"
                    sp = spare[0]
                    E(eng, lambda g, sl=sl, sp=sp: g.tensor_tensor(
                        out=xh[:, sp, :].bitcast(I32), in0=xh[:, sl, :], in1=sxdq[:, 0, :],
                        op=ALU.mult),
                      reads=(f"xh/{sl}", "sxdq/0"), writes=(f"xh/{sp}",))
                    E(eng, lambda g, c=c, sp=sp, ql=ql: g.tensor_tensor(
                        out=xq[:, c, ql * 256:(ql + 1) * 256], in0=xh[:, sp, :].bitcast(I32),
                        in1=sxdq[:, 1, :], op=ALU.mult),
                      reads=(f"xh/{sp}", "sxdq/1"), writes=(f"xq/{c}/{ql}",))

                quant_tile(xh, 16, slots, spare, "xh", 256, ps_tp, "ps_tp", T0, out_q, dbg=(ql == 0 and hb == 0))
                if ql == 0 and hb == 0:
                    ckpt("E1")

            if hb == 0:
                ckpt("EA")
            if "xq" in tap_out and hb == 0:
                E(Q, lambda g: g.dma_start(out=tap_out["xq"][:, :, :], in_=xq[:, :, :]), reads=xqkeys, qsem="qout")

            f_tp = mem.pop("p", "ps_tp")
            f_x1 = mem.pop("p", "ps_x1")
            f_xh = mem.pop("sr", "xh") + mem.pop("sr", "ub2")

            wst3 = mem.push("sr", "wst3", [128, NCH * 128], F32, fence_marks=f_xh)
            wtb = mem.push("sr", "wtb", [128, 2, NCH * 128], BF16, fence_marks=f_xh)
            sgn = mem.push("sr", "sgn", [128, NCH * 128], BF16, fence_marks=f_xh)
            rsc2 = mem.push("sr", "rsc2", [128, 2, 512], F32, fence_marks=f_xh)
            ps_up = mem.push("p", "ps_up", [128, 2, 512], F32, fence_marks=f_x1 + f_tp)

            for oc in range(UPCH):
                wb = oc % 2
                if hb == 0:
                    E(Q, lambda g, oc=oc: g.dma_start(out=wst3[:, :], in_=wup[oc, :, :]), writes=("wst3",), qsem="qw")
                    E("V", lambda g: g.tensor_scalar(out=sgn[:, :], in0=wst3[:, :],
                                                     scalar1=P["nthrup"][:, 0:1], scalar2=None,
                                                     op0=ALU.is_lt),
                      reads=("wst3", "nthrup"), writes=("sgn",))
                    E("V", lambda g, wb=wb: g.tensor_scalar(out=wtb[:, wb, :], in0=wst3[:, :],
                                                            scalar1=P["thrup"][:, 0:1], scalar2=None,
                                                            op0=ALU.is_gt),
                      reads=("wst3", "thrup"), writes=(f"wtb/{wb}",))
                    E("V", lambda g, wb=wb: g.tensor_sub(out=wtb[:, wb, :], in0=wtb[:, wb, :],
                                                         in1=sgn[:, :]),
                      reads=("sgn", f"wtb/{wb}"), writes=(f"wtb/{wb}",))
                    E(Q, lambda g, oc=oc, wb=wb: g.dma_start(out=wup_cache[oc, :, :], in_=wtb[:, wb, :]),
                      reads=(f"wtb/{wb}",), writes=(f"wupc/{oc}",), qsem="qc")
                else:
                    E(Q, lambda g, oc=oc, wb=wb: g.dma_start(out=wtb[:, wb, :], in_=wup_cache[oc, :, :]),
                      reads=(f"wupc/{oc}",), writes=(f"wtb/{wb}",), qsem=f"qt{wb}")
                pp = oc % 2
                for ql in range(2):
                    for kc in range(NCH):
                        E("T", lambda g, wb=wb, kc=kc, pp=pp, ql=ql: g.matmul(
                            ps_up[:, pp, ql * 256:(ql + 1) * 256],
                            wtb[:, wb, kc * 128:(kc + 1) * 128],
                            xq[:, kc, ql * 256:(ql + 1) * 256],
                            start=(kc == 0), stop=(kc == NCH - 1)),
                          reads=(f"wtb/{wb}", f"xq/{kc}/{ql}"), writes=(f"ps_up/{pp}",))
                E("S", lambda g, pp=pp: g.activation(out=rsc2[:, pp, :], in_=ps_up[:, pp, :],
                                                     func=AF.Relu, bias=0.0, scale=1.0),
                  reads=(f"ps_up/{pp}",), writes=(f"rsc2/{pp}",))
                E("V", lambda g, oc=oc, pp=pp: g.tensor_mul(out=hbf[:, oc, :], in0=rsc2[:, pp, :],
                                                            in1=rsc2[:, pp, :]),
                  reads=(f"rsc2/{pp}",), writes=(f"hbf/{oc}",))

            if hb == 0:
                trk.settle("qc", tuple(f"wupc/{oc}" for oc in range(UPCH)))
                ckpt("EB")
            if "h" in tap_out and hb == 0:
                E(Q, lambda g: g.dma_start(out=tap_out["h"][:, :, :], in_=hbf[:, :, :]), reads=hbkeys, qsem="qout")

            f_up = mem.pop("p", "ps_up")
            f_s2 = mem.pop("sr", "rsc2") + mem.pop("sr", "sgn") + mem.pop("sr", "wtb") + \
                mem.pop("sr", "wst3")

            xh2 = mem.push("sr", "xh2", [128, 34, 256], F32, fence_marks=f_s2)
            ps_x2 = mem.push("p", "ps_x2", [128, 2, 512], F32, fence_marks=f_up)
            ps_t2 = mem.push("p", "ps_t2", [128, 4, 512], F32, fence_marks=f_up)

            for ql in range(2):
                QO = ql * 256
                slots2 = list(range(32))
                spare2 = [32, 33]
                for c in range(UPCH):
                    bi = c % 2
                    E("T", lambda g, c=c, bi=bi, QO=QO: g.matmul(
                        ps_x2[:, bi, 0:256], P["h128bf"][:], hbf[:, c, QO:QO + 256], start=True, stop=True),
                      reads=(f"hbf/{c}", "h128bf"), writes=(f"ps_x2/{bi}",))
                    E("S", lambda g, c=c, bi=bi: g.activation(out=xh2[:, c, :], in_=ps_x2[:, bi, 0:256],
                                                              func=AF.Copy, bias=0.0, scale=0.015625),
                      reads=(f"ps_x2/{bi}",), writes=(f"xh2/{c}",))

                butterflies(xh2, 32, slots2, spare2, "xh2", 256)

                def out_h(c, sl, ql=ql):
                    eng = "V"
                    sp = spare2[0]
                    E(eng, lambda g, sl=sl, sp=sp: g.tensor_tensor(
                        out=xh2[:, sp, :].bitcast(I32), in0=xh2[:, sl, :], in1=sxdq[:, 0, :],
                        op=ALU.mult),
                      reads=(f"xh2/{sl}", "sxdq/0"), writes=(f"xh2/{sp}",))
                    E(eng, lambda g, c=c, sp=sp, ql=ql: g.tensor_tensor(
                        out=hq[:, c, ql * 256:(ql + 1) * 256], in0=xh2[:, sp, :].bitcast(I32),
                        in1=sxdq[:, 1, :], op=ALU.mult),
                      reads=(f"xh2/{sp}", "sxdq/1"), writes=(f"hq/{c}/{ql * 256}",))

                quant_tile(xh2, 32, slots2, spare2, "xh2", 256, ps_t2, "ps_t2", None, out_h)
                if ql == 0 and hb == 0:
                    ckpt("E2")

            if hb == 0:
                ckpt("EC")
            if "hq" in tap_out and hb == 0:
                E(Q, lambda g: g.dma_start(out=tap_out["hq"][:, :, :], in_=hq[:, :, :]), reads=hqkeys, qsem="qout")

            f_t2 = mem.pop("p", "ps_t2")
            f_x2 = mem.pop("p", "ps_x2")
            f_xh2 = mem.pop("sr", "xh2")

            wst4 = mem.push("sr", "wst4", [128, UPCH * 128], F32, fence_marks=f_xh2)
            wtb2 = mem.push("sr", "wtb2", [128, 2, UPCH * 128], BF16, fence_marks=f_xh2)
            sgn2 = mem.push("sr", "sgn2", [128, UPCH * 128], BF16, fence_marks=f_xh2)
            ps_dn = mem.push("p", "ps_dn", [128, 2, 512], F32, fence_marks=f_x2 + f_t2)

            for oc in range(NCH):
                wb = oc % 2
                if hb == 0:
                    E(Q, lambda g, oc=oc: g.dma_start(out=wst4[:, :], in_=wdn[oc, :, :]), writes=("wst4",), qsem="qw")
                    E("V", lambda g: g.tensor_scalar(out=sgn2[:, :], in0=wst4[:, :],
                                                     scalar1=P["nthrdn"][:, 0:1], scalar2=None,
                                                     op0=ALU.is_lt),
                      reads=("wst4", "nthrdn"), writes=("sgn2",))
                    E("V", lambda g, wb=wb: g.tensor_scalar(out=wtb2[:, wb, :], in0=wst4[:, :],
                                                            scalar1=P["thrdn"][:, 0:1], scalar2=None,
                                                            op0=ALU.is_gt),
                      reads=("wst4", "thrdn"), writes=(f"wtb2/{wb}",))
                    E("V", lambda g, wb=wb: g.tensor_sub(out=wtb2[:, wb, :], in0=wtb2[:, wb, :],
                                                         in1=sgn2[:, :]),
                      reads=("sgn2", f"wtb2/{wb}"), writes=(f"wtb2/{wb}",))
                    E(Q, lambda g, oc=oc, wb=wb: g.dma_start(out=wdn_cache[oc, :, :], in_=wtb2[:, wb, :]),
                      reads=(f"wtb2/{wb}",), writes=(f"wdnc/{oc}",), qsem="qc")
                else:
                    E(Q, lambda g, oc=oc, wb=wb: g.dma_start(out=wtb2[:, wb, :], in_=wdn_cache[oc, :, :]),
                      reads=(f"wdnc/{oc}",), writes=(f"wtb2/{wb}",), qsem=f"qt{wb}")
                pp = oc % 2
                for ql in range(2):
                    for kc in range(UPCH):
                        E("T", lambda g, wb=wb, kc=kc, pp=pp, ql=ql: g.matmul(
                            ps_dn[:, pp, ql * 256:(ql + 1) * 256],
                            wtb2[:, wb, kc * 128:(kc + 1) * 128],
                            hq[:, kc, ql * 256:(ql + 1) * 256],
                            start=(kc == 0), stop=(kc == UPCH - 1)),
                          reads=(f"wtb2/{wb}", f"hq/{kc}/{ql * 256}"), writes=(f"ps_dn/{pp}",))
                E("V", lambda g, oc=oc, pp=pp, hb=hb: g.tensor_add(
                    out=acc[:, oc, hb * 512:(hb + 1) * 512], in0=acc[:, oc, hb * 512:(hb + 1) * 512],
                    in1=ps_dn[:, pp, :]),
                  reads=(f"ps_dn/{pp}", f"acc/{oc}"), writes=(f"acc/{oc}", f"accf/{oc}/{hb}"))

            if hb == 0:
                trk.settle("qc", tuple(f"wdnc/{oc}" for oc in range(NCH)))
                ckpt("ED")
            last_dn_ps = mem.pop("p", "ps_dn")
            last_dn_r = mem.pop("sr", "sgn2") + mem.pop("sr", "wtb2") + mem.pop("sr", "wst4")
            return last_dn_ps, last_dn_r

        for hb in range(2):
            last_dn_ps, last_dn_r = emit_ffn_half(hb, last_dn_ps, last_dn_r)

        for c in range(NCH):
            E(Q, lambda g, c=c: g.dma_start(out=out_T[c * 128:(c + 1) * 128, :], in_=acc[:, c, :]),
              reads=(f"acc/{c}", f"accf/{c}/0", f"accf/{c}/1"), qsem="qout")


    except _Stop:
        pass


    es = contextlib.ExitStack()
    for e in ENGS:
        trk.sems[e] = es.enter_context(nc.semaphore(f"sem{e}"))
    for qn in trk.qcnt:
        trk.sems[qn] = es.enter_context(nc.semaphore(f"sem_{qn}"))
    qout_total = trk.qcnt.get("qout", 0)
    with es, nc.Block() as block:
        @block.sync
        def _(eng):
            for f in trk.prog["Q"]:
                f(eng)
            if qout_total:
                eng.wait_ge(trk.sems["qout"], qout_total)

        @block.tensor
        def _(eng):
            for f in trk.prog["T"]:
                f(eng)

        @block.scalar
        def _(eng):
            for f in trk.prog["S"]:
                f(eng)

        @block.vector
        def _(eng):
            for f in trk.prog["V"]:
                f(eng)

        @block.gpsimd
        def _(eng):
            for f in trk.prog["P"]:
                f(eng)

    return nc, tap_out


_CACHE = {}


def _get_program(taps=()):
    key = tuple(sorted(taps))
    if key not in _CACHE:
        _CACHE[key] = build_program(taps)
    return _CACHE[key]


def host_inputs(L0_attn_u, L0_attn_Wqkv, L0_attn_bqkv, L0_attn_Wo, L0_attn_bo,
                L0_attn_sinks, L0_ffn_w_up, L0_ffn_w_down):
    wqkv_s = _strips(np.asarray(L0_attn_Wqkv, np.float32), QKV_OC, NCH)
    wo_s = _strips(np.asarray(L0_attn_Wo, np.float32), NCH, NCH)
    wup_s = _strips(np.asarray(L0_ffn_w_up, np.float32), UPCH, NCH)
    wdn_s = _strips(np.asarray(L0_ffn_w_down, np.float32), NCH, UPCH)
    bq_t = np.ascontiguousarray(np.asarray(L0_attn_bqkv, np.float32).reshape(QKV_OC, 128).T)
    bo_t = np.ascontiguousarray(np.asarray(L0_attn_bo, np.float32).reshape(NCH, 128).T)
    sinks = np.asarray(L0_attn_sinks, np.float32)
    s4 = sinks.reshape(16, 2, NS, DH)
    sinkT = np.ascontiguousarray(s4.transpose(1, 3, 0, 2).reshape(128, 16, NS))
    sinkV = np.ascontiguousarray(sinks.transpose(1, 0, 2).reshape(NS, NH * DH))

    h128 = _hadamard128()
    consts = np.zeros((128, 8), np.float32)
    for j, v in enumerate([0.0, 1.0, 1e-5, -0.5, 2048.0, 0.015625]):
        consts[:, j] = v

    shared = dict(
        wqkv=wqkv_s, bqkv_t=bq_t, wo=wo_s, bo_t=bo_t, wup=wup_s, wdn=wdn_s,
        sinkT=sinkT, sinkV=sinkV,
        h128bf=h128.astype(ml_dtypes.bfloat16), h128f=h128,
        idenbf=np.eye(128).astype(ml_dtypes.bfloat16), idenf=np.eye(128, dtype=np.float32),
        pswap=_pswap().astype(ml_dtypes.bfloat16),
        consts=consts, onesf=np.ones((1, 128), np.float32), onesc=np.ones((128, 1), np.float32),
    )

    u = np.asarray(L0_attn_u, np.float32)
    in_maps = []
    for core in range(8):
        b, half = core // 2, core % 2
        qstart = half * 1024
        u_Tc = np.zeros((D, TKV), np.float32)
        lo, hi = qstart - 64, qstart - 64 + TKV
        slo, shi = max(lo, 0), min(hi, L)
        u_Tc[:, slo - lo:shi - lo] = u[b, slo:shi, :].T
        cq, sq, ck, sk = _trig_tables(qstart)
        m0, m1 = _masks(qstart)
        m = dict(shared)
        m.update(u_T=np.ascontiguousarray(u_Tc), cq=cq, sq=sq, ck=ck, sk=sk,
                 mask0=m0, mask1=m1)
        in_maps.append(m)
    return in_maps


def kernel(**inputs):
    nc, _ = _get_program()
    in_maps = host_inputs(**inputs)
    res = run_bass_kernel_spmd(nc, in_maps, core_ids=list(range(8)))
    out = np.empty((B, L, D), np.float32)
    for core in range(8):
        b, half = core // 2, core % 2
        out[b, half * 1024:(half + 1) * 1024, :] = np.asarray(res.results[core]["out_T"]).T
    return out


# version 05
# speedup vs baseline: 1.7286x; 1.0171x over previous
import contextlib
import numpy as np
import ml_dtypes
import concourse.bass as bass
import concourse.mybir as mybir
from concourse.bass_utils import run_bass_kernel_spmd

F32 = mybir.dt.float32
BF16 = mybir.dt.bfloat16
I32 = mybir.dt.int32
AF = mybir.ActivationFunctionType
ALU = mybir.AluOpType
AX = mybir.AxisListType

B, L, D = 4, 2048, 2048
NCH = D // 128
TKV = 1152
TQ = 1024
QOFF = 64
NH, DH, NS = 32, 64, 16
UP = 4096
UPCH = UP // 128
QKV_OC = 3 * NCH
NEG = -1.0e30
NT_KV = ((0, 512), (512, 512), (1024, 128))
NT_Q = ((64, 512), (576, 512))

ENGS = ("T", "S", "V", "P", "Q")


class Trk:
    def __init__(self):
        self.prog = {e: [] for e in ENGS}
        self.cnt = {e: 0 for e in ENGS}
        self.qcnt = {}
        self.writer = {}
        self.readers = {}
        self.waited = {e: {} for e in ENGS}
        self.pfence = {}
        self.sems = {}
        import os
        self.log = open("/tmp/trk.log", "w") if os.environ.get("TRKLOG") else None

    def _add_wait(self, e, oe, val):
        if val <= 0:
            return
        if self.waited[e].get(oe, 0) >= val:
            return
        self.waited[e][oe] = val
        sems = self.sems
        self.prog[e].append(lambda eng, oe=oe, val=val: eng.wait_ge(sems[oe], val))

    def emit(self, e, fn, reads=(), writes=(), tag="", qsem=None):
        need = {}
        for r in set(reads) | set(writes):
            p = r.split("/", 1)[0]
            marks = self.pfence.get(p)
            if marks:
                for (oe, c) in marks:
                    if oe != e:
                        need[oe] = max(need.get(oe, 0),
                                       self.qcnt[oe] if oe in self.qcnt else c)
        skipT = (e == "T")

        def mval(oe, c):
            return self.qcnt[oe] if oe in self.qcnt else c

        for r in reads:
            w = self.writer.get(r)
            if w is not None and not (skipT and w[0] == "T"):
                need[w[0]] = max(need.get(w[0], 0), mval(*w))
        for r in writes:
            w = self.writer.get(r)
            if w is not None and not (skipT and w[0] == "T"):
                need[w[0]] = max(need.get(w[0], 0), mval(*w))
            for oe, c in self.readers.get(r, {}).items():
                if not (skipT and oe == "T"):
                    need[oe] = max(need.get(oe, 0), mval(oe, c))
        for oe, v in need.items():
            self._add_wait(e, oe, v)
        if e == "Q":
            assert qsem is not None
            self.qcnt[qsem] = self.qcnt.get(qsem, 0) + 16
            me = self.qcnt[qsem]
            mark = (qsem, me)
            sems = self.sems
            self.prog["Q"].append(lambda eng, fn=fn, qsem=qsem: fn(eng).then_inc(sems[qsem], 16))
        else:
            self.cnt[e] += 1
            me = self.cnt[e]
            mark = (e, me)
            sems = self.sems
            self.prog[e].append(lambda eng, fn=fn, e=e: fn(eng).then_inc(sems[e], 1))
        if self.log is not None:
            self.log.write(f"{e}/{qsem} cnt={me} waits={need} reads={reads} writes={writes} {tag}\n")
        for r in writes:
            self.writer[r] = mark
            self.readers[r] = {}
        for r in reads:
            self.readers.setdefault(r, {})[mark[0]] = me
        return mark

    def settle(self, qsem, regions):
        total = self.qcnt.get(qsem, 0)
        for r in regions:
            w = self.writer.get(r)
            if w is not None and w[0] == qsem:
                self.writer[r] = (qsem, total)

    def marks_for(self, prefix):
        out = []
        for r, w in list(self.writer.items()):
            if r == prefix or r.startswith(prefix + "/"):
                if w is not None:
                    out.append(w)
                for oe, c in self.readers.get(r, {}).items():
                    out.append((oe, c))
        return out

    def clear_prefix(self, prefix):
        self.pfence.pop(prefix, None)
        for r in list(self.writer.keys()):
            if r == prefix or r.startswith(prefix + "/"):
                del self.writer[r]
                self.readers.pop(r, None)

    def fence(self, region, marks):
        self.pfence[region] = list(marks)


class Mem:
    def __init__(self, nc, trk):
        self.nc = nc
        self.trk = trk
        self.stack = {"sl": [], "sr": [], "p": []}
        self.serial = 0

    def push(self, which, name, shape, dtype, fence_marks=None):
        self.serial += 1
        tname = f"{name}__{self.serial}"
        if which == "p":
            g = self.nc.psum_tensor(tname, shape, dtype)
        else:
            g = self.nc.sbuf_tensor(tname, shape, dtype,
                                    side=("left" if which == "sl" else "right"))
        h = g.__enter__()
        self.stack[which].append((name, g))
        if fence_marks:
            self.trk.fence(name, fence_marks)
        return h

    def pop(self, which, name):
        top, g = self.stack[which].pop()
        assert top == name, f"LIFO violation: pop {name} but top is {top}"
        marks = self.trk.marks_for(name)
        self.trk.clear_prefix(name)
        g.__exit__(None, None, None)
        return marks


def _hadamard128():
    i = np.arange(128)
    pc = np.bitwise_count(i[:, None] & i[None, :]) if hasattr(np, "bitwise_count") else \
        np.array([bin(a & b).count("1") for a in i for b in i]).reshape(128, 128)
    return np.where(pc % 2 == 0, np.float32(1.0), np.float32(-1.0)).astype(np.float32)


def _pswap():
    p = np.zeros((128, 128), np.float32)
    for m in range(128):
        k = m + 32 if (m % 64) < 32 else m - 32
        p[k, m] = 1.0
    return p


def _trig_tables(qstart):
    import jax
    import jax.numpy as jnp
    cpu = jax.devices("cpu")[0]
    with jax.default_device(cpu):
        half = 32
        inv_freq = 1.0 / 10000.0 ** (jnp.arange(half, dtype=jnp.float32) / half)

        def tables(p0, n, scale):
            pos = jnp.arange(p0, p0 + n, dtype=jnp.float32)
            ang = pos[None, :] * inv_freq[:, None]
            c = np.asarray(jnp.cos(ang), np.float32)
            s = np.asarray(jnp.sin(ang), np.float32)
            C = np.tile(c, (4, 1)) * scale
            S = np.tile(s, (4, 1)) * scale
            sign = np.where((np.arange(128) % 64) < 32, np.float32(-1.0), np.float32(1.0))
            return np.ascontiguousarray(C), np.ascontiguousarray(S * sign[:, None])

        ck, sk = tables(qstart - 64, TKV, np.float32(1.0))
        cq, sq = tables(qstart, TQ, np.float32(0.125))
    return cq, sq, ck, sk


def _masks(qstart):
    out = []
    for g in range(2):
        m = np.zeros((128, 4, 272), np.float32)
        for i in range(4):
            qb = g * 4 + i
            q = np.arange(128)[:, None]
            jw = np.arange(256)[None, :]
            valid = (jw >= q) & (jw <= q + 128)
            kpos = qstart - 64 + qb * 128 + jw
            valid = valid & (kpos >= 0) & (kpos < L)
            m[:, i, 16:] = np.where(valid, np.float32(0.0), np.float32(NEG))
        out.append(np.ascontiguousarray(m.reshape(128, 4 * 272)).astype(ml_dtypes.bfloat16))
    return out


def _strips(w, n_oc, n_kc):
    return np.ascontiguousarray(
        w.reshape(n_oc, 128, n_kc, 128).transpose(0, 3, 2, 1).reshape(n_oc, 128, n_kc * 128))


class _Stop(Exception):
    pass


def build_program(taps=(), stop_after=None):
    nc = bass.Bass("TRN2", target_bir_lowering=False)
    trk = Trk()
    mem = Mem(nc, trk)
    E = trk.emit
    Q = "Q"

    dI = lambda n, s, dt=F32: nc.dram_tensor(n, s, dt, kind="ExternalInput").ap()
    dO = lambda n, s, dt=F32: nc.dram_tensor(n, s, dt, kind="ExternalOutput").ap()

    u_T = dI("u_T", [D, TKV])
    wqkv = dI("wqkv", [QKV_OC, 128, NCH * 128])
    bqkv = dI("bqkv_t", [128, QKV_OC])
    wo = dI("wo", [NCH, 128, NCH * 128])
    bo_in = dI("bo_t", [128, NCH])
    wup = dI("wup", [UPCH, 128, NCH * 128])
    wdn = dI("wdn", [NCH, 128, UPCH * 128])
    sinkT_in = dI("sinkT", [128, NCH, NS])
    sinkV_in = dI("sinkV", [NS, NH * DH])
    cq_in = dI("cq", [128, TQ])
    sq_in = dI("sq", [128, TQ])
    ck_in = dI("ck", [128, TKV])
    sk_in = dI("sk", [128, TKV])
    mask0_in = dI("mask0", [128, 4 * 272], BF16)
    mask1_in = dI("mask1", [128, 4 * 272], BF16)
    h128_in = dI("h128bf", [128, 128], BF16)
    h128f_in = dI("h128f", [128, 128])
    iden_in = dI("idenbf", [128, 128], BF16)
    idenf_in = dI("idenf", [128, 128])
    pswap_in = dI("pswap", [128, 128], BF16)
    consts_in = dI("consts", [128, 8])
    onesf_in = dI("onesf", [1, 128])
    onesc_in = dI("onesc", [128, 1])

    out_T = dO("out_T", [D, TQ])

    wup_cache = nc.dram_tensor("wup_cache", [UPCH, 128, NCH * 128], BF16, kind="Internal").ap()
    wdn_cache = nc.dram_tensor("wdn_cache", [NCH, 128, UPCH * 128], BF16, kind="Internal").ap()

    tapset = set(taps)
    tap_out = {}
    for nm, shape, dt in [("xn", [128, NCH, TKV], BF16), ("q", [128, NCH, TQ], BF16),
                          ("k", [128, NCH, TKV], BF16), ("v", [128, 9, NH * DH], BF16),
                          ("y", [128, NCH, TQ], BF16), ("accwo", [128, NCH, TQ], F32),
                          ("xq", [128, NCH, 512], BF16), ("h", [128, UPCH, 512], BF16),
                          ("hq", [128, UPCH, 512], BF16), ("sw", [128, 2], F32),
                          ("pe", [128, 4, 272], F32), ("pe2", [128, 4, 272], F32),
                          ("pb", [128, 4, 272], BF16), ("pt", [128, 4, 384], BF16),
                          ("ysu", [64, 512], BF16),
                          ("sr0", [128, 16], F32), ("mj", [128, 8], F32),
                          ("rows", [1, TKV * 2], F32)]:
        if nm in tapset:
            tap_out[nm] = dO("tap_" + nm, shape, dt)

    def ckpt(ph):
        if stop_after == ph:
            raise _Stop()

    P = {}
    for nm, shape, dt in [
        ("consts", [128, 8], F32), ("onesc", [128, 1], F32),
        ("onesbf", [128, 1], BF16), ("idenbf", [128, 128], BF16), ("idenf", [128, 128], F32),
        ("pswap", [128, 128], BF16), ("h128bf", [128, 128], BF16), ("h128f", [128, 128], F32),
        ("h128nbf", [128, 128], BF16),
        ("bq", [128, QKV_OC], F32), ("bo", [128, NCH], F32),
        ("rsc2048", [128, 1], F32),
        ("swup", [128, 1], F32), ("thrup", [128, 1], F32), ("nthrup", [128, 1], F32),
        ("swdn", [128, 1], F32), ("thrdn", [128, 1], F32), ("nthrdn", [128, 1], F32),
        ("wacc", [128, 2], F32), ("wsums", [128, 64], F32),
        ("onesf", [1, 128], F32), ("mrow", [1, TKV], F32), ("rrow", [1, TKV], F32),
        ("sc2048", [1, 1], F32), ("ln2048", [1, 1], F32),
    ]:
        P[nm] = mem.push("sl", "sb_" + nm, shape, dt)

    for val, col in [(0.0, 0), (1.0, 1), (1e-5, 2), (-0.5, 3), (2048.0, 4), (0.015625, 5), (0.5, 6)]:
        nc.const_aps.register(F32, val, P["consts"][:, col:col + 1])

    try:
        for nm, src in [("consts", consts_in), ("onesf", onesf_in), ("onesc", onesc_in),
                        ("idenbf", iden_in), ("idenf", idenf_in), ("pswap", pswap_in),
                        ("h128bf", h128_in), ("h128f", h128f_in), ("bq", bqkv), ("bo", bo_in)]:
            E(Q, lambda g, d=P[nm], s=src: g.dma_start(out=d[:], in_=s[:, :]), writes=(nm,), qsem="qx")
        trk.settle("qx", ("consts", "onesf", "onesc", "idenbf", "idenf", "pswap",
                          "h128bf", "h128f", "bq", "bo"))
        E("V", lambda g: g.tensor_copy(out=P["onesbf"][:], in_=P["onesc"][:]),
          reads=("onesc",), writes=("onesbf",))
        E("V", lambda g: g.tensor_scalar(out=P["h128nbf"][:], in0=P["h128bf"][:],
                                         scalar1=-1.0, scalar2=None, op0=ALU.mult),
          reads=("h128bf",), writes=("h128nbf",))
        E("S", lambda g: g.activation(out=P["ln2048"][:], in_=P["consts"][0:1, 4:5], func=AF.Ln,
                                      bias=0.0, scale=1.0), reads=("consts",), writes=("ln2048",))
        E("S", lambda g: g.activation(out=P["sc2048"][:], in_=P["ln2048"][:], func=AF.Exp,
                                      bias=0.0, scale=0.5), reads=("ln2048",), writes=("sc2048",))

        xn = mem.push("sl", "xn", [128, NCH, TKV], BF16)

        ubuf = mem.push("sr", "ubuf", [128, 2, TKV], F32)
        sqb = mem.push("sr", "sqb", [128, 2, TKV], BF16)
        ubfb = mem.push("sr", "ubfb", [128, 2, TKV], BF16)
        arow = mem.push("sr", "arow", [1, 4 * TKV], F32)
        RB = mem.push("sr", "RB", [128, 2, TKV], F32)

        ps_st1 = mem.push("p", "ps_st1", [1, TKV], F32)
        ps_st2 = mem.push("p", "ps_st2", [1, TKV], F32)

        for c in range(NCH):
            bi = c % 2
            E(Q, lambda g, c=c, bi=bi: g.dma_start(out=ubuf[:, bi, :], in_=u_T[c * 128:(c + 1) * 128, :]),
              writes=(f"ubuf/{bi}",), qsem=f"qu{bi}")
            E("V", lambda g, bi=bi: g.tensor_copy(out=ubfb[:, bi, :], in_=ubuf[:, bi, :]),
              reads=(f"ubuf/{bi}",), writes=(f"ubfb/{bi}",))
            E("V", lambda g, bi=bi: g.tensor_mul(out=sqb[:, bi, :], in0=ubfb[:, bi, :], in1=ubfb[:, bi, :]),
              reads=(f"ubfb/{bi}",), writes=(f"sqb/{bi}",))
            for (o, n) in NT_KV:
                E("T", lambda g, c=c, bi=bi, o=o, n=n: g.matmul(
                    ps_st1[:, o:o + n], P["onesbf"][:], ubfb[:, bi, o:o + n],
                    start=(c == 0), stop=(c == NCH - 1)),
                  reads=(f"ubfb/{bi}", "onesbf"), writes=(f"ps_st1/{o}",))
                E("T", lambda g, c=c, bi=bi, o=o, n=n: g.matmul(
                    ps_st2[:, o:o + n], P["onesbf"][:], sqb[:, bi, o:o + n],
                    start=(c == 0), stop=(c == NCH - 1)),
                  reads=(f"sqb/{bi}", "onesbf"), writes=(f"ps_st2/{o}",))

        mr, rr = P["mrow"], P["rrow"]
        e2 = arow[0:1, 0:TKV]
        var = arow[0:1, TKV:2 * TKV]
        lnv = arow[0:1, 2 * TKV:3 * TKV]
        brow = arow[0:1, 3 * TKV:4 * TKV]
        st1keys = tuple(f"ps_st1/{o}" for (o, n) in NT_KV)
        st2keys = tuple(f"ps_st2/{o}" for (o, n) in NT_KV)
        E("V", lambda g: g.tensor_scalar(out=mr[:], in0=ps_st1[:, :], scalar1=1.0 / 2048.0,
                                         scalar2=None, op0=ALU.mult), reads=st1keys, writes=("mrow",))
        E("V", lambda g: g.tensor_scalar(out=e2, in0=ps_st2[:, :], scalar1=1.0 / 2048.0,
                                         scalar2=None, op0=ALU.mult), reads=st2keys, writes=("arow/e2",))
        E("V", lambda g: g.tensor_mul(out=var, in0=mr[:], in1=mr[:]), reads=("mrow",), writes=("arow/var",))
        E("V", lambda g: g.tensor_sub(out=var, in0=e2, in1=var),
          reads=("arow/e2", "arow/var"), writes=("arow/var",))
        E("S", lambda g: g.activation(out=lnv, in_=var, func=AF.Ln, bias=1e-5, scale=1.0),
          reads=("arow/var",), writes=("arow/lnv",))
        E("S", lambda g: g.activation(out=rr[:], in_=lnv, func=AF.Exp, bias=0.0, scale=-0.5),
          reads=("arow/lnv",), writes=("rrow",))
        E("V", lambda g: g.tensor_mul(out=brow, in0=mr[:], in1=rr[:]),
          reads=("mrow", "rrow"), writes=("arow/brow",))
        E("V", lambda g: g.tensor_scalar(out=brow, in0=brow, scalar1=-1.0, scalar2=None, op0=ALU.mult),
          reads=("arow/brow",), writes=("arow/brow",))
        if "rows" in tap_out:
            E(Q, lambda g: g.dma_start(out=tap_out["rows"][0:1, 0:TKV], in_=mr[:]), reads=("mrow",), qsem="qout")
            E(Q, lambda g: g.dma_start(out=tap_out["rows"][0:1, TKV:2 * TKV], in_=rr[:]), reads=("rrow",), qsem="qout")
        E("S", lambda g: g.activation(out=arow[0:1, 0:1], in_=P["ln2048"][:], func=AF.Exp,
                                      bias=0.0, scale=-0.5),
          reads=("ln2048",), writes=("arow/rsc",))

        f_st2 = mem.pop("p", "ps_st2")
        f_st1 = mem.pop("p", "ps_st1")
        ps_rb = mem.push("p", "ps_rb", [128, 2, 1536], F32, fence_marks=f_st1 + f_st2)
        for (o, n) in NT_KV:
            E("T", lambda g, o=o, n=n: g.matmul(ps_rb[:, 0, o:o + n], P["onesf"][:], rr[0:1, o:o + n],
                                                start=True, stop=True),
              reads=("rrow", "onesf"), writes=(f"ps_rb/0/{o}",))
            E("T", lambda g, o=o, n=n: g.matmul(ps_rb[:, 1, o:o + n], P["onesf"][:],
                                                arow[0:1, 3 * TKV + o:3 * TKV + o + n],
                                                start=True, stop=True),
              reads=("arow/brow", "onesf"), writes=(f"ps_rb/1/{o}",))
        E("T", lambda g: g.matmul(ps_rb[:, 0, TKV:TKV + 1], P["onesf"][:], arow[0:1, 0:1],
                                  start=True, stop=True),
          reads=("arow/rsc", "onesf"), writes=("ps_rb/rsc",))
        E("V", lambda g: g.tensor_copy(out=P["rsc2048"][:], in_=ps_rb[:, 0, TKV:TKV + 1]),
          reads=("ps_rb/rsc",), writes=("rsc2048",))
        rbk0 = tuple(f"ps_rb/0/{o}" for (o, n) in NT_KV)
        rbk1 = tuple(f"ps_rb/1/{o}" for (o, n) in NT_KV)
        E("V", lambda g: g.tensor_copy(out=RB[:, 0, :], in_=ps_rb[:, 0, 0:TKV]), reads=rbk0, writes=("RB/0",))
        E("V", lambda g: g.tensor_copy(out=RB[:, 1, :], in_=ps_rb[:, 1, 0:TKV]), reads=rbk1, writes=("RB/1",))

        for c in range(NCH):
            bi = c % 2
            E(Q, lambda g, c=c, bi=bi: g.dma_start(out=ubuf[:, bi, :], in_=u_T[c * 128:(c + 1) * 128, :]),
              writes=(f"ubuf/{bi}",), qsem=f"qu{bi}")
            eng = "V"
            E(eng, lambda g, bi=bi: g.tensor_mul(out=ubuf[:, bi, :], in0=ubuf[:, bi, :], in1=RB[:, 0, :]),
              reads=(f"ubuf/{bi}", "RB/0"), writes=(f"ubuf/{bi}",))
            E(eng, lambda g, c=c, bi=bi: g.tensor_add(out=xn[:, c, :], in0=ubuf[:, bi, :], in1=RB[:, 1, :]),
              reads=(f"ubuf/{bi}", "RB/1"), writes=(f"xn/{c}",))

        xnkeys = tuple(f"xn/{c}" for c in range(NCH))
        if "xn" in tap_out:
            E(Q, lambda g: g.dma_start(out=tap_out["xn"][:, :, :], in_=xn[:, :, :]), reads=xnkeys, qsem="qout")
        ckpt("A")

        f_rb = mem.pop("p", "ps_rb")
        fA = []
        for nm in ("RB", "arow", "ubfb", "sqb", "ubuf"):
            fA += mem.pop("sr", nm)

        q_sb = mem.push("sr", "q", [128, NCH, TQ], BF16, fence_marks=fA)
        k_sb = mem.push("sr", "k", [128, NCH, TKV], BF16, fence_marks=fA)
        v_tm = mem.push("sr", "V", [128, 9, NH * DH], BF16, fence_marks=fA)
        wst = mem.push("sr", "wst", [128, NCH * 128], F32, fence_marks=fA)
        wbf = mem.push("sr", "wbf", [128, 2, NCH * 128], BF16, fence_marks=fA)
        vbf = mem.push("sr", "vbf", [128, 2, 512], BF16, fence_marks=fA)

        trig = mem.push("sl", "trig", [128, 2, TKV], F32)
        rsc = mem.push("sl", "rsc", [128, 2, 512], F32)

        ps_qkv = mem.push("p", "ps_qkv", [128, 2, 3, 512], F32, fence_marks=f_rb)
        ps_sw = mem.push("p", "ps_sw", [128, 512], F32, fence_marks=f_rb)
        ps_vt = mem.push("p", "ps_vt", [128, 1024], BF16, fence_marks=f_rb)

        def cvt(e, dst, src, rkey, wkey):
            if e == "S":
                E(e, lambda g: g.activation(out=dst, in_=src, func=AF.Copy, bias=0.0, scale=1.0),
                  reads=(rkey,) if isinstance(rkey, str) else rkey,
                  writes=(wkey,) if isinstance(wkey, str) else wkey)
            else:
                E(e, lambda g: g.tensor_copy(out=dst, in_=src),
                  reads=(rkey,) if isinstance(rkey, str) else rkey,
                  writes=(wkey,) if isinstance(wkey, str) else wkey)

        E(Q, lambda g: g.dma_start(out=trig[:, 0, 0:TQ], in_=cq_in[:, :]), writes=("trig/c",), qsem="qtr")
        E(Q, lambda g: g.dma_start(out=trig[:, 1, 0:TQ], in_=sq_in[:, :]), writes=("trig/s",), qsem="qtr")
        trk.settle("qtr", ("trig/c", "trig/s"))

        def rope_chunk(c, kind):
            tens = q_sb if kind == "q" else k_sb
            tiles = NT_Q if kind == "q" else NT_KV
            toff = QOFF if kind == "q" else 0
            for (o, n) in tiles:
                co = o - toff
                E("T", lambda g, c=c, o=o, n=n, co=co, tens=tens: g.matmul(
                    ps_sw[:, 0:n], P["pswap"][:], tens[:, c, co:co + n], start=True, stop=True),
                  reads=(f"{kind}/n/{c}", "pswap"), writes=("ps_sw",))
                E("V", lambda g, n=n, co=co: g.tensor_tensor(
                    out=rsc[:, 0, 0:n], in0=ps_sw[:, 0:n], in1=trig[:, 1, co:co + n], op=ALU.mult),
                  reads=("ps_sw", "trig/s"), writes=("rsc/m",))
                E("V", lambda g, c=c, n=n, co=co, tens=tens: g.tensor_tensor(
                    out=rsc[:, 1, 0:n], in0=tens[:, c, co:co + n], in1=trig[:, 0, co:co + n], op=ALU.mult),
                  reads=(f"{kind}/n/{c}", "trig/c"), writes=("rsc/t1",))
                E("V", lambda g, c=c, n=n, co=co, tens=tens: g.tensor_add(
                    out=tens[:, c, co:co + n], in0=rsc[:, 1, 0:n], in1=rsc[:, 0, 0:n]),
                  reads=("rsc/t1", "rsc/m"), writes=(f"{kind}/{c}",))

        conv_eng = ("S", "V", "S")
        for oc in range(QKV_OC):
            wb = oc % 2
            E(Q, lambda g, oc=oc: g.dma_start(out=wst[:, :], in_=wqkv[oc, :, :]), writes=("wst",), qsem="qw")
            cvt(conv_eng[oc % 3], wbf[:, wb, :], wst[:, :], "wst", f"wbf/{wb}")
            kind = "q" if oc < NCH else ("k" if oc < 2 * NCH else "v")
            c = oc % NCH
            tiles = NT_Q if kind == "q" else NT_KV
            pp = oc % 2
            for ni, (o, n) in enumerate(tiles):
                for kc in range(NCH):
                    E("T", lambda g, wb=wb, kc=kc, pp=pp, ni=ni, o=o, n=n: g.matmul(
                        ps_qkv[:, pp, ni, 0:n], wbf[:, wb, kc * 128:(kc + 1) * 128],
                        xn[:, kc, o:o + n], start=(kc == 0), stop=(kc == NCH - 1)),
                      reads=(f"wbf/{wb}", f"xn/{kc}"), writes=(f"ps_qkv/{pp}/{ni}",))
            if kind in ("q", "k"):
                tens = q_sb if kind == "q" else k_sb
                toff = QOFF if kind == "q" else 0
                for ni, (o, n) in enumerate(tiles):
                    E("S", lambda g, oc=oc, c=c, pp=pp, ni=ni, o=o, n=n, toff=toff, tens=tens: g.activation(
                        out=tens[:, c, o - toff:o - toff + n], in_=ps_qkv[:, pp, ni, 0:n],
                        func=AF.Identity, bias=P["bq"][:, oc:oc + 1], scale=1.0),
                      reads=(f"ps_qkv/{pp}/{ni}", "bq"), writes=(f"{kind}/n/{c}",))
                rope_chunk(c, kind)
                if kind == "q" and c == NCH - 1:
                    E(Q, lambda g: g.dma_start(out=trig[:, 0, :], in_=ck_in[:, :]), writes=("trig/c",), qsem="qtr")
                    E(Q, lambda g: g.dma_start(out=trig[:, 1, :], in_=sk_in[:, :]), writes=("trig/s",), qsem="qtr")
                    trk.settle("qtr", ("trig/c", "trig/s"))
            else:
                vc = oc - 2 * NCH
                for ni, (o, n) in enumerate(tiles):
                    vb = ni % 2
                    E("S", lambda g, oc=oc, pp=pp, ni=ni, n=n, vb=vb: g.activation(
                        out=vbf[:, vb, 0:n], in_=ps_qkv[:, pp, ni, 0:n],
                        func=AF.Identity, bias=P["bq"][:, oc:oc + 1], scale=1.0),
                      reads=(f"ps_qkv/{pp}/{ni}", "bq"), writes=(f"vbf/{vb}",))
                    ntp = n // 128
                    for j in range(ntp):
                        E("T", lambda g, vb=vb, j=j: g.transpose(
                            ps_vt[:, j * 128:(j + 1) * 128], vbf[:, vb, j * 128:(j + 1) * 128],
                            P["idenbf"][:]),
                          reads=(f"vbf/{vb}", "idenbf"), writes=(f"ps_vt/{j}",))
                    jt0 = o // 128
                    E("V", lambda g, vc=vc, jt0=jt0, n=n, ntp=ntp: g.tensor_copy(
                        out=v_tm[:, jt0:jt0 + ntp, vc * 128:(vc + 1) * 128],
                        in_=ps_vt[:, 0:n].rearrange("p (j f) -> p j f", f=128)),
                      reads=tuple(f"ps_vt/{j}" for j in range(ntp)), writes=(f"V/{vc}",))

        qkeys = tuple(f"q/{c}" for c in range(NCH))
        kkeys = tuple(f"k/{c}" for c in range(NCH))
        vkeys = tuple(f"V/{vc}" for vc in range(NCH))
        if "q" in tap_out:
            E(Q, lambda g: g.dma_start(out=tap_out["q"][:, :, :], in_=q_sb[:, :, :]), reads=qkeys, qsem="qout")
        if "k" in tap_out:
            E(Q, lambda g: g.dma_start(out=tap_out["k"][:, :, :], in_=k_sb[:, :, :]), reads=kkeys, qsem="qout")
        if "v" in tap_out:
            E(Q, lambda g: g.dma_start(out=tap_out["v"][:, :, :], in_=v_tm[:, :, :]), reads=vkeys, qsem="qout")
        ckpt("B")

        fB_ps = []
        for nm in ("ps_vt", "ps_sw", "ps_qkv"):
            fB_ps += mem.pop("p", nm)
        fB_l = mem.pop("sl", "rsc") + mem.pop("sl", "trig")
        fB_l += mem.pop("sl", "xn")
        y_sb = mem.push("sl", "y", [128, NCH, TQ], BF16, fence_marks=fB_l)
        fB_r = []
        for nm in ("vbf", "wbf", "wst"):
            fB_r += mem.pop("sr", nm)

        masks = mem.push("sr", "masks", [128, 2, 4 * 272], BF16, fence_marks=fB_r)
        sbf = mem.push("sr", "sbf", [128, NCH, NS], BF16, fence_marks=fB_r)
        svbf = mem.push("sr", "svbf", [NS, NH * DH], BF16, fence_marks=fB_r)
        stf = mem.push("sr", "stf", [128, NCH, NS], F32, fence_marks=fB_r)
        pe_sc = mem.push("sr", "pe", [128, 2, 4, 272], F32, fence_marks=fB_r)
        pe2_sc = mem.push("sr", "pe2", [128, 2, 4, 272], F32, fence_marks=fB_r)
        pb_sc = mem.push("sr", "pb", [128, 2, 4, 272], BF16, fence_marks=fB_r)
        pt_sb = mem.push("sr", "pt", [128, 2, 4, 384], BF16, fence_marks=fB_r)
        srow = mem.push("sr", "srow", [128, 2, 16], F32, fence_marks=fB_r)
        mst = mem.push("sr", "mst", [128, 2, 2048], F32, fence_marks=fB_r)

        E(Q, lambda g: g.dma_start(out=masks[:, 0, :], in_=mask0_in[:, :]), writes=("masks/0",), qsem="qx2")
        E(Q, lambda g: g.dma_start(out=masks[:, 1, :], in_=mask1_in[:, :]), writes=("masks/1",), qsem="qx2")
        E(Q, lambda g: g.dma_start(out=stf[:, :, :], in_=sinkT_in[:, :, :]), writes=("stf",), qsem="qx2")
        E(Q, lambda g: g.dma_start(out=mst[0:NS, 0, :], in_=sinkV_in[:, 0:2048]), writes=("mst/0",), qsem="qx2")
        trk.settle("qx2", ("masks/0", "masks/1", "stf", "mst/0"))
        E("V", lambda g: g.tensor_copy(out=sbf[:, :, :], in_=stf[:, :, :]), reads=("stf",), writes=("sbf",))
        E("V", lambda g: g.tensor_copy(out=svbf[:, :], in_=mst[0:NS, 0, :]), reads=("mst/0",), writes=("svbf",))

        ps_S = mem.push("p", "ps_S", [128, 4, 512], F32, fence_marks=fB_ps)
        ps_PT = mem.push("p", "ps_PT", [128, 4, 512], BF16, fence_marks=fB_ps)
        E("S", lambda g: g.memzero(ps_PT[:, :, :]),
          writes=tuple(f"ps_PT/{i}" for i in range(4)))
        ps_Y = mem.push("p", "ps_Y", [64, 2, 512], F32, fence_marks=fB_ps)

        mean_jobs = [("up", oc, 0) for oc in range(UPCH)] + \
                    [("dn", oc, off) for oc in range(NCH) for off in (0, 2048)]

        def emit_mean(idx):
            which, oc, off = mean_jobs[idx]
            bi = idx % 2
            src = wup[oc, :, :] if which == "up" else wdn[oc, :, off:off + 2048]
            wcol = 0 if which == "up" else 1
            E(Q, lambda g, src=src, bi=bi: g.dma_start(out=mst[:, bi, :], in_=src), writes=(f"mst/{bi}",), qsem=f"qm{bi}")
            E("V", lambda g, bi=bi, idx=idx: g.reduce_sum(out=P["wsums"][:, idx:idx + 1],
                                                          in_=mst[:, bi, :], axis=AX.X,
                                                          apply_absolute_value=True),
              reads=(f"mst/{bi}",), writes=(f"wsums/{idx}",))

        def su(h, g2):
            hp = (h % 2) * 64
            ch = h // 2
            vch = (h * DH) // 128
            pp = (h * 2 + g2) % 2
            for i in range(4):
                qb = g2 * 4 + i
                E("T", lambda g, hp=hp, ch=ch, qb=qb, i=i: g.matmul(
                    ps_S[:, i, 0:16], q_sb[hp:hp + 64, ch, qb * 128:(qb + 1) * 128],
                    sbf[hp:hp + 64, ch, :], start=True, stop=True),
                  reads=(f"q/{ch}", "sbf"), writes=(f"ps_S/{i}",))
                E("T", lambda g, hp=hp, ch=ch, qb=qb, i=i: g.matmul(
                    ps_S[:, i, 16:272], q_sb[hp:hp + 64, ch, qb * 128:(qb + 1) * 128],
                    k_sb[hp:hp + 64, ch, qb * 128:qb * 128 + 256], start=True, stop=True),
                  reads=(f"q/{ch}", f"k/{ch}"), writes=(f"ps_S/{i}",))
                E("V", lambda g, i=i, g2=g2, pp=pp: g.scalar_tensor_tensor(
                    out=pe_sc[:, pp, i, :], in0=ps_S[:, i, 0:272], scalar=1.0,
                    in1=masks[:, g2, i * 272:(i + 1) * 272], op0=ALU.mult, op1=ALU.add),
                  reads=(f"ps_S/{i}", f"masks/{g2}"), writes=(f"pe/{pp}/{i}",))
            E("V", lambda g, pp=pp: g.reduce_max(out=srow[:, pp, 0:4], in_=pe_sc[:, pp, :, :],
                                                 axis=AX.X, negate=True),
              reads=tuple(f"pe/{pp}/{i}" for i in range(4)), writes=(f"srow/{pp}/negmx",))
            for i in range(4):
                E("S", lambda g, pp=pp, i=i: g.activation(
                    out=pe_sc[:, pp, i, :], in_=pe_sc[:, pp, i, :], func=AF.Exp,
                    bias=srow[:, pp, i:i + 1], scale=1.0, accum_out=srow[:, pp, 4 + i:5 + i]),
                  reads=(f"pe/{pp}/{i}", f"srow/{pp}/negmx"),
                  writes=(f"pe/{pp}/{i}", f"srow/{pp}/sum",))
            E("V", lambda g, pp=pp: g.reciprocal(out=srow[:, pp, 8:12], in_=srow[:, pp, 4:8]),
              reads=(f"srow/{pp}/sum",), writes=(f"srow/{pp}/rec",))
            for i in range(4):
                eng = "S"
                if eng == "S":
                    E(eng, lambda g, pp=pp, i=i: g.activation(
                        out=pb_sc[:, pp, i, :], in_=pe_sc[:, pp, i, :], func=AF.Identity,
                        bias=0.0, scale=srow[:, pp, 8 + i:9 + i]),
                      reads=(f"pe/{pp}/{i}", f"srow/{pp}/rec"), writes=(f"pb/{pp}/{i}",))
                else:
                    E(eng, lambda g, pp=pp, i=i: g.tensor_scalar(
                        out=pb_sc[:, pp, i, :], in0=pe_sc[:, pp, i, :],
                        scalar1=srow[:, pp, 8 + i:9 + i], scalar2=None, op0=ALU.mult),
                      reads=(f"pe/{pp}/{i}", f"srow/{pp}/rec"), writes=(f"pb/{pp}/{i}",))
                E("T", lambda g, pp=pp, i=i: g.transpose(
                    ps_PT[:, i, 0:128], pb_sc[:, pp, i, 16:144], P["idenbf"][:]),
                  reads=(f"pb/{pp}/{i}", "idenbf"), writes=(f"ps_PT/{i}",))
                E("T", lambda g, pp=pp, i=i: g.transpose(
                    ps_PT[:, i, 128:256], pb_sc[:, pp, i, 144:272], P["idenbf"][:]),
                  reads=(f"pb/{pp}/{i}",), writes=(f"ps_PT/{i}",))
                E("T", lambda g, pp=pp, i=i: g.transpose(
                    ps_PT[0:16, i, 256:384], pb_sc[:, pp, i, 0:16], P["idenbf"][:]),
                  reads=(f"pb/{pp}/{i}",), writes=(f"ps_PT/{i}",))
            for half in range(2):
                E("V", lambda g, pp=pp, half=half: g.tensor_copy(
                    out=pt_sb[:, pp, 2 * half:2 * half + 2, :],
                    in_=ps_PT[:, 2 * half:2 * half + 2, 0:384]),
                  reads=(f"ps_PT/{2 * half}", f"ps_PT/{2 * half + 1}"),
                  writes=(f"pt/{pp}/{2 * half}", f"pt/{pp}/{2 * half + 1}"))
            yp = (h * 2 + g2) % 2
            for i in range(4):
                qb = g2 * 4 + i
                E("T", lambda g, pp=pp, i=i, h=h: g.matmul(
                    ps_Y[:, yp, i * 128:(i + 1) * 128], svbf[:, h * 64:(h + 1) * 64],
                    pt_sb[0:16, pp, i, 256:384], start=True, stop=False),
                  reads=(f"pt/{pp}/{i}", "svbf"), writes=(f"ps_Y/{yp}",))
                E("T", lambda g, pp=pp, i=i, h=h, qb=qb: g.matmul(
                    ps_Y[:, yp, i * 128:(i + 1) * 128], v_tm[:, qb, h * 64:(h + 1) * 64],
                    pt_sb[:, pp, i, 0:128], start=False, stop=False),
                  reads=(f"pt/{pp}/{i}", f"V/{vch}"), writes=(f"ps_Y/{yp}",))
                E("T", lambda g, pp=pp, i=i, h=h, qb=qb: g.matmul(
                    ps_Y[:, yp, i * 128:(i + 1) * 128], v_tm[:, qb + 1, h * 64:(h + 1) * 64],
                    pt_sb[:, pp, i, 128:256], start=False, stop=True),
                  reads=(f"pt/{pp}/{i}", f"V/{vch}"), writes=(f"ps_Y/{yp}",))
            E("S", lambda g, h=h, g2=g2, yp=yp, hp=hp, ch=ch: g.activation(
                out=y_sb[hp:hp + 64, ch, g2 * 512:(g2 + 1) * 512], in_=ps_Y[:, yp, :],
                func=AF.Copy, bias=0.0, scale=1.0),
              reads=(f"ps_Y/{yp}",), writes=(f"y/{h}/{g2}",))

        import os as _os
        _dsu = int(_os.environ.get("DSU", "0"))
        tap_wacc_final = None
        if "pe" in tap_out:
            su(_dsu, 0)
            E(Q, lambda g: g.dma_start(out=tap_out["pe"][:, :, :], in_=pe_sc[:, 0, :, :]),
              reads=tuple(f"pe/0/{i}" for i in range(4)), qsem="qout")
            E(Q, lambda g: g.dma_start(out=tap_out["pe2"][:, :, :], in_=pe2_sc[:, 0, :, :]),
              reads=tuple(f"pe2/0/{i}" for i in range(4)), qsem="qout")
            E(Q, lambda g: g.dma_start(out=tap_out["sr0"][:, :], in_=srow[:, 0, :]),
              reads=("srow/0/negmx", "srow/0/sum", "srow/0/rec"), qsem="qout")
            if "pb" in tap_out:
                E(Q, lambda g: g.dma_start(out=tap_out["pb"][:, :, :], in_=pb_sc[:, 0, :, :]),
                  reads=tuple(f"pb/0/{i}" for i in range(4)), qsem="qout")
                E(Q, lambda g: g.dma_start(out=tap_out["pt"][:, :, :], in_=pt_sb[:, 0, :, :]),
                  reads=tuple(f"pt/0/{i}" for i in range(4)), qsem="qout")
                E(Q, lambda g, hp=(_dsu % 2) * 64, ch=_dsu // 2: g.dma_start(
                    out=tap_out["ysu"][:, :], in_=y_sb[hp:hp + 64, ch, 0:512]),
                  reads=(f"y/{_dsu}/0",), qsem="qout")
        if "mj" in tap_out:
            _mjdone = [False]
            def _tap_wacc_final():
                E(Q, lambda g: g.dma_start(out=tap_out["mj"][:, 2:4], in_=P["wacc"][:, 0:2]),
                  reads=("wacc",), qsem="qout")
            tap_wacc_final = _tap_wacc_final
        else:
            tap_wacc_final = None
        _nosu = bool(_os.environ.get("DBG_NOSU"))
        mean_i = 0
        for h in range(NH):
            for g2 in range(2):
                if not _nosu and not ("pe" in tap_out and h == _dsu and g2 == 0):
                    su(h, g2)
                want = ((h * 2 + g2 + 1) * len(mean_jobs)) // 64
                while mean_i < min(want, len(mean_jobs)):
                    emit_mean(mean_i)
                    mean_i += 1
        while mean_i < len(mean_jobs):
            emit_mean(mean_i)
            mean_i += 1

        ykeys = tuple(f"y/{h}/{g2}" for h in range(NH) for g2 in range(2))
        if "y" in tap_out:
            E(Q, lambda g: g.dma_start(out=tap_out["y"][:, :, :], in_=y_sb[:, :, :]), reads=ykeys, qsem="qout")

        fC_ps = []
        for nm in ("ps_Y", "ps_PT", "ps_S"):
            fC_ps += mem.pop("p", nm)
        E("V", lambda g: g.reduce_sum(out=P["wacc"][:, 0:1], in_=P["wsums"][:, 0:32], axis=AX.X),
          reads=tuple(f"wsums/{i}" for i in range(32)), writes=("wacc",))
        E("V", lambda g: g.reduce_sum(out=P["wacc"][:, 1:2], in_=P["wsums"][:, 32:64], axis=AX.X),
          reads=tuple(f"wsums/{i}" for i in range(32, 64)), writes=("wacc",))
        ps_w = mem.push("p", "ps_w", [1, 2], F32, fence_marks=fC_ps)
        E("T", lambda g: g.matmul(ps_w[:, 0:1], P["wacc"][:, 0:1], P["onesc"][:], start=True, stop=True),
          reads=("wacc", "onesc"), writes=("ps_w/0",))
        E("T", lambda g: g.matmul(ps_w[:, 1:2], P["wacc"][:, 1:2], P["onesc"][:], start=True, stop=True),
          reads=("wacc", "onesc"), writes=("ps_w/1",))
        E("V", lambda g: g.tensor_copy(out=srow[0:1, 0, 0:2], in_=ps_w[:, 0:2]),
          reads=("ps_w/0", "ps_w/1"), writes=("srow/swr",))
        ps_wb = mem.push("p", "ps_wb", [128, 4], F32)
        E("T", lambda g: g.matmul(ps_wb[:, 0:2], P["onesf"][:], srow[0:1, 0, 0:2], start=True, stop=True),
          reads=("srow/swr", "onesf"), writes=("ps_wb",))
        for nm, col in (("swup", 0), ("swdn", 1)):
            E("V", lambda g, nm=nm, col=col: g.tensor_scalar(
                out=P[nm][:], in0=ps_wb[:, col:col + 1], scalar1=1.0 / (UP * D), scalar2=1e-5,
                op0=ALU.mult, op1=ALU.max),
              reads=("ps_wb",), writes=(nm,))
        for nm, sw, f in (("thrup", "swup", 0.5), ("thrdn", "swdn", 0.5),
                          ("nthrup", "swup", -0.5), ("nthrdn", "swdn", -0.5)):
            E("V", lambda g, nm=nm, sw=sw, f=f: g.tensor_scalar(out=P[nm][:], in0=P[sw][:], scalar1=f,
                                                                scalar2=None, op0=ALU.mult),
              reads=(sw,), writes=(nm,))
        if tap_wacc_final is not None:
            E(Q, lambda g: g.dma_start(out=tap_out["mj"][:, 0:2], in_=P["wsums"][:, 0:2]),
              reads=("wsums/0", "wsums/1"), qsem="qout")
            E(Q, lambda g: g.dma_start(out=tap_out["mj"][:, 2:4], in_=P["wacc"][:, 0:2]),
              reads=("wacc",), qsem="qout")
            E(Q, lambda g: g.dma_start(out=tap_out["mj"][0:1, 4:6], in_=srow[0:1, 0, 0:2]),
              reads=("srow/swr",), qsem="qout")
        if "sw" in tap_out:
            E("V", lambda g: g.tensor_copy(out=P["wacc"][:, 0:1], in_=P["swup"][:, :]),
              reads=("swup",), writes=("wacc",))
            E("V", lambda g: g.tensor_copy(out=P["wacc"][:, 1:2], in_=P["swdn"][:, :]),
              reads=("swdn",), writes=("wacc",))
            E(Q, lambda g: g.dma_start(out=tap_out["sw"][:, :], in_=P["wacc"][:, :]),
              reads=("wacc",), qsem="qout")
        fC_ps += mem.pop("p", "ps_wb")
        fC_ps += mem.pop("p", "ps_w")
        ckpt("C")

        fC = []
        for nm in ("mst", "srow", "pt", "pb", "pe2", "pe", "stf", "svbf", "sbf", "masks",
                   "V", "k", "q"):
            fC += mem.pop("sr", nm)

        acc = mem.push("sr", "acc", [128, NCH, TQ], F32, fence_marks=fC)
        wst2 = mem.push("sr", "wst2", [128, NCH * 128], F32, fence_marks=fC)
        wbf2 = mem.push("sr", "wbf2", [128, 2, NCH * 128], BF16, fence_marks=fC)
        ps_wo = mem.push("p", "ps_wo", [128, 2, 2, 512], F32, fence_marks=fC_ps)

        for c in range(NCH):
            E(Q, lambda g, c=c: g.dma_start(out=acc[:, c, :], in_=u_T[c * 128:(c + 1) * 128, QOFF:QOFF + TQ]),
              writes=(f"acc/{c}",), qsem="qa")
        trk.settle("qa", tuple(f"acc/{c}" for c in range(NCH)))

        for oc in range(NCH):
            wb = oc % 2
            E(Q, lambda g, oc=oc: g.dma_start(out=wst2[:, :], in_=wo[oc, :, :]), writes=("wst2",), qsem="qw")
            cvt(conv_eng[oc % 3], wbf2[:, wb, :], wst2[:, :], "wst2", f"wbf2/{wb}")
            pp = oc % 2
            for ni in range(2):
                for kc in range(NCH):
                    E("T", lambda g, wb=wb, kc=kc, ni=ni, pp=pp: g.matmul(
                        ps_wo[:, pp, ni, :], wbf2[:, wb, kc * 128:(kc + 1) * 128],
                        y_sb[:, kc, ni * 512:(ni + 1) * 512], start=(kc == 0), stop=(kc == NCH - 1)),
                      reads=(f"wbf2/{wb}",) + ykeys, writes=(f"ps_wo/{pp}/{ni}",))
                E("V", lambda g, oc=oc, ni=ni, pp=pp: g.scalar_tensor_tensor(
                    out=acc[:, oc, ni * 512:(ni + 1) * 512], in0=ps_wo[:, pp, ni, :],
                    scalar=P["bo"][:, oc:oc + 1], in1=acc[:, oc, ni * 512:(ni + 1) * 512],
                    op0=ALU.add, op1=ALU.add),
                  reads=(f"ps_wo/{pp}/{ni}", "bo", f"acc/{oc}"), writes=(f"acc/{oc}",))

        acckeys = tuple(f"acc/{c}" for c in range(NCH))
        if "accwo" in tap_out:
            E(Q, lambda g: g.dma_start(out=tap_out["accwo"][:, :, :], in_=acc[:, :, :]), reads=acckeys, qsem="qout")
        ckpt("D")

        fD_ps = mem.pop("p", "ps_wo")
        fD = mem.pop("sr", "wbf2") + mem.pop("sr", "wst2")
        fy = mem.pop("sl", "y")

        hq = mem.push("sl", "hq", [128, UPCH, 512], BF16, fence_marks=fy)
        hbf = mem.push("sl", "hbf", [128, UPCH, 512], BF16, fence_marks=fy)

        xq = mem.push("sr", "xq", [128, NCH, 512], BF16, fence_marks=fD)
        sxdq = mem.push("sr", "sxdq", [128, 6, 256], F32, fence_marks=fD)
        qrow = mem.push("sr", "qrow", [128, 16], F32, fence_marks=fD)
        qr2 = mem.push("sr", "qr2", [1, 1024], F32, fence_marks=fD)

        def quant_tile(xht, nch, slots, spare, key, width, psq, pk, dq_rstd_q0, out_fn, dbg=False):
            E("V", lambda g, s0=slots[0]: g.tensor_copy(out=sxdq[:, 2, 0:width], in_=xht[:, s0, :]),
              reads=(f"{key}/{slots[0]}",), writes=("sxdq/2",))
            E("V", lambda g, s0=slots[0]: g.tensor_copy(out=sxdq[:, 4, 0:width], in_=xht[:, s0, :]),
              reads=(f"{key}/{slots[0]}",), writes=("sxdq/4",))
            for c in range(1, nch):
                s_ = slots[c]
                E("V", lambda g, s_=s_: g.tensor_tensor(
                    out=sxdq[:, 2, 0:width], in0=sxdq[:, 2, 0:width], in1=xht[:, s_, :], op=ALU.max),
                  reads=(f"{key}/{s_}", "sxdq/2"), writes=("sxdq/2",))
                E("V", lambda g, s_=s_: g.tensor_tensor(
                    out=sxdq[:, 4, 0:width], in0=sxdq[:, 4, 0:width], in1=xht[:, s_, :], op=ALU.min),
                  reads=(f"{key}/{s_}", "sxdq/4"), writes=("sxdq/4",))
            E("V", lambda g: g.tensor_scalar(out=sxdq[:, 4, 0:width], in0=sxdq[:, 4, 0:width],
                                             scalar1=-1.0, scalar2=None, op0=ALU.mult),
              reads=("sxdq/4",), writes=("sxdq/4",))
            E("V", lambda g: g.tensor_tensor(out=sxdq[:, 2, 0:width], in0=sxdq[:, 2, 0:width],
                                             in1=sxdq[:, 4, 0:width], op=ALU.max),
              reads=("sxdq/2", "sxdq/4"), writes=("sxdq/2",))
            ntp = width // 128
            for j in range(ntp):
                E("T", lambda g, j=j: g.transpose(psq[:, 0, j * 128:(j + 1) * 128],
                                                  sxdq[:, 2, j * 128:(j + 1) * 128], P["idenf"][:]),
                  reads=("sxdq/2", "idenf"), writes=(f"{pk}/a{j}",))
            E("V", lambda g, ntp=ntp: g.reduce_max(
                out=qrow[:, 0:ntp], in_=psq[:, 0, 0:ntp * 128].rearrange("p (j t) -> p j t", j=ntp),
                axis=AX.X),
              reads=tuple(f"{pk}/a{j}" for j in range(ntp)), writes=("qrow/mx",))
            if dbg:
                ckpt("E0c")
            if dq_rstd_q0 is not None:
                for j in range(ntp):
                    E("T", lambda g, j=j, q0=dq_rstd_q0: g.transpose(
                        psq[:, 1, j * 128:j * 128 + 1],
                        P["rrow"][0:1, q0 + j * 128:q0 + (j + 1) * 128], P["idenf"][0:1, 0:1]),
                      reads=("rrow", "idenf"), writes=(f"{pk}/b{j}",))
                    E("V", lambda g, j=j: g.tensor_copy(out=qrow[:, 8 + j:9 + j],
                                                        in_=psq[:, 1, j * 128:j * 128 + 1]),
                      reads=(f"{pk}/b{j}",), writes=("qrow/rst",))
                E("V", lambda g, ntp=ntp: g.reciprocal(out=qrow[:, 12:12 + ntp], in_=qrow[:, 8:8 + ntp]),
                  reads=("qrow/rst",), writes=("qrow/irst",))
                E("V", lambda g, ntp=ntp: g.tensor_scalar(out=qrow[:, 12:12 + ntp],
                                                          in0=qrow[:, 12:12 + ntp], scalar1=1e-5,
                                                          scalar2=None, op0=ALU.mult),
                  reads=("qrow/irst",), writes=("qrow/irst",))
                E("V", lambda g, ntp=ntp: g.tensor_tensor(out=qrow[:, 0:ntp], in0=qrow[:, 0:ntp],
                                                          in1=qrow[:, 12:12 + ntp], op=ALU.max),
                  reads=("qrow/mx", "qrow/irst"), writes=("qrow/mx",))
            else:
                E("V", lambda g, ntp=ntp: g.tensor_scalar(out=qrow[:, 0:ntp], in0=qrow[:, 0:ntp],
                                                          scalar1=1e-5, scalar2=None, op0=ALU.max),
                  reads=("qrow/mx",), writes=("qrow/mx",))
            E("V", lambda g, ntp=ntp: g.reciprocal(out=qrow[:, 4:4 + ntp], in_=qrow[:, 0:ntp]),
              reads=("qrow/mx",), writes=("qrow/sx",))
            E("V", lambda g, ntp=ntp: g.tensor_tensor(out=qrow[:, 12:12 + ntp],
                                                      in0=qrow[:, 4:4 + ntp], in1=qrow[:, 0:ntp],
                                                      op=ALU.mult),
              reads=("qrow/sx", "qrow/mx"), writes=("qrow/nt",))
            E("V", lambda g, ntp=ntp: g.tensor_scalar(out=qrow[:, 12:12 + ntp],
                                                      in0=qrow[:, 12:12 + ntp], scalar1=-1.0,
                                                      scalar2=2.0, op0=ALU.mult, op1=ALU.add),
              reads=("qrow/nt",), writes=("qrow/nt",))
            E("V", lambda g, ntp=ntp: g.tensor_tensor(out=qrow[:, 4:4 + ntp],
                                                      in0=qrow[:, 4:4 + ntp],
                                                      in1=qrow[:, 12:12 + ntp], op=ALU.mult),
              reads=("qrow/sx", "qrow/nt"), writes=("qrow/sx",))
            E("V", lambda g, ntp=ntp: g.tensor_scalar(out=qrow[:, 4:4 + ntp], in0=qrow[:, 4:4 + ntp],
                                                      scalar1=127.0, scalar2=None, op0=ALU.mult),
              reads=("qrow/sx",), writes=("qrow/sx",))
            sw = P["swup"] if dq_rstd_q0 is not None else P["swdn"]
            swkey = "swup" if dq_rstd_q0 is not None else "swdn"
            E("V", lambda g, ntp=ntp, sw=sw: g.tensor_scalar(out=qrow[:, 12:12 + ntp],
                                                             in0=qrow[:, 0:ntp],
                                                             scalar1=1.0 / 127.0,
                                                             scalar2=sw[:, 0:1],
                                                             op0=ALU.mult, op1=ALU.mult),
              reads=("qrow/mx", swkey), writes=("qrow/dq",))
            if dq_rstd_q0 is not None:
                E("V", lambda g, ntp=ntp: g.tensor_tensor(out=qrow[:, 12:12 + ntp],
                                                          in0=qrow[:, 12:12 + ntp],
                                                          in1=qrow[:, 8:8 + ntp], op=ALU.mult),
                  reads=("qrow/dq", "qrow/rst"), writes=("qrow/dq",))
            for j in range(ntp):
                E("T", lambda g, j=j: g.transpose(psq[0:1, 2, j * 128:(j + 1) * 128],
                                                  qrow[:, 4 + j:5 + j], P["idenf"][:]),
                  reads=("qrow/sx", "idenf"), writes=(f"{pk}/s{j}",))
                E("T", lambda g, j=j: g.transpose(psq[0:1, 3, j * 128:(j + 1) * 128],
                                                  qrow[:, 12 + j:13 + j], P["idenf"][:]),
                  reads=("qrow/dq", "idenf"), writes=(f"{pk}/d{j}",))
            E("V", lambda g: g.tensor_copy(out=qr2[0:1, 0:width], in_=psq[0:1, 2, 0:width]),
              reads=tuple(f"{pk}/s{j}" for j in range(ntp)), writes=("qr2/sx",))
            E("V", lambda g: g.tensor_copy(out=qr2[0:1, 512:512 + width], in_=psq[0:1, 3, 0:width]),
              reads=tuple(f"{pk}/d{j}" for j in range(ntp)), writes=("qr2/dq",))
            for j in range(ntp):
                E("T", lambda g, j=j: g.matmul(psq[:, 0, j * 128:(j + 1) * 128], P["onesf"][:],
                                               qr2[0:1, j * 128:(j + 1) * 128], start=True, stop=True),
                  reads=("qr2/sx", "onesf"), writes=(f"{pk}/a{j}",))
                E("T", lambda g, j=j: g.matmul(psq[:, 1, 128 + j * 128:128 + (j + 1) * 128],
                                               P["onesf"][:],
                                               qr2[0:1, 512 + j * 128:512 + (j + 1) * 128],
                                               start=True, stop=True),
                  reads=("qr2/dq", "onesf"), writes=(f"{pk}/e{j}",))
            E("V", lambda g: g.tensor_copy(out=sxdq[:, 0, 0:width], in_=psq[:, 0, 0:width]),
              reads=tuple(f"{pk}/a{j}" for j in range(ntp)), writes=("sxdq/0",))
            E("V", lambda g: g.tensor_copy(out=sxdq[:, 1, 0:width], in_=psq[:, 1, 128:128 + width]),
              reads=tuple(f"{pk}/e{j}" for j in range(ntp)), writes=("sxdq/1",))
            if dbg:
                ckpt("E0d")
            for c in range(nch):
                out_fn(c, slots[c])

        def butterflies(xht, nch, slots, spare, key, width, s0=1):
            s = s0
            while s < nch:
                pairs = [c for c in range(nch) if not (c & s)]
                for pi, c in enumerate(pairs):
                    c2 = c + s
                    eng = "V"
                    spi = 0 if eng == "V" else 1
                    sp = spare[spi]
                    a, b = slots[c], slots[c2]
                    E(eng, lambda g, a=a, b=b, sp=sp: g.tensor_add(
                        out=xht[:, sp, :], in0=xht[:, a, :], in1=xht[:, b, :]),
                      reads=(f"{key}/{a}", f"{key}/{b}"), writes=(f"{key}/{sp}",))
                    E(eng, lambda g, a=a, b=b: g.tensor_sub(
                        out=xht[:, b, :], in0=xht[:, a, :], in1=xht[:, b, :]),
                      reads=(f"{key}/{a}", f"{key}/{b}"), writes=(f"{key}/{b}",))
                    spare[spi] = a
                    slots[c] = sp
                s *= 2

        last_dn_ps = fD_ps
        last_dn_r = []
        xqkeys = tuple(f"xq/{c}/{ql}" for c in range(NCH) for ql in range(2))
        hqkeys = tuple(f"hq/{c}/{qo}" for c in range(UPCH) for qo in (0, 256))
        hbkeys = tuple(f"hbf/{c}" for c in range(UPCH))

        def emit_ffn_half(hb, last_dn_ps, last_dn_r):
            CH0 = QOFF + hb * 512

            ub2 = mem.push("sr", "ub2", [128, 2, 256], F32, fence_marks=last_dn_r)
            xh = mem.push("sr", "xh", [128, 18, 256], F32, fence_marks=last_dn_r)
            ps_x1 = mem.push("p", "ps_x1", [128, 2, 512], F32, fence_marks=last_dn_ps)
            ps_tp = mem.push("p", "ps_tp", [128, 4, 512], F32, fence_marks=last_dn_ps)

            for ql in range(2):
                T0 = CH0 + ql * 256
                slots = list(range(16))
                spare = [16, 17]
                if ql == 0 and hb == 0:
                    ckpt("D2")
                for c in range(NCH):
                    bi = c % 2
                    E(Q, lambda g, c=c, bi=bi, T0=T0: g.dma_start(
                        out=ub2[:, bi, :], in_=u_T[c * 128:(c + 1) * 128, T0:T0 + 256]),
                      writes=(f"ub2/{bi}",), qsem=f"qu{bi}")
                    if c == 0 and ql == 0 and hb == 0:
                        ckpt("D3")
                    E("T", lambda g, bi=bi: g.matmul(ps_x1[:, bi, 0:256], P["h128f"][:], ub2[:, bi, :],
                                                     start=True, stop=True),
                      reads=(f"ub2/{bi}", "h128f"), writes=(f"ps_x1/{bi}",))
                    if c == 0 and ql == 0 and hb == 0:
                        ckpt("D4")
                    E("S", lambda g, c=c, bi=bi: g.activation(out=xh[:, c, :], in_=ps_x1[:, bi, 0:256],
                                                              func=AF.Identity, bias=0.0,
                                                              scale=P["rsc2048"][:, 0:1]),
                      reads=(f"ps_x1/{bi}", "rsc2048"), writes=(f"xh/{c}",))
                    if c == 1 and ql == 0 and hb == 0:
                        ckpt("D5")
                    if c == 3 and ql == 0 and hb == 0:
                        ckpt("D6")
                    if c == 7 and ql == 0 and hb == 0:
                        ckpt("D7")
                    if c == 15 and ql == 0 and hb == 0:
                        ckpt("D8")
                if ql == 0 and hb == 0:
                    ckpt("E0a")
                butterflies(xh, 16, slots, spare, "xh", 256)

                E("V", lambda g, T0=T0: g.tensor_tensor(
                    out=qr2[0:1, 512:768], in0=P["mrow"][0:1, T0:T0 + 256],
                    in1=P["sc2048"][0:1, 0:1].broadcast_to([1, 256]), op=ALU.mult),
                  reads=("mrow", "sc2048"), writes=("qr2/corr",))
                E("V", lambda g, s0=slots[0], xh=xh: g.tensor_sub(
                    out=xh[0:1, s0, :], in0=xh[0:1, s0, :], in1=qr2[0:1, 512:768]),
                  reads=(f"xh/{slots[0]}", "qr2/corr"), writes=(f"xh/{slots[0]}",))
                if ql == 0 and hb == 0:
                    ckpt("E0b")

                def out_q(c, sl, ql=ql):
                    eng = "V"
                    sp = spare[0]
                    E(eng, lambda g, sl=sl, sp=sp: g.tensor_tensor(
                        out=xh[:, sp, :].bitcast(I32), in0=xh[:, sl, :], in1=sxdq[:, 0, :],
                        op=ALU.mult),
                      reads=(f"xh/{sl}", "sxdq/0"), writes=(f"xh/{sp}",))
                    E(eng, lambda g, c=c, sp=sp, ql=ql: g.tensor_tensor(
                        out=xq[:, c, ql * 256:(ql + 1) * 256], in0=xh[:, sp, :].bitcast(I32),
                        in1=sxdq[:, 1, :], op=ALU.mult),
                      reads=(f"xh/{sp}", "sxdq/1"), writes=(f"xq/{c}/{ql}",))

                quant_tile(xh, 16, slots, spare, "xh", 256, ps_tp, "ps_tp", T0, out_q, dbg=(ql == 0 and hb == 0))
                if ql == 0 and hb == 0:
                    ckpt("E1")

            if hb == 0:
                ckpt("EA")
            if "xq" in tap_out and hb == 0:
                E(Q, lambda g: g.dma_start(out=tap_out["xq"][:, :, :], in_=xq[:, :, :]), reads=xqkeys, qsem="qout")

            f_tp = mem.pop("p", "ps_tp")
            f_x1 = mem.pop("p", "ps_x1")
            f_xh = mem.pop("sr", "xh") + mem.pop("sr", "ub2")

            wst3 = mem.push("sr", "wst3", [128, NCH * 128], F32, fence_marks=f_xh)
            wtb = mem.push("sr", "wtb", [128, 2, NCH * 128], BF16, fence_marks=f_xh)
            sgn = mem.push("sr", "sgn", [128, NCH * 128], BF16, fence_marks=f_xh)
            rsc2 = mem.push("sr", "rsc2", [128, 2, 512], F32, fence_marks=f_xh)
            ps_up = mem.push("p", "ps_up", [128, 2, 512], F32, fence_marks=f_x1 + f_tp)

            for oc in range(UPCH):
                wb = oc % 2
                if hb == 0:
                    E(Q, lambda g, oc=oc: g.dma_start(out=wst3[:, :], in_=wup[oc, :, :]), writes=("wst3",), qsem="qw")
                    E("V", lambda g: g.tensor_scalar(out=sgn[:, :], in0=wst3[:, :],
                                                     scalar1=P["nthrup"][:, 0:1], scalar2=None,
                                                     op0=ALU.is_lt),
                      reads=("wst3", "nthrup"), writes=("sgn",))
                    E("V", lambda g, wb=wb: g.tensor_scalar(out=wtb[:, wb, :], in0=wst3[:, :],
                                                            scalar1=P["thrup"][:, 0:1], scalar2=None,
                                                            op0=ALU.is_gt),
                      reads=("wst3", "thrup"), writes=(f"wtb/{wb}",))
                    E("V", lambda g, wb=wb: g.tensor_sub(out=wtb[:, wb, :], in0=wtb[:, wb, :],
                                                         in1=sgn[:, :]),
                      reads=("sgn", f"wtb/{wb}"), writes=(f"wtb/{wb}",))
                    E(Q, lambda g, oc=oc, wb=wb: g.dma_start(out=wup_cache[oc, :, :], in_=wtb[:, wb, :]),
                      reads=(f"wtb/{wb}",), writes=(f"wupc/{oc}",), qsem="qc")
                else:
                    E(Q, lambda g, oc=oc, wb=wb: g.dma_start(out=wtb[:, wb, :], in_=wup_cache[oc, :, :]),
                      reads=(f"wupc/{oc}",), writes=(f"wtb/{wb}",), qsem=f"qt{wb}")
                pp = oc % 2
                for ql in range(2):
                    for kc in range(NCH):
                        E("T", lambda g, wb=wb, kc=kc, pp=pp, ql=ql: g.matmul(
                            ps_up[:, pp, ql * 256:(ql + 1) * 256],
                            wtb[:, wb, kc * 128:(kc + 1) * 128],
                            xq[:, kc, ql * 256:(ql + 1) * 256],
                            start=(kc == 0), stop=(kc == NCH - 1)),
                          reads=(f"wtb/{wb}", f"xq/{kc}/{ql}"), writes=(f"ps_up/{pp}",))
                E("S", lambda g, pp=pp: g.activation(out=rsc2[:, pp, :], in_=ps_up[:, pp, :],
                                                     func=AF.Relu, bias=0.0, scale=1.0),
                  reads=(f"ps_up/{pp}",), writes=(f"rsc2/{pp}",))
                E("V", lambda g, oc=oc, pp=pp: g.tensor_mul(out=hbf[:, oc, :], in0=rsc2[:, pp, :],
                                                            in1=rsc2[:, pp, :]),
                  reads=(f"rsc2/{pp}",), writes=(f"hbf/{oc}",))

            if hb == 0:
                trk.settle("qc", tuple(f"wupc/{oc}" for oc in range(UPCH)))
                ckpt("EB")
            if "h" in tap_out and hb == 0:
                E(Q, lambda g: g.dma_start(out=tap_out["h"][:, :, :], in_=hbf[:, :, :]), reads=hbkeys, qsem="qout")

            f_up = mem.pop("p", "ps_up")
            f_s2 = mem.pop("sr", "rsc2") + mem.pop("sr", "sgn") + mem.pop("sr", "wtb") + \
                mem.pop("sr", "wst3")

            xh2 = mem.push("sr", "xh2", [128, 34, 256], F32, fence_marks=f_s2)
            ps_x2 = mem.push("p", "ps_x2", [128, 2, 512], F32, fence_marks=f_up)
            ps_t2 = mem.push("p", "ps_t2", [128, 4, 512], F32, fence_marks=f_up)

            for ql in range(2):
                QO = ql * 256
                slots2 = list(range(32))
                spare2 = [32, 33]
                for c in range(UPCH):
                    bi = c % 2
                    g4 = (c // 4) * 4
                    for j2 in range(4):
                        neg4 = bin((c % 4) & j2).count("1") & 1
                        wt = "h128nbf" if neg4 else "h128bf"
                        E("T", lambda g, bi=bi, QO=QO, wt=wt, j2=j2, src=g4 + j2: g.matmul(
                            ps_x2[:, bi, 0:256], P[wt][:], hbf[:, src, QO:QO + 256],
                            start=(j2 == 0), stop=(j2 == 3)),
                          reads=(f"hbf/{src}", wt), writes=(f"ps_x2/{bi}",))
                    E("S", lambda g, c=c, bi=bi: g.activation(out=xh2[:, c, :], in_=ps_x2[:, bi, 0:256],
                                                              func=AF.Copy, bias=0.0, scale=0.015625),
                      reads=(f"ps_x2/{bi}",), writes=(f"xh2/{c}",))

                butterflies(xh2, 32, slots2, spare2, "xh2", 256, s0=4)

                def out_h(c, sl, ql=ql):
                    eng = "V"
                    sp = spare2[0]
                    E(eng, lambda g, sl=sl, sp=sp: g.tensor_tensor(
                        out=xh2[:, sp, :].bitcast(I32), in0=xh2[:, sl, :], in1=sxdq[:, 0, :],
                        op=ALU.mult),
                      reads=(f"xh2/{sl}", "sxdq/0"), writes=(f"xh2/{sp}",))
                    E(eng, lambda g, c=c, sp=sp, ql=ql: g.tensor_tensor(
                        out=hq[:, c, ql * 256:(ql + 1) * 256], in0=xh2[:, sp, :].bitcast(I32),
                        in1=sxdq[:, 1, :], op=ALU.mult),
                      reads=(f"xh2/{sp}", "sxdq/1"), writes=(f"hq/{c}/{ql * 256}",))

                quant_tile(xh2, 32, slots2, spare2, "xh2", 256, ps_t2, "ps_t2", None, out_h)
                if ql == 0 and hb == 0:
                    ckpt("E2")

            if hb == 0:
                ckpt("EC")
            if "hq" in tap_out and hb == 0:
                E(Q, lambda g: g.dma_start(out=tap_out["hq"][:, :, :], in_=hq[:, :, :]), reads=hqkeys, qsem="qout")

            f_t2 = mem.pop("p", "ps_t2")
            f_x2 = mem.pop("p", "ps_x2")
            f_xh2 = mem.pop("sr", "xh2")

            wst4 = mem.push("sr", "wst4", [128, UPCH * 128], F32, fence_marks=f_xh2)
            wtb2 = mem.push("sr", "wtb2", [128, 2, UPCH * 128], BF16, fence_marks=f_xh2)
            sgn2 = mem.push("sr", "sgn2", [128, UPCH * 128], BF16, fence_marks=f_xh2)
            ps_dn = mem.push("p", "ps_dn", [128, 2, 512], F32, fence_marks=f_x2 + f_t2)

            for oc in range(NCH):
                wb = oc % 2
                if hb == 0:
                    E(Q, lambda g, oc=oc: g.dma_start(out=wst4[:, :], in_=wdn[oc, :, :]), writes=("wst4",), qsem="qw")
                    E("V", lambda g: g.tensor_scalar(out=sgn2[:, :], in0=wst4[:, :],
                                                     scalar1=P["nthrdn"][:, 0:1], scalar2=None,
                                                     op0=ALU.is_lt),
                      reads=("wst4", "nthrdn"), writes=("sgn2",))
                    E("V", lambda g, wb=wb: g.tensor_scalar(out=wtb2[:, wb, :], in0=wst4[:, :],
                                                            scalar1=P["thrdn"][:, 0:1], scalar2=None,
                                                            op0=ALU.is_gt),
                      reads=("wst4", "thrdn"), writes=(f"wtb2/{wb}",))
                    E("V", lambda g, wb=wb: g.tensor_sub(out=wtb2[:, wb, :], in0=wtb2[:, wb, :],
                                                         in1=sgn2[:, :]),
                      reads=("sgn2", f"wtb2/{wb}"), writes=(f"wtb2/{wb}",))
                    E(Q, lambda g, oc=oc, wb=wb: g.dma_start(out=wdn_cache[oc, :, :], in_=wtb2[:, wb, :]),
                      reads=(f"wtb2/{wb}",), writes=(f"wdnc/{oc}",), qsem="qc")
                else:
                    E(Q, lambda g, oc=oc, wb=wb: g.dma_start(out=wtb2[:, wb, :], in_=wdn_cache[oc, :, :]),
                      reads=(f"wdnc/{oc}",), writes=(f"wtb2/{wb}",), qsem=f"qt{wb}")
                pp = oc % 2
                for ql in range(2):
                    for kc in range(UPCH):
                        E("T", lambda g, wb=wb, kc=kc, pp=pp, ql=ql: g.matmul(
                            ps_dn[:, pp, ql * 256:(ql + 1) * 256],
                            wtb2[:, wb, kc * 128:(kc + 1) * 128],
                            hq[:, kc, ql * 256:(ql + 1) * 256],
                            start=(kc == 0), stop=(kc == UPCH - 1)),
                          reads=(f"wtb2/{wb}", f"hq/{kc}/{ql * 256}"), writes=(f"ps_dn/{pp}",))
                E("V", lambda g, oc=oc, pp=pp, hb=hb: g.tensor_add(
                    out=acc[:, oc, hb * 512:(hb + 1) * 512], in0=acc[:, oc, hb * 512:(hb + 1) * 512],
                    in1=ps_dn[:, pp, :]),
                  reads=(f"ps_dn/{pp}", f"acc/{oc}"), writes=(f"acc/{oc}", f"accf/{oc}/{hb}"))

            if hb == 0:
                trk.settle("qc", tuple(f"wdnc/{oc}" for oc in range(NCH)))
                ckpt("ED")
            last_dn_ps = mem.pop("p", "ps_dn")
            last_dn_r = mem.pop("sr", "sgn2") + mem.pop("sr", "wtb2") + mem.pop("sr", "wst4")
            return last_dn_ps, last_dn_r

        for hb in range(2):
            last_dn_ps, last_dn_r = emit_ffn_half(hb, last_dn_ps, last_dn_r)

        for c in range(NCH):
            E(Q, lambda g, c=c: g.dma_start(out=out_T[c * 128:(c + 1) * 128, :], in_=acc[:, c, :]),
              reads=(f"acc/{c}", f"accf/{c}/0", f"accf/{c}/1"), qsem="qout")


    except _Stop:
        pass


    es = contextlib.ExitStack()
    for e in ENGS:
        trk.sems[e] = es.enter_context(nc.semaphore(f"sem{e}"))
    for qn in trk.qcnt:
        trk.sems[qn] = es.enter_context(nc.semaphore(f"sem_{qn}"))
    qout_total = trk.qcnt.get("qout", 0)
    with es, nc.Block() as block:
        @block.sync
        def _(eng):
            for f in trk.prog["Q"]:
                f(eng)
            if qout_total:
                eng.wait_ge(trk.sems["qout"], qout_total)

        @block.tensor
        def _(eng):
            for f in trk.prog["T"]:
                f(eng)

        @block.scalar
        def _(eng):
            for f in trk.prog["S"]:
                f(eng)

        @block.vector
        def _(eng):
            for f in trk.prog["V"]:
                f(eng)

        @block.gpsimd
        def _(eng):
            for f in trk.prog["P"]:
                f(eng)

    return nc, tap_out


_CACHE = {}


def _get_program(taps=()):
    key = tuple(sorted(taps))
    if key not in _CACHE:
        _CACHE[key] = build_program(taps)
    return _CACHE[key]


def host_inputs(L0_attn_u, L0_attn_Wqkv, L0_attn_bqkv, L0_attn_Wo, L0_attn_bo,
                L0_attn_sinks, L0_ffn_w_up, L0_ffn_w_down):
    wqkv_s = _strips(np.asarray(L0_attn_Wqkv, np.float32), QKV_OC, NCH)
    wo_s = _strips(np.asarray(L0_attn_Wo, np.float32), NCH, NCH)
    wup_s = _strips(np.asarray(L0_ffn_w_up, np.float32), UPCH, NCH)
    wdn_s = _strips(np.asarray(L0_ffn_w_down, np.float32), NCH, UPCH)
    bq_t = np.ascontiguousarray(np.asarray(L0_attn_bqkv, np.float32).reshape(QKV_OC, 128).T)
    bo_t = np.ascontiguousarray(np.asarray(L0_attn_bo, np.float32).reshape(NCH, 128).T)
    sinks = np.asarray(L0_attn_sinks, np.float32)
    s4 = sinks.reshape(16, 2, NS, DH)
    sinkT = np.ascontiguousarray(s4.transpose(1, 3, 0, 2).reshape(128, 16, NS))
    sinkV = np.ascontiguousarray(sinks.transpose(1, 0, 2).reshape(NS, NH * DH))

    h128 = _hadamard128()
    consts = np.zeros((128, 8), np.float32)
    for j, v in enumerate([0.0, 1.0, 1e-5, -0.5, 2048.0, 0.015625]):
        consts[:, j] = v

    shared = dict(
        wqkv=wqkv_s, bqkv_t=bq_t, wo=wo_s, bo_t=bo_t, wup=wup_s, wdn=wdn_s,
        sinkT=sinkT, sinkV=sinkV,
        h128bf=h128.astype(ml_dtypes.bfloat16), h128f=h128,
        idenbf=np.eye(128).astype(ml_dtypes.bfloat16), idenf=np.eye(128, dtype=np.float32),
        pswap=_pswap().astype(ml_dtypes.bfloat16),
        consts=consts, onesf=np.ones((1, 128), np.float32), onesc=np.ones((128, 1), np.float32),
    )

    u = np.asarray(L0_attn_u, np.float32)
    in_maps = []
    for core in range(8):
        b, half = core // 2, core % 2
        qstart = half * 1024
        u_Tc = np.zeros((D, TKV), np.float32)
        lo, hi = qstart - 64, qstart - 64 + TKV
        slo, shi = max(lo, 0), min(hi, L)
        u_Tc[:, slo - lo:shi - lo] = u[b, slo:shi, :].T
        cq, sq, ck, sk = _trig_tables(qstart)
        m0, m1 = _masks(qstart)
        m = dict(shared)
        m.update(u_T=np.ascontiguousarray(u_Tc), cq=cq, sq=sq, ck=ck, sk=sk,
                 mask0=m0, mask1=m1)
        in_maps.append(m)
    return in_maps


def kernel(**inputs):
    nc, _ = _get_program()
    in_maps = host_inputs(**inputs)
    res = run_bass_kernel_spmd(nc, in_maps, core_ids=list(range(8)))
    out = np.empty((B, L, D), np.float32)
    for core in range(8):
        b, half = core // 2, core % 2
        out[b, half * 1024:(half + 1) * 1024, :] = np.asarray(res.results[core]["out_T"]).T
    return out


# version 06 (final)
# speedup vs baseline: 1.7867x; 1.0336x over previous
import contextlib
import numpy as np
import ml_dtypes
import concourse.bass as bass
import concourse.mybir as mybir
from concourse.bass_utils import run_bass_kernel_spmd

F32 = mybir.dt.float32
BF16 = mybir.dt.bfloat16
I32 = mybir.dt.int32
AF = mybir.ActivationFunctionType
ALU = mybir.AluOpType
AX = mybir.AxisListType

B, L, D = 4, 2048, 2048
NCH = D // 128
TKV = 1152
TQ = 1024
QOFF = 64
NH, DH, NS = 32, 64, 16
UP = 4096
UPCH = UP // 128
QKV_OC = 3 * NCH
NEG = -1.0e30
NT_KV = ((0, 512), (512, 512), (1024, 128))
NT_Q = ((64, 512), (576, 512))

ENGS = ("T", "S", "V", "P", "Q")


class Trk:
    def __init__(self):
        self.prog = {e: [] for e in ENGS}
        self.cnt = {e: 0 for e in ENGS}
        self.qcnt = {}
        self.writer = {}
        self.readers = {}
        self.waited = {e: {} for e in ENGS}
        self.pfence = {}
        self.sems = {}
        import os
        self.log = open("/tmp/trk.log", "w") if os.environ.get("TRKLOG") else None

    def _add_wait(self, e, oe, val):
        if val <= 0:
            return
        if self.waited[e].get(oe, 0) >= val:
            return
        self.waited[e][oe] = val
        sems = self.sems
        self.prog[e].append(lambda eng, oe=oe, val=val: eng.wait_ge(sems[oe], val))

    def emit(self, e, fn, reads=(), writes=(), tag="", qsem=None):
        need = {}
        for r in set(reads) | set(writes):
            p = r.split("/", 1)[0]
            marks = self.pfence.get(p)
            if marks:
                for (oe, c) in marks:
                    if oe != e:
                        need[oe] = max(need.get(oe, 0),
                                       self.qcnt[oe] if oe in self.qcnt else c)
        skipT = (e == "T")

        def mval(oe, c):
            return self.qcnt[oe] if oe in self.qcnt else c

        for r in reads:
            w = self.writer.get(r)
            if w is not None and not (skipT and w[0] == "T"):
                need[w[0]] = max(need.get(w[0], 0), mval(*w))
        for r in writes:
            w = self.writer.get(r)
            if w is not None and not (skipT and w[0] == "T"):
                need[w[0]] = max(need.get(w[0], 0), mval(*w))
            for oe, c in self.readers.get(r, {}).items():
                if not (skipT and oe == "T"):
                    need[oe] = max(need.get(oe, 0), mval(oe, c))
        for oe, v in need.items():
            self._add_wait(e, oe, v)
        if e == "Q":
            assert qsem is not None
            self.qcnt[qsem] = self.qcnt.get(qsem, 0) + 16
            me = self.qcnt[qsem]
            mark = (qsem, me)
            sems = self.sems
            self.prog["Q"].append(lambda eng, fn=fn, qsem=qsem: fn(eng).then_inc(sems[qsem], 16))
        else:
            self.cnt[e] += 1
            me = self.cnt[e]
            mark = (e, me)
            sems = self.sems
            self.prog[e].append(lambda eng, fn=fn, e=e: fn(eng).then_inc(sems[e], 1))
        if self.log is not None:
            self.log.write(f"{e}/{qsem} cnt={me} waits={need} reads={reads} writes={writes} {tag}\n")
        for r in writes:
            self.writer[r] = mark
            self.readers[r] = {}
        for r in reads:
            self.readers.setdefault(r, {})[mark[0]] = me
        return mark

    def settle(self, qsem, regions):
        total = self.qcnt.get(qsem, 0)
        for r in regions:
            w = self.writer.get(r)
            if w is not None and w[0] == qsem:
                self.writer[r] = (qsem, total)

    def marks_for(self, prefix):
        out = []
        for r, w in list(self.writer.items()):
            if r == prefix or r.startswith(prefix + "/"):
                if w is not None:
                    out.append(w)
                for oe, c in self.readers.get(r, {}).items():
                    out.append((oe, c))
        return out

    def clear_prefix(self, prefix):
        self.pfence.pop(prefix, None)
        for r in list(self.writer.keys()):
            if r == prefix or r.startswith(prefix + "/"):
                del self.writer[r]
                self.readers.pop(r, None)

    def fence(self, region, marks):
        self.pfence[region] = list(marks)


class Mem:
    def __init__(self, nc, trk):
        self.nc = nc
        self.trk = trk
        self.stack = {"sl": [], "sr": [], "p": []}
        self.serial = 0

    def push(self, which, name, shape, dtype, fence_marks=None):
        self.serial += 1
        tname = f"{name}__{self.serial}"
        if which == "p":
            g = self.nc.psum_tensor(tname, shape, dtype)
        else:
            g = self.nc.sbuf_tensor(tname, shape, dtype,
                                    side=("left" if which == "sl" else "right"))
        h = g.__enter__()
        self.stack[which].append((name, g))
        if fence_marks:
            self.trk.fence(name, fence_marks)
        return h

    def pop(self, which, name):
        top, g = self.stack[which].pop()
        assert top == name, f"LIFO violation: pop {name} but top is {top}"
        marks = self.trk.marks_for(name)
        self.trk.clear_prefix(name)
        g.__exit__(None, None, None)
        return marks


def _hadamard128():
    i = np.arange(128)
    pc = np.bitwise_count(i[:, None] & i[None, :]) if hasattr(np, "bitwise_count") else \
        np.array([bin(a & b).count("1") for a in i for b in i]).reshape(128, 128)
    return np.where(pc % 2 == 0, np.float32(1.0), np.float32(-1.0)).astype(np.float32)


def _pswap():
    p = np.zeros((128, 128), np.float32)
    for m in range(128):
        k = m + 32 if (m % 64) < 32 else m - 32
        p[k, m] = 1.0
    return p


def _trig_tables(qstart):
    import jax
    import jax.numpy as jnp
    cpu = jax.devices("cpu")[0]
    with jax.default_device(cpu):
        half = 32
        inv_freq = 1.0 / 10000.0 ** (jnp.arange(half, dtype=jnp.float32) / half)

        def tables(p0, n, scale):
            pos = jnp.arange(p0, p0 + n, dtype=jnp.float32)
            ang = pos[None, :] * inv_freq[:, None]
            c = np.asarray(jnp.cos(ang), np.float32)
            s = np.asarray(jnp.sin(ang), np.float32)
            C = np.tile(c, (4, 1)) * scale
            S = np.tile(s, (4, 1)) * scale
            sign = np.where((np.arange(128) % 64) < 32, np.float32(-1.0), np.float32(1.0))
            return np.ascontiguousarray(C), np.ascontiguousarray(S * sign[:, None])

        ck, sk = tables(qstart - 64, TKV, np.float32(1.0))
        cq, sq = tables(qstart, TQ, np.float32(0.125))
    return cq, sq, ck, sk


def _masks(qstart):
    out = []
    for g in range(2):
        m = np.zeros((128, 4, 272), np.float32)
        for i in range(4):
            qb = g * 4 + i
            q = np.arange(128)[:, None]
            jw = np.arange(256)[None, :]
            valid = (jw >= q) & (jw <= q + 128)
            kpos = qstart - 64 + qb * 128 + jw
            valid = valid & (kpos >= 0) & (kpos < L)
            m[:, i, 16:] = np.where(valid, np.float32(0.0), np.float32(NEG))
        out.append(np.ascontiguousarray(m.reshape(128, 4 * 272)).astype(ml_dtypes.bfloat16))
    return out


def _strips(w, n_oc, n_kc):
    return np.ascontiguousarray(
        w.reshape(n_oc, 128, n_kc, 128).transpose(0, 3, 2, 1).reshape(n_oc, 128, n_kc * 128))


class _Stop(Exception):
    pass


def build_program(taps=(), stop_after=None):
    nc = bass.Bass("TRN2", target_bir_lowering=False)
    trk = Trk()
    mem = Mem(nc, trk)
    E = trk.emit
    Q = "Q"

    dI = lambda n, s, dt=F32: nc.dram_tensor(n, s, dt, kind="ExternalInput").ap()
    dO = lambda n, s, dt=F32: nc.dram_tensor(n, s, dt, kind="ExternalOutput").ap()

    u_T = dI("u_T", [D, TKV])
    wqkv = dI("wqkv", [QKV_OC, 128, NCH * 128])
    bqkv = dI("bqkv_t", [128, QKV_OC])
    wo = dI("wo", [NCH, 128, NCH * 128])
    bo_in = dI("bo_t", [128, NCH])
    wup = dI("wup", [UPCH, 128, NCH * 128])
    wdn = dI("wdn", [NCH, 128, UPCH * 128])
    sinkT_in = dI("sinkT", [128, NCH, NS])
    sinkV_in = dI("sinkV", [NS, NH * DH])
    cq_in = dI("cq", [128, TQ])
    sq_in = dI("sq", [128, TQ])
    ck_in = dI("ck", [128, TKV])
    sk_in = dI("sk", [128, TKV])
    mask0_in = dI("mask0", [128, 4 * 272], BF16)
    mask1_in = dI("mask1", [128, 4 * 272], BF16)
    h128_in = dI("h128bf", [128, 128], BF16)
    h128f_in = dI("h128f", [128, 128])
    iden_in = dI("idenbf", [128, 128], BF16)
    idenf_in = dI("idenf", [128, 128])
    pswap_in = dI("pswap", [128, 128], BF16)
    consts_in = dI("consts", [128, 8])
    onesf_in = dI("onesf", [1, 128])
    onesc_in = dI("onesc", [128, 1])

    out_T = dO("out_T", [D, TQ])

    wup_cache = nc.dram_tensor("wup_cache", [UPCH, 128, NCH * 128], BF16, kind="Internal").ap()
    wdn_cache = nc.dram_tensor("wdn_cache", [NCH, 128, UPCH * 128], BF16, kind="Internal").ap()

    tapset = set(taps)
    tap_out = {}
    for nm, shape, dt in [("xn", [128, NCH, TKV], BF16), ("q", [128, NCH, TQ], BF16),
                          ("k", [128, NCH, TKV], BF16), ("v", [128, 9, NH * DH], BF16),
                          ("y", [128, NCH, TQ], BF16), ("accwo", [128, NCH, TQ], F32),
                          ("xq", [128, NCH, 512], BF16), ("h", [128, UPCH, 512], BF16),
                          ("hq", [128, UPCH, 512], BF16), ("sw", [128, 2], F32),
                          ("pe", [128, 4, 272], F32), ("pe2", [128, 4, 272], F32),
                          ("pb", [128, 4, 272], BF16), ("pt", [128, 4, 384], BF16),
                          ("ysu", [64, 512], BF16),
                          ("sr0", [128, 16], F32), ("mj", [128, 8], F32),
                          ("rows", [1, TKV * 2], F32)]:
        if nm in tapset:
            tap_out[nm] = dO("tap_" + nm, shape, dt)

    def ckpt(ph):
        if stop_after == ph:
            raise _Stop()

    P = {}
    for nm, shape, dt in [
        ("consts", [128, 8], F32), ("onesc", [128, 1], F32),
        ("onesbf", [128, 1], BF16), ("idenbf", [128, 128], BF16), ("idenf", [128, 128], F32),
        ("pswap", [128, 128], BF16), ("h128bf", [128, 128], BF16), ("h128f", [128, 128], F32),
        ("h128nbf", [128, 128], BF16),
        ("bq", [128, QKV_OC], F32), ("bo", [128, NCH], F32),
        ("rsc2048", [128, 1], F32),
        ("swup", [128, 1], F32), ("thrup", [128, 1], F32), ("nthrup", [128, 1], F32),
        ("swdn", [128, 1], F32), ("thrdn", [128, 1], F32), ("nthrdn", [128, 1], F32),
        ("wacc", [128, 2], F32), ("wsums", [128, 64], F32),
        ("onesf", [1, 128], F32), ("mrow", [1, TKV], F32), ("rrow", [1, TKV], F32),
        ("sc2048", [1, 1], F32), ("ln2048", [1, 1], F32),
    ]:
        P[nm] = mem.push("sl", "sb_" + nm, shape, dt)

    for val, col in [(0.0, 0), (1.0, 1), (1e-5, 2), (-0.5, 3), (2048.0, 4), (0.015625, 5), (0.5, 6)]:
        nc.const_aps.register(F32, val, P["consts"][:, col:col + 1])

    try:
        for nm, src in [("consts", consts_in), ("onesf", onesf_in), ("onesc", onesc_in),
                        ("idenbf", iden_in), ("idenf", idenf_in), ("pswap", pswap_in),
                        ("h128bf", h128_in), ("h128f", h128f_in), ("bq", bqkv), ("bo", bo_in)]:
            E(Q, lambda g, d=P[nm], s=src: g.dma_start(out=d[:], in_=s[:, :]), writes=(nm,), qsem="qx")
        trk.settle("qx", ("consts", "onesf", "onesc", "idenbf", "idenf", "pswap",
                          "h128bf", "h128f", "bq", "bo"))
        E("V", lambda g: g.tensor_copy(out=P["onesbf"][:], in_=P["onesc"][:]),
          reads=("onesc",), writes=("onesbf",))
        E("V", lambda g: g.tensor_scalar(out=P["h128nbf"][:], in0=P["h128bf"][:],
                                         scalar1=-1.0, scalar2=None, op0=ALU.mult),
          reads=("h128bf",), writes=("h128nbf",))
        E("S", lambda g: g.activation(out=P["ln2048"][:], in_=P["consts"][0:1, 4:5], func=AF.Ln,
                                      bias=0.0, scale=1.0), reads=("consts",), writes=("ln2048",))
        E("S", lambda g: g.activation(out=P["sc2048"][:], in_=P["ln2048"][:], func=AF.Exp,
                                      bias=0.0, scale=0.5), reads=("ln2048",), writes=("sc2048",))

        xn = mem.push("sl", "xn", [128, NCH, TKV], BF16)

        ubuf = mem.push("sr", "ubuf", [128, 2, TKV], F32)
        sqb = mem.push("sr", "sqb", [128, 2, TKV], BF16)
        ubfb = mem.push("sr", "ubfb", [128, 2, TKV], BF16)
        arow = mem.push("sr", "arow", [1, 4 * TKV], F32)
        RB = mem.push("sr", "RB", [128, 2, TKV], F32)

        ps_st1 = mem.push("p", "ps_st1", [1, TKV], F32)
        ps_st2 = mem.push("p", "ps_st2", [1, TKV], F32)

        for c in range(NCH):
            bi = c % 2
            E(Q, lambda g, c=c, bi=bi: g.dma_start(out=ubuf[:, bi, :], in_=u_T[c * 128:(c + 1) * 128, :]),
              writes=(f"ubuf/{bi}",), qsem=f"qu{bi}")
            E("V", lambda g, bi=bi: g.tensor_copy(out=ubfb[:, bi, :], in_=ubuf[:, bi, :]),
              reads=(f"ubuf/{bi}",), writes=(f"ubfb/{bi}",))
            E("V", lambda g, bi=bi: g.tensor_mul(out=sqb[:, bi, :], in0=ubfb[:, bi, :], in1=ubfb[:, bi, :]),
              reads=(f"ubfb/{bi}",), writes=(f"sqb/{bi}",))
            for (o, n) in NT_KV:
                E("T", lambda g, c=c, bi=bi, o=o, n=n: g.matmul(
                    ps_st1[:, o:o + n], P["onesbf"][:], ubfb[:, bi, o:o + n],
                    start=(c == 0), stop=(c == NCH - 1)),
                  reads=(f"ubfb/{bi}", "onesbf"), writes=(f"ps_st1/{o}",))
                E("T", lambda g, c=c, bi=bi, o=o, n=n: g.matmul(
                    ps_st2[:, o:o + n], P["onesbf"][:], sqb[:, bi, o:o + n],
                    start=(c == 0), stop=(c == NCH - 1)),
                  reads=(f"sqb/{bi}", "onesbf"), writes=(f"ps_st2/{o}",))

        mr, rr = P["mrow"], P["rrow"]
        e2 = arow[0:1, 0:TKV]
        var = arow[0:1, TKV:2 * TKV]
        lnv = arow[0:1, 2 * TKV:3 * TKV]
        brow = arow[0:1, 3 * TKV:4 * TKV]
        st1keys = tuple(f"ps_st1/{o}" for (o, n) in NT_KV)
        st2keys = tuple(f"ps_st2/{o}" for (o, n) in NT_KV)
        E("V", lambda g: g.tensor_scalar(out=mr[:], in0=ps_st1[:, :], scalar1=1.0 / 2048.0,
                                         scalar2=None, op0=ALU.mult), reads=st1keys, writes=("mrow",))
        E("V", lambda g: g.tensor_scalar(out=e2, in0=ps_st2[:, :], scalar1=1.0 / 2048.0,
                                         scalar2=None, op0=ALU.mult), reads=st2keys, writes=("arow/e2",))
        E("V", lambda g: g.tensor_mul(out=var, in0=mr[:], in1=mr[:]), reads=("mrow",), writes=("arow/var",))
        E("V", lambda g: g.tensor_sub(out=var, in0=e2, in1=var),
          reads=("arow/e2", "arow/var"), writes=("arow/var",))
        E("S", lambda g: g.activation(out=lnv, in_=var, func=AF.Ln, bias=1e-5, scale=1.0),
          reads=("arow/var",), writes=("arow/lnv",))
        E("S", lambda g: g.activation(out=rr[:], in_=lnv, func=AF.Exp, bias=0.0, scale=-0.5),
          reads=("arow/lnv",), writes=("rrow",))
        E("V", lambda g: g.tensor_mul(out=brow, in0=mr[:], in1=rr[:]),
          reads=("mrow", "rrow"), writes=("arow/brow",))
        E("V", lambda g: g.tensor_scalar(out=brow, in0=brow, scalar1=-1.0, scalar2=None, op0=ALU.mult),
          reads=("arow/brow",), writes=("arow/brow",))
        if "rows" in tap_out:
            E(Q, lambda g: g.dma_start(out=tap_out["rows"][0:1, 0:TKV], in_=mr[:]), reads=("mrow",), qsem="qout")
            E(Q, lambda g: g.dma_start(out=tap_out["rows"][0:1, TKV:2 * TKV], in_=rr[:]), reads=("rrow",), qsem="qout")
        E("S", lambda g: g.activation(out=arow[0:1, 0:1], in_=P["ln2048"][:], func=AF.Exp,
                                      bias=0.0, scale=-0.5),
          reads=("ln2048",), writes=("arow/rsc",))

        f_st2 = mem.pop("p", "ps_st2")
        f_st1 = mem.pop("p", "ps_st1")
        ps_rb = mem.push("p", "ps_rb", [128, 2, 1536], F32, fence_marks=f_st1 + f_st2)
        for (o, n) in NT_KV:
            E("T", lambda g, o=o, n=n: g.matmul(ps_rb[:, 0, o:o + n], P["onesf"][:], rr[0:1, o:o + n],
                                                start=True, stop=True),
              reads=("rrow", "onesf"), writes=(f"ps_rb/0/{o}",))
            E("T", lambda g, o=o, n=n: g.matmul(ps_rb[:, 1, o:o + n], P["onesf"][:],
                                                arow[0:1, 3 * TKV + o:3 * TKV + o + n],
                                                start=True, stop=True),
              reads=("arow/brow", "onesf"), writes=(f"ps_rb/1/{o}",))
        E("T", lambda g: g.matmul(ps_rb[:, 0, TKV:TKV + 1], P["onesf"][:], arow[0:1, 0:1],
                                  start=True, stop=True),
          reads=("arow/rsc", "onesf"), writes=("ps_rb/rsc",))
        E("V", lambda g: g.tensor_copy(out=P["rsc2048"][:], in_=ps_rb[:, 0, TKV:TKV + 1]),
          reads=("ps_rb/rsc",), writes=("rsc2048",))
        rbk0 = tuple(f"ps_rb/0/{o}" for (o, n) in NT_KV)
        rbk1 = tuple(f"ps_rb/1/{o}" for (o, n) in NT_KV)
        E("V", lambda g: g.tensor_copy(out=RB[:, 0, :], in_=ps_rb[:, 0, 0:TKV]), reads=rbk0, writes=("RB/0",))
        E("V", lambda g: g.tensor_copy(out=RB[:, 1, :], in_=ps_rb[:, 1, 0:TKV]), reads=rbk1, writes=("RB/1",))

        for c in range(NCH):
            bi = c % 2
            E(Q, lambda g, c=c, bi=bi: g.dma_start(out=ubuf[:, bi, :], in_=u_T[c * 128:(c + 1) * 128, :]),
              writes=(f"ubuf/{bi}",), qsem=f"qu{bi}")
            eng = "V"
            E(eng, lambda g, bi=bi: g.tensor_mul(out=ubuf[:, bi, :], in0=ubuf[:, bi, :], in1=RB[:, 0, :]),
              reads=(f"ubuf/{bi}", "RB/0"), writes=(f"ubuf/{bi}",))
            E(eng, lambda g, c=c, bi=bi: g.tensor_add(out=xn[:, c, :], in0=ubuf[:, bi, :], in1=RB[:, 1, :]),
              reads=(f"ubuf/{bi}", "RB/1"), writes=(f"xn/{c}",))

        xnkeys = tuple(f"xn/{c}" for c in range(NCH))
        if "xn" in tap_out:
            E(Q, lambda g: g.dma_start(out=tap_out["xn"][:, :, :], in_=xn[:, :, :]), reads=xnkeys, qsem="qout")
        ckpt("A")

        f_rb = mem.pop("p", "ps_rb")
        fA = []
        for nm in ("RB", "arow", "ubfb", "sqb", "ubuf"):
            fA += mem.pop("sr", nm)

        q_sb = mem.push("sr", "q", [128, NCH, TQ], BF16, fence_marks=fA)
        k_sb = mem.push("sr", "k", [128, NCH, TKV], BF16, fence_marks=fA)
        v_tm = mem.push("sr", "V", [128, 9, NH * DH], BF16, fence_marks=fA)
        wst = mem.push("sr", "wst", [128, NCH * 128], F32, fence_marks=fA)
        wbf = mem.push("sr", "wbf", [128, 2, NCH * 128], BF16, fence_marks=fA)
        vbf = mem.push("sr", "vbf", [128, 2, 512], BF16, fence_marks=fA)

        trig = mem.push("sl", "trig", [128, 2, TKV], F32)
        rsc = mem.push("sl", "rsc", [128, 2, 512], F32)

        ps_qkv = mem.push("p", "ps_qkv", [128, 2, 3, 512], F32, fence_marks=f_rb)
        ps_sw = mem.push("p", "ps_sw", [128, 512], F32, fence_marks=f_rb)
        ps_vt = mem.push("p", "ps_vt", [128, 1024], BF16, fence_marks=f_rb)

        def cvt(e, dst, src, rkey, wkey):
            if e == "S":
                E(e, lambda g: g.activation(out=dst, in_=src, func=AF.Copy, bias=0.0, scale=1.0),
                  reads=(rkey,) if isinstance(rkey, str) else rkey,
                  writes=(wkey,) if isinstance(wkey, str) else wkey)
            else:
                E(e, lambda g: g.tensor_copy(out=dst, in_=src),
                  reads=(rkey,) if isinstance(rkey, str) else rkey,
                  writes=(wkey,) if isinstance(wkey, str) else wkey)

        E(Q, lambda g: g.dma_start(out=trig[:, 0, 0:TQ], in_=cq_in[:, :]), writes=("trig/c",), qsem="qtr")
        E(Q, lambda g: g.dma_start(out=trig[:, 1, 0:TQ], in_=sq_in[:, :]), writes=("trig/s",), qsem="qtr")
        trk.settle("qtr", ("trig/c", "trig/s"))

        def rope_chunk(c, kind):
            tens = q_sb if kind == "q" else k_sb
            tiles = NT_Q if kind == "q" else NT_KV
            toff = QOFF if kind == "q" else 0
            for (o, n) in tiles:
                co = o - toff
                E("T", lambda g, c=c, o=o, n=n, co=co, tens=tens: g.matmul(
                    ps_sw[:, 0:n], P["pswap"][:], tens[:, c, co:co + n], start=True, stop=True),
                  reads=(f"{kind}/n/{c}", "pswap"), writes=("ps_sw",))
                E("V", lambda g, n=n, co=co: g.tensor_tensor(
                    out=rsc[:, 0, 0:n], in0=ps_sw[:, 0:n], in1=trig[:, 1, co:co + n], op=ALU.mult),
                  reads=("ps_sw", "trig/s"), writes=("rsc/m",))
                E("V", lambda g, c=c, n=n, co=co, tens=tens: g.tensor_tensor(
                    out=rsc[:, 1, 0:n], in0=tens[:, c, co:co + n], in1=trig[:, 0, co:co + n], op=ALU.mult),
                  reads=(f"{kind}/n/{c}", "trig/c"), writes=("rsc/t1",))
                E("V", lambda g, c=c, n=n, co=co, tens=tens: g.tensor_add(
                    out=tens[:, c, co:co + n], in0=rsc[:, 1, 0:n], in1=rsc[:, 0, 0:n]),
                  reads=("rsc/t1", "rsc/m"), writes=(f"{kind}/{c}",))

        conv_eng = ("S", "V", "S")
        for oc in range(QKV_OC):
            wb = oc % 2
            E(Q, lambda g, oc=oc: g.dma_start(out=wst[:, :], in_=wqkv[oc, :, :]), writes=("wst",), qsem="qw")
            cvt(conv_eng[oc % 3], wbf[:, wb, :], wst[:, :], "wst", f"wbf/{wb}")
            kind = "q" if oc < NCH else ("k" if oc < 2 * NCH else "v")
            c = oc % NCH
            tiles = NT_Q if kind == "q" else NT_KV
            pp = oc % 2
            for ni, (o, n) in enumerate(tiles):
                for kc in range(NCH):
                    E("T", lambda g, wb=wb, kc=kc, pp=pp, ni=ni, o=o, n=n: g.matmul(
                        ps_qkv[:, pp, ni, 0:n], wbf[:, wb, kc * 128:(kc + 1) * 128],
                        xn[:, kc, o:o + n], start=(kc == 0), stop=(kc == NCH - 1)),
                      reads=(f"wbf/{wb}", f"xn/{kc}"), writes=(f"ps_qkv/{pp}/{ni}",))
            if kind in ("q", "k"):
                tens = q_sb if kind == "q" else k_sb
                toff = QOFF if kind == "q" else 0
                for ni, (o, n) in enumerate(tiles):
                    E("S", lambda g, oc=oc, c=c, pp=pp, ni=ni, o=o, n=n, toff=toff, tens=tens: g.activation(
                        out=tens[:, c, o - toff:o - toff + n], in_=ps_qkv[:, pp, ni, 0:n],
                        func=AF.Identity, bias=P["bq"][:, oc:oc + 1], scale=1.0),
                      reads=(f"ps_qkv/{pp}/{ni}", "bq"), writes=(f"{kind}/n/{c}",))
                rope_chunk(c, kind)
                if kind == "q" and c == NCH - 1:
                    E(Q, lambda g: g.dma_start(out=trig[:, 0, :], in_=ck_in[:, :]), writes=("trig/c",), qsem="qtr")
                    E(Q, lambda g: g.dma_start(out=trig[:, 1, :], in_=sk_in[:, :]), writes=("trig/s",), qsem="qtr")
                    trk.settle("qtr", ("trig/c", "trig/s"))
            else:
                vc = oc - 2 * NCH
                for ni, (o, n) in enumerate(tiles):
                    vb = ni % 2
                    E("S", lambda g, oc=oc, pp=pp, ni=ni, n=n, vb=vb: g.activation(
                        out=vbf[:, vb, 0:n], in_=ps_qkv[:, pp, ni, 0:n],
                        func=AF.Identity, bias=P["bq"][:, oc:oc + 1], scale=1.0),
                      reads=(f"ps_qkv/{pp}/{ni}", "bq"), writes=(f"vbf/{vb}",))
                    ntp = n // 128
                    for j in range(ntp):
                        E("T", lambda g, vb=vb, j=j: g.transpose(
                            ps_vt[:, j * 128:(j + 1) * 128], vbf[:, vb, j * 128:(j + 1) * 128],
                            P["idenbf"][:]),
                          reads=(f"vbf/{vb}", "idenbf"), writes=(f"ps_vt/{j}",))
                    jt0 = o // 128
                    E("V", lambda g, vc=vc, jt0=jt0, n=n, ntp=ntp: g.tensor_copy(
                        out=v_tm[:, jt0:jt0 + ntp, vc * 128:(vc + 1) * 128],
                        in_=ps_vt[:, 0:n].rearrange("p (j f) -> p j f", f=128)),
                      reads=tuple(f"ps_vt/{j}" for j in range(ntp)), writes=(f"V/{vc}",))

        qkeys = tuple(f"q/{c}" for c in range(NCH))
        kkeys = tuple(f"k/{c}" for c in range(NCH))
        vkeys = tuple(f"V/{vc}" for vc in range(NCH))
        if "q" in tap_out:
            E(Q, lambda g: g.dma_start(out=tap_out["q"][:, :, :], in_=q_sb[:, :, :]), reads=qkeys, qsem="qout")
        if "k" in tap_out:
            E(Q, lambda g: g.dma_start(out=tap_out["k"][:, :, :], in_=k_sb[:, :, :]), reads=kkeys, qsem="qout")
        if "v" in tap_out:
            E(Q, lambda g: g.dma_start(out=tap_out["v"][:, :, :], in_=v_tm[:, :, :]), reads=vkeys, qsem="qout")
        ckpt("B")

        fB_ps = []
        for nm in ("ps_vt", "ps_sw", "ps_qkv"):
            fB_ps += mem.pop("p", nm)
        fB_l = mem.pop("sl", "rsc") + mem.pop("sl", "trig")
        fB_l += mem.pop("sl", "xn")
        y_sb = mem.push("sl", "y", [128, NCH, TQ], BF16, fence_marks=fB_l)
        fB_r = []
        for nm in ("vbf", "wbf", "wst"):
            fB_r += mem.pop("sr", nm)

        masks = mem.push("sr", "masks", [128, 2, 4 * 272], BF16, fence_marks=fB_r)
        sbf = mem.push("sr", "sbf", [128, NCH, NS], BF16, fence_marks=fB_r)
        svbf = mem.push("sr", "svbf", [NS, NH * DH], BF16, fence_marks=fB_r)
        stf = mem.push("sr", "stf", [128, NCH, NS], F32, fence_marks=fB_r)
        pe_sc = mem.push("sr", "pe", [128, 2, 4, 272], F32, fence_marks=fB_r)
        pe2_sc = mem.push("sr", "pe2", [128, 2, 4, 272], F32, fence_marks=fB_r)
        pb_sc = mem.push("sr", "pb", [128, 2, 4, 272], BF16, fence_marks=fB_r)
        pt_sb = mem.push("sr", "pt", [128, 2, 4, 384], BF16, fence_marks=fB_r)
        srow = mem.push("sr", "srow", [128, 2, 16], F32, fence_marks=fB_r)
        mst = mem.push("sr", "mst", [128, 2, 2048], F32, fence_marks=fB_r)

        E(Q, lambda g: g.dma_start(out=masks[:, 0, :], in_=mask0_in[:, :]), writes=("masks/0",), qsem="qx2")
        E(Q, lambda g: g.dma_start(out=masks[:, 1, :], in_=mask1_in[:, :]), writes=("masks/1",), qsem="qx2")
        E(Q, lambda g: g.dma_start(out=stf[:, :, :], in_=sinkT_in[:, :, :]), writes=("stf",), qsem="qx2")
        E(Q, lambda g: g.dma_start(out=mst[0:NS, 0, :], in_=sinkV_in[:, 0:2048]), writes=("mst/0",), qsem="qx2")
        trk.settle("qx2", ("masks/0", "masks/1", "stf", "mst/0"))
        E("V", lambda g: g.tensor_copy(out=sbf[:, :, :], in_=stf[:, :, :]), reads=("stf",), writes=("sbf",))
        E("V", lambda g: g.tensor_copy(out=svbf[:, :], in_=mst[0:NS, 0, :]), reads=("mst/0",), writes=("svbf",))

        ps_S = mem.push("p", "ps_S", [128, 4, 512], F32, fence_marks=fB_ps)
        ps_PT = mem.push("p", "ps_PT", [128, 4, 512], BF16, fence_marks=fB_ps)
        E("S", lambda g: g.memzero(ps_PT[:, :, :]),
          writes=tuple(f"ps_PT/{i}" for i in range(4)))
        ps_Y = mem.push("p", "ps_Y", [64, 2, 512], F32, fence_marks=fB_ps)

        mean_jobs = [("up", oc, 0) for oc in range(UPCH)] + \
                    [("dn", oc, off) for oc in range(NCH) for off in (0, 2048)]

        def emit_mean(idx):
            which, oc, off = mean_jobs[idx]
            bi = idx % 2
            src = wup[oc, :, :] if which == "up" else wdn[oc, :, off:off + 2048]
            wcol = 0 if which == "up" else 1
            E(Q, lambda g, src=src, bi=bi: g.dma_start(out=mst[:, bi, :], in_=src), writes=(f"mst/{bi}",), qsem=f"qm{bi}")
            E("V", lambda g, bi=bi, idx=idx: g.reduce_sum(out=P["wsums"][:, idx:idx + 1],
                                                          in_=mst[:, bi, :], axis=AX.X,
                                                          apply_absolute_value=True),
              reads=(f"mst/{bi}",), writes=(f"wsums/{idx}",))

        def su(h, g2):
            hp = (h % 2) * 64
            ch = h // 2
            vch = (h * DH) // 128
            pp = (h * 2 + g2) % 2
            for i in range(4):
                qb = g2 * 4 + i
                E("T", lambda g, hp=hp, ch=ch, qb=qb, i=i: g.matmul(
                    ps_S[:, i, 0:16], q_sb[hp:hp + 64, ch, qb * 128:(qb + 1) * 128],
                    sbf[hp:hp + 64, ch, :], start=True, stop=True),
                  reads=(f"q/{ch}", "sbf"), writes=(f"ps_S/{i}",))
                E("T", lambda g, hp=hp, ch=ch, qb=qb, i=i: g.matmul(
                    ps_S[:, i, 16:272], q_sb[hp:hp + 64, ch, qb * 128:(qb + 1) * 128],
                    k_sb[hp:hp + 64, ch, qb * 128:qb * 128 + 256], start=True, stop=True),
                  reads=(f"q/{ch}", f"k/{ch}"), writes=(f"ps_S/{i}",))
                E("V", lambda g, i=i, g2=g2, pp=pp: g.scalar_tensor_tensor(
                    out=pe_sc[:, pp, i, :], in0=ps_S[:, i, 0:272], scalar=1.0,
                    in1=masks[:, g2, i * 272:(i + 1) * 272], op0=ALU.mult, op1=ALU.add),
                  reads=(f"ps_S/{i}", f"masks/{g2}"), writes=(f"pe/{pp}/{i}",))
            E("V", lambda g, pp=pp: g.reduce_max(out=srow[:, pp, 0:4], in_=pe_sc[:, pp, :, :],
                                                 axis=AX.X, negate=True),
              reads=tuple(f"pe/{pp}/{i}" for i in range(4)), writes=(f"srow/{pp}/negmx",))
            for i in range(4):
                E("S", lambda g, pp=pp, i=i: g.activation(
                    out=pe_sc[:, pp, i, :], in_=pe_sc[:, pp, i, :], func=AF.Exp,
                    bias=srow[:, pp, i:i + 1], scale=1.0, accum_out=srow[:, pp, 4 + i:5 + i]),
                  reads=(f"pe/{pp}/{i}", f"srow/{pp}/negmx"),
                  writes=(f"pe/{pp}/{i}", f"srow/{pp}/sum",))
            E("V", lambda g, pp=pp: g.reciprocal(out=srow[:, pp, 8:12], in_=srow[:, pp, 4:8]),
              reads=(f"srow/{pp}/sum",), writes=(f"srow/{pp}/rec",))
            for i in range(4):
                eng = "S"
                if eng == "S":
                    E(eng, lambda g, pp=pp, i=i: g.activation(
                        out=pb_sc[:, pp, i, :], in_=pe_sc[:, pp, i, :], func=AF.Identity,
                        bias=0.0, scale=srow[:, pp, 8 + i:9 + i]),
                      reads=(f"pe/{pp}/{i}", f"srow/{pp}/rec"), writes=(f"pb/{pp}/{i}",))
                else:
                    E(eng, lambda g, pp=pp, i=i: g.tensor_scalar(
                        out=pb_sc[:, pp, i, :], in0=pe_sc[:, pp, i, :],
                        scalar1=srow[:, pp, 8 + i:9 + i], scalar2=None, op0=ALU.mult),
                      reads=(f"pe/{pp}/{i}", f"srow/{pp}/rec"), writes=(f"pb/{pp}/{i}",))
                E("T", lambda g, pp=pp, i=i: g.transpose(
                    ps_PT[:, i, 0:128], pb_sc[:, pp, i, 16:144], P["idenbf"][:]),
                  reads=(f"pb/{pp}/{i}", "idenbf"), writes=(f"ps_PT/{i}",))
                E("T", lambda g, pp=pp, i=i: g.transpose(
                    ps_PT[:, i, 128:256], pb_sc[:, pp, i, 144:272], P["idenbf"][:]),
                  reads=(f"pb/{pp}/{i}",), writes=(f"ps_PT/{i}",))
                E("T", lambda g, pp=pp, i=i: g.transpose(
                    ps_PT[0:16, i, 256:384], pb_sc[:, pp, i, 0:16], P["idenbf"][:]),
                  reads=(f"pb/{pp}/{i}",), writes=(f"ps_PT/{i}",))
            for half in range(2):
                E("V", lambda g, pp=pp, half=half: g.tensor_copy(
                    out=pt_sb[:, pp, 2 * half:2 * half + 2, :],
                    in_=ps_PT[:, 2 * half:2 * half + 2, 0:384]),
                  reads=(f"ps_PT/{2 * half}", f"ps_PT/{2 * half + 1}"),
                  writes=(f"pt/{pp}/{2 * half}", f"pt/{pp}/{2 * half + 1}"))
            yp = (h * 2 + g2) % 2
            for i in range(4):
                qb = g2 * 4 + i
                E("T", lambda g, pp=pp, i=i, h=h: g.matmul(
                    ps_Y[:, yp, i * 128:(i + 1) * 128], svbf[:, h * 64:(h + 1) * 64],
                    pt_sb[0:16, pp, i, 256:384], start=True, stop=False),
                  reads=(f"pt/{pp}/{i}", "svbf"), writes=(f"ps_Y/{yp}",))
                E("T", lambda g, pp=pp, i=i, h=h, qb=qb: g.matmul(
                    ps_Y[:, yp, i * 128:(i + 1) * 128], v_tm[:, qb, h * 64:(h + 1) * 64],
                    pt_sb[:, pp, i, 0:128], start=False, stop=False),
                  reads=(f"pt/{pp}/{i}", f"V/{vch}"), writes=(f"ps_Y/{yp}",))
                E("T", lambda g, pp=pp, i=i, h=h, qb=qb: g.matmul(
                    ps_Y[:, yp, i * 128:(i + 1) * 128], v_tm[:, qb + 1, h * 64:(h + 1) * 64],
                    pt_sb[:, pp, i, 128:256], start=False, stop=True),
                  reads=(f"pt/{pp}/{i}", f"V/{vch}"), writes=(f"ps_Y/{yp}",))
            E("S", lambda g, h=h, g2=g2, yp=yp, hp=hp, ch=ch: g.activation(
                out=y_sb[hp:hp + 64, ch, g2 * 512:(g2 + 1) * 512], in_=ps_Y[:, yp, :],
                func=AF.Copy, bias=0.0, scale=1.0),
              reads=(f"ps_Y/{yp}",), writes=(f"y/{h}/{g2}",))

        import os as _os
        _dsu = int(_os.environ.get("DSU", "0"))
        tap_wacc_final = None
        if "pe" in tap_out:
            su(_dsu, 0)
            E(Q, lambda g: g.dma_start(out=tap_out["pe"][:, :, :], in_=pe_sc[:, 0, :, :]),
              reads=tuple(f"pe/0/{i}" for i in range(4)), qsem="qout")
            E(Q, lambda g: g.dma_start(out=tap_out["pe2"][:, :, :], in_=pe2_sc[:, 0, :, :]),
              reads=tuple(f"pe2/0/{i}" for i in range(4)), qsem="qout")
            E(Q, lambda g: g.dma_start(out=tap_out["sr0"][:, :], in_=srow[:, 0, :]),
              reads=("srow/0/negmx", "srow/0/sum", "srow/0/rec"), qsem="qout")
            if "pb" in tap_out:
                E(Q, lambda g: g.dma_start(out=tap_out["pb"][:, :, :], in_=pb_sc[:, 0, :, :]),
                  reads=tuple(f"pb/0/{i}" for i in range(4)), qsem="qout")
                E(Q, lambda g: g.dma_start(out=tap_out["pt"][:, :, :], in_=pt_sb[:, 0, :, :]),
                  reads=tuple(f"pt/0/{i}" for i in range(4)), qsem="qout")
                E(Q, lambda g, hp=(_dsu % 2) * 64, ch=_dsu // 2: g.dma_start(
                    out=tap_out["ysu"][:, :], in_=y_sb[hp:hp + 64, ch, 0:512]),
                  reads=(f"y/{_dsu}/0",), qsem="qout")
        if "mj" in tap_out:
            _mjdone = [False]
            def _tap_wacc_final():
                E(Q, lambda g: g.dma_start(out=tap_out["mj"][:, 2:4], in_=P["wacc"][:, 0:2]),
                  reads=("wacc",), qsem="qout")
            tap_wacc_final = _tap_wacc_final
        else:
            tap_wacc_final = None
        _nosu = bool(_os.environ.get("DBG_NOSU"))
        mean_i = 0
        for h in range(NH):
            for g2 in range(2):
                if not _nosu and not ("pe" in tap_out and h == _dsu and g2 == 0):
                    su(h, g2)
                want = ((h * 2 + g2 + 1) * len(mean_jobs)) // 64
                while mean_i < min(want, len(mean_jobs)):
                    emit_mean(mean_i)
                    mean_i += 1
        while mean_i < len(mean_jobs):
            emit_mean(mean_i)
            mean_i += 1

        ykeys = tuple(f"y/{h}/{g2}" for h in range(NH) for g2 in range(2))
        if "y" in tap_out:
            E(Q, lambda g: g.dma_start(out=tap_out["y"][:, :, :], in_=y_sb[:, :, :]), reads=ykeys, qsem="qout")

        fC_ps = []
        for nm in ("ps_Y", "ps_PT", "ps_S"):
            fC_ps += mem.pop("p", nm)
        E("V", lambda g: g.reduce_sum(out=P["wacc"][:, 0:1], in_=P["wsums"][:, 0:32], axis=AX.X),
          reads=tuple(f"wsums/{i}" for i in range(32)), writes=("wacc",))
        E("V", lambda g: g.reduce_sum(out=P["wacc"][:, 1:2], in_=P["wsums"][:, 32:64], axis=AX.X),
          reads=tuple(f"wsums/{i}" for i in range(32, 64)), writes=("wacc",))
        ps_w = mem.push("p", "ps_w", [1, 2], F32, fence_marks=fC_ps)
        E("T", lambda g: g.matmul(ps_w[:, 0:1], P["wacc"][:, 0:1], P["onesc"][:], start=True, stop=True),
          reads=("wacc", "onesc"), writes=("ps_w/0",))
        E("T", lambda g: g.matmul(ps_w[:, 1:2], P["wacc"][:, 1:2], P["onesc"][:], start=True, stop=True),
          reads=("wacc", "onesc"), writes=("ps_w/1",))
        E("V", lambda g: g.tensor_copy(out=srow[0:1, 0, 0:2], in_=ps_w[:, 0:2]),
          reads=("ps_w/0", "ps_w/1"), writes=("srow/swr",))
        ps_wb = mem.push("p", "ps_wb", [128, 4], F32)
        E("T", lambda g: g.matmul(ps_wb[:, 0:2], P["onesf"][:], srow[0:1, 0, 0:2], start=True, stop=True),
          reads=("srow/swr", "onesf"), writes=("ps_wb",))
        for nm, col in (("swup", 0), ("swdn", 1)):
            E("V", lambda g, nm=nm, col=col: g.tensor_scalar(
                out=P[nm][:], in0=ps_wb[:, col:col + 1], scalar1=1.0 / (UP * D), scalar2=1e-5,
                op0=ALU.mult, op1=ALU.max),
              reads=("ps_wb",), writes=(nm,))
        for nm, sw, f in (("thrup", "swup", 0.5), ("thrdn", "swdn", 0.5),
                          ("nthrup", "swup", -0.5), ("nthrdn", "swdn", -0.5)):
            E("V", lambda g, nm=nm, sw=sw, f=f: g.tensor_scalar(out=P[nm][:], in0=P[sw][:], scalar1=f,
                                                                scalar2=None, op0=ALU.mult),
              reads=(sw,), writes=(nm,))
        if tap_wacc_final is not None:
            E(Q, lambda g: g.dma_start(out=tap_out["mj"][:, 0:2], in_=P["wsums"][:, 0:2]),
              reads=("wsums/0", "wsums/1"), qsem="qout")
            E(Q, lambda g: g.dma_start(out=tap_out["mj"][:, 2:4], in_=P["wacc"][:, 0:2]),
              reads=("wacc",), qsem="qout")
            E(Q, lambda g: g.dma_start(out=tap_out["mj"][0:1, 4:6], in_=srow[0:1, 0, 0:2]),
              reads=("srow/swr",), qsem="qout")
        if "sw" in tap_out:
            E("V", lambda g: g.tensor_copy(out=P["wacc"][:, 0:1], in_=P["swup"][:, :]),
              reads=("swup",), writes=("wacc",))
            E("V", lambda g: g.tensor_copy(out=P["wacc"][:, 1:2], in_=P["swdn"][:, :]),
              reads=("swdn",), writes=("wacc",))
            E(Q, lambda g: g.dma_start(out=tap_out["sw"][:, :], in_=P["wacc"][:, :]),
              reads=("wacc",), qsem="qout")
        fC_ps += mem.pop("p", "ps_wb")
        fC_ps += mem.pop("p", "ps_w")
        ckpt("C")

        fC = []
        for nm in ("mst", "srow", "pt", "pb", "pe2", "pe", "stf", "svbf", "sbf", "masks",
                   "V", "k", "q"):
            fC += mem.pop("sr", nm)

        acc = mem.push("sr", "acc", [128, NCH, TQ], F32, fence_marks=fC)
        wst2 = mem.push("sr", "wst2", [128, NCH * 128], F32, fence_marks=fC)
        wbf2 = mem.push("sr", "wbf2", [128, 2, NCH * 128], BF16, fence_marks=fC)
        ps_wo = mem.push("p", "ps_wo", [128, 2, 2, 512], F32, fence_marks=fC_ps)

        for c in range(NCH):
            E(Q, lambda g, c=c: g.dma_start(out=acc[:, c, :], in_=u_T[c * 128:(c + 1) * 128, QOFF:QOFF + TQ]),
              writes=(f"acc/{c}",), qsem="qa")
        trk.settle("qa", tuple(f"acc/{c}" for c in range(NCH)))

        for oc in range(NCH):
            wb = oc % 2
            E(Q, lambda g, oc=oc: g.dma_start(out=wst2[:, :], in_=wo[oc, :, :]), writes=("wst2",), qsem="qw")
            cvt(conv_eng[oc % 3], wbf2[:, wb, :], wst2[:, :], "wst2", f"wbf2/{wb}")
            pp = oc % 2
            for ni in range(2):
                for kc in range(NCH):
                    E("T", lambda g, wb=wb, kc=kc, ni=ni, pp=pp: g.matmul(
                        ps_wo[:, pp, ni, :], wbf2[:, wb, kc * 128:(kc + 1) * 128],
                        y_sb[:, kc, ni * 512:(ni + 1) * 512], start=(kc == 0), stop=(kc == NCH - 1)),
                      reads=(f"wbf2/{wb}",) + ykeys, writes=(f"ps_wo/{pp}/{ni}",))
                E("V", lambda g, oc=oc, ni=ni, pp=pp: g.scalar_tensor_tensor(
                    out=acc[:, oc, ni * 512:(ni + 1) * 512], in0=ps_wo[:, pp, ni, :],
                    scalar=P["bo"][:, oc:oc + 1], in1=acc[:, oc, ni * 512:(ni + 1) * 512],
                    op0=ALU.add, op1=ALU.add),
                  reads=(f"ps_wo/{pp}/{ni}", "bo", f"acc/{oc}"), writes=(f"acc/{oc}",))

        acckeys = tuple(f"acc/{c}" for c in range(NCH))
        if "accwo" in tap_out:
            E(Q, lambda g: g.dma_start(out=tap_out["accwo"][:, :, :], in_=acc[:, :, :]), reads=acckeys, qsem="qout")
        ckpt("D")

        fD_ps = mem.pop("p", "ps_wo")
        fD = mem.pop("sr", "wbf2") + mem.pop("sr", "wst2")
        fy = mem.pop("sl", "y")

        hq = mem.push("sl", "hq", [128, UPCH, 512], BF16, fence_marks=fy)
        hbf = mem.push("sl", "hbf", [128, UPCH, 512], BF16, fence_marks=fy)

        xq = mem.push("sr", "xq", [128, NCH, 512], BF16, fence_marks=fD)
        sxdq = mem.push("sr", "sxdq", [128, 6, 256], F32, fence_marks=fD)
        qrow = mem.push("sr", "qrow", [128, 16], F32, fence_marks=fD)
        qr2 = mem.push("sr", "qr2", [1, 1024], F32, fence_marks=fD)

        def quant_tile(xht, nch, slots, spare, key, width, psq, pk, dq_rstd_q0, out_fn, dbg=False):
            E("V", lambda g, s0=slots[0]: g.tensor_copy(out=sxdq[:, 2, 0:width], in_=xht[:, s0, :]),
              reads=(f"{key}/{slots[0]}",), writes=("sxdq/2",))
            E("V", lambda g, s0=slots[0]: g.tensor_copy(out=sxdq[:, 4, 0:width], in_=xht[:, s0, :]),
              reads=(f"{key}/{slots[0]}",), writes=("sxdq/4",))
            for c in range(1, nch):
                s_ = slots[c]
                E("V", lambda g, s_=s_: g.tensor_tensor(
                    out=sxdq[:, 2, 0:width], in0=sxdq[:, 2, 0:width], in1=xht[:, s_, :], op=ALU.max),
                  reads=(f"{key}/{s_}", "sxdq/2"), writes=("sxdq/2",))
                E("V", lambda g, s_=s_: g.tensor_tensor(
                    out=sxdq[:, 4, 0:width], in0=sxdq[:, 4, 0:width], in1=xht[:, s_, :], op=ALU.min),
                  reads=(f"{key}/{s_}", "sxdq/4"), writes=("sxdq/4",))
            E("V", lambda g: g.tensor_scalar(out=sxdq[:, 4, 0:width], in0=sxdq[:, 4, 0:width],
                                             scalar1=-1.0, scalar2=None, op0=ALU.mult),
              reads=("sxdq/4",), writes=("sxdq/4",))
            E("V", lambda g: g.tensor_tensor(out=sxdq[:, 2, 0:width], in0=sxdq[:, 2, 0:width],
                                             in1=sxdq[:, 4, 0:width], op=ALU.max),
              reads=("sxdq/2", "sxdq/4"), writes=("sxdq/2",))
            ntp = width // 128
            for j in range(ntp):
                E("T", lambda g, j=j: g.transpose(psq[:, 0, j * 128:(j + 1) * 128],
                                                  sxdq[:, 2, j * 128:(j + 1) * 128], P["idenf"][:]),
                  reads=("sxdq/2", "idenf"), writes=(f"{pk}/a{j}",))
            E("V", lambda g, ntp=ntp: g.reduce_max(
                out=qrow[:, 0:ntp], in_=psq[:, 0, 0:ntp * 128].rearrange("p (j t) -> p j t", j=ntp),
                axis=AX.X),
              reads=tuple(f"{pk}/a{j}" for j in range(ntp)), writes=("qrow/mx",))
            if dbg:
                ckpt("E0c")
            if dq_rstd_q0 is not None:
                for j in range(ntp):
                    E("T", lambda g, j=j, q0=dq_rstd_q0: g.transpose(
                        psq[:, 1, j * 128:j * 128 + 1],
                        P["rrow"][0:1, q0 + j * 128:q0 + (j + 1) * 128], P["idenf"][0:1, 0:1]),
                      reads=("rrow", "idenf"), writes=(f"{pk}/b{j}",))
                    E("V", lambda g, j=j: g.tensor_copy(out=qrow[:, 8 + j:9 + j],
                                                        in_=psq[:, 1, j * 128:j * 128 + 1]),
                      reads=(f"{pk}/b{j}",), writes=("qrow/rst",))
                E("V", lambda g, ntp=ntp: g.reciprocal(out=qrow[:, 12:12 + ntp], in_=qrow[:, 8:8 + ntp]),
                  reads=("qrow/rst",), writes=("qrow/irst",))
                E("V", lambda g, ntp=ntp: g.tensor_scalar(out=qrow[:, 12:12 + ntp],
                                                          in0=qrow[:, 12:12 + ntp], scalar1=1e-5,
                                                          scalar2=None, op0=ALU.mult),
                  reads=("qrow/irst",), writes=("qrow/irst",))
                E("V", lambda g, ntp=ntp: g.tensor_tensor(out=qrow[:, 0:ntp], in0=qrow[:, 0:ntp],
                                                          in1=qrow[:, 12:12 + ntp], op=ALU.max),
                  reads=("qrow/mx", "qrow/irst"), writes=("qrow/mx",))
            else:
                E("V", lambda g, ntp=ntp: g.tensor_scalar(out=qrow[:, 0:ntp], in0=qrow[:, 0:ntp],
                                                          scalar1=1e-5, scalar2=None, op0=ALU.max),
                  reads=("qrow/mx",), writes=("qrow/mx",))
            E("V", lambda g, ntp=ntp: g.reciprocal(out=qrow[:, 4:4 + ntp], in_=qrow[:, 0:ntp]),
              reads=("qrow/mx",), writes=("qrow/sx",))
            E("V", lambda g, ntp=ntp: g.tensor_tensor(out=qrow[:, 12:12 + ntp],
                                                      in0=qrow[:, 4:4 + ntp], in1=qrow[:, 0:ntp],
                                                      op=ALU.mult),
              reads=("qrow/sx", "qrow/mx"), writes=("qrow/nt",))
            E("V", lambda g, ntp=ntp: g.tensor_scalar(out=qrow[:, 12:12 + ntp],
                                                      in0=qrow[:, 12:12 + ntp], scalar1=-1.0,
                                                      scalar2=2.0, op0=ALU.mult, op1=ALU.add),
              reads=("qrow/nt",), writes=("qrow/nt",))
            E("V", lambda g, ntp=ntp: g.tensor_tensor(out=qrow[:, 4:4 + ntp],
                                                      in0=qrow[:, 4:4 + ntp],
                                                      in1=qrow[:, 12:12 + ntp], op=ALU.mult),
              reads=("qrow/sx", "qrow/nt"), writes=("qrow/sx",))
            E("V", lambda g, ntp=ntp: g.tensor_scalar(out=qrow[:, 4:4 + ntp], in0=qrow[:, 4:4 + ntp],
                                                      scalar1=127.0, scalar2=None, op0=ALU.mult),
              reads=("qrow/sx",), writes=("qrow/sx",))
            sw = P["swup"] if dq_rstd_q0 is not None else P["swdn"]
            swkey = "swup" if dq_rstd_q0 is not None else "swdn"
            E("V", lambda g, ntp=ntp, sw=sw: g.tensor_scalar(out=qrow[:, 12:12 + ntp],
                                                             in0=qrow[:, 0:ntp],
                                                             scalar1=1.0 / 127.0,
                                                             scalar2=sw[:, 0:1],
                                                             op0=ALU.mult, op1=ALU.mult),
              reads=("qrow/mx", swkey), writes=("qrow/dq",))
            if dq_rstd_q0 is not None:
                E("V", lambda g, ntp=ntp: g.tensor_tensor(out=qrow[:, 12:12 + ntp],
                                                          in0=qrow[:, 12:12 + ntp],
                                                          in1=qrow[:, 8:8 + ntp], op=ALU.mult),
                  reads=("qrow/dq", "qrow/rst"), writes=("qrow/dq",))
            for j in range(ntp):
                E("T", lambda g, j=j: g.transpose(psq[0:1, 2, j * 128:(j + 1) * 128],
                                                  qrow[:, 4 + j:5 + j], P["idenf"][:]),
                  reads=("qrow/sx", "idenf"), writes=(f"{pk}/s{j}",))
                E("T", lambda g, j=j: g.transpose(psq[0:1, 3, j * 128:(j + 1) * 128],
                                                  qrow[:, 12 + j:13 + j], P["idenf"][:]),
                  reads=("qrow/dq", "idenf"), writes=(f"{pk}/d{j}",))
            E("V", lambda g: g.tensor_copy(out=qr2[0:1, 0:width], in_=psq[0:1, 2, 0:width]),
              reads=tuple(f"{pk}/s{j}" for j in range(ntp)), writes=("qr2/sx",))
            E("V", lambda g: g.tensor_copy(out=qr2[0:1, 512:512 + width], in_=psq[0:1, 3, 0:width]),
              reads=tuple(f"{pk}/d{j}" for j in range(ntp)), writes=("qr2/dq",))
            for j in range(ntp):
                E("T", lambda g, j=j: g.matmul(psq[:, 0, j * 128:(j + 1) * 128], P["onesf"][:],
                                               qr2[0:1, j * 128:(j + 1) * 128], start=True, stop=True),
                  reads=("qr2/sx", "onesf"), writes=(f"{pk}/a{j}",))
                E("T", lambda g, j=j: g.matmul(psq[:, 1, 128 + j * 128:128 + (j + 1) * 128],
                                               P["onesf"][:],
                                               qr2[0:1, 512 + j * 128:512 + (j + 1) * 128],
                                               start=True, stop=True),
                  reads=("qr2/dq", "onesf"), writes=(f"{pk}/e{j}",))
            E("V", lambda g: g.tensor_copy(out=sxdq[:, 0, 0:width], in_=psq[:, 0, 0:width]),
              reads=tuple(f"{pk}/a{j}" for j in range(ntp)), writes=("sxdq/0",))
            E("V", lambda g: g.tensor_copy(out=sxdq[:, 1, 0:width], in_=psq[:, 1, 128:128 + width]),
              reads=tuple(f"{pk}/e{j}" for j in range(ntp)), writes=("sxdq/1",))
            if dbg:
                ckpt("E0d")
            for c in range(nch):
                out_fn(c, slots[c])

        def butterflies(xht, nch, slots, spare, key, width, s0=1):
            s = s0
            while s < nch:
                pairs = [c for c in range(nch) if not (c & s)]
                for pi, c in enumerate(pairs):
                    c2 = c + s
                    eng = "V"
                    spi = 0 if eng == "V" else 1
                    sp = spare[spi]
                    a, b = slots[c], slots[c2]
                    E(eng, lambda g, a=a, b=b, sp=sp: g.tensor_add(
                        out=xht[:, sp, :], in0=xht[:, a, :], in1=xht[:, b, :]),
                      reads=(f"{key}/{a}", f"{key}/{b}"), writes=(f"{key}/{sp}",))
                    E(eng, lambda g, a=a, b=b: g.tensor_sub(
                        out=xht[:, b, :], in0=xht[:, a, :], in1=xht[:, b, :]),
                      reads=(f"{key}/{a}", f"{key}/{b}"), writes=(f"{key}/{b}",))
                    spare[spi] = a
                    slots[c] = sp
                s *= 2

        last_dn_ps = fD_ps
        last_dn_r = []
        xqkeys = tuple(f"xq/{c}/{ql}" for c in range(NCH) for ql in range(2))
        hqkeys = tuple(f"hq/{c}/{qo}" for c in range(UPCH) for qo in (0, 256))
        hbkeys = tuple(f"hbf/{c}" for c in range(UPCH))

        def emit_ffn_half(hb, last_dn_ps, last_dn_r):
            CH0 = QOFF + hb * 512

            ub2 = mem.push("sr", "ub2", [128, 2, 256], F32, fence_marks=last_dn_r)
            xh = mem.push("sr", "xh", [128, 18, 256], F32, fence_marks=last_dn_r)
            ps_x1 = mem.push("p", "ps_x1", [128, 2, 512], F32, fence_marks=last_dn_ps)
            ps_tp = mem.push("p", "ps_tp", [128, 4, 512], F32, fence_marks=last_dn_ps)

            for ql in range(2):
                T0 = CH0 + ql * 256
                slots = list(range(16))
                spare = [16, 17]
                if ql == 0 and hb == 0:
                    ckpt("D2")
                for c in range(NCH):
                    bi = c % 2
                    E(Q, lambda g, c=c, bi=bi, T0=T0: g.dma_start(
                        out=ub2[:, bi, :], in_=u_T[c * 128:(c + 1) * 128, T0:T0 + 256]),
                      writes=(f"ub2/{bi}",), qsem=f"qu{bi}")
                    if c == 0 and ql == 0 and hb == 0:
                        ckpt("D3")
                    E("T", lambda g, bi=bi: g.matmul(ps_x1[:, bi, 0:256], P["h128f"][:], ub2[:, bi, :],
                                                     start=True, stop=True),
                      reads=(f"ub2/{bi}", "h128f"), writes=(f"ps_x1/{bi}",))
                    if c == 0 and ql == 0 and hb == 0:
                        ckpt("D4")
                    E("S", lambda g, c=c, bi=bi: g.activation(out=xh[:, c, :], in_=ps_x1[:, bi, 0:256],
                                                              func=AF.Identity, bias=0.0,
                                                              scale=P["rsc2048"][:, 0:1]),
                      reads=(f"ps_x1/{bi}", "rsc2048"), writes=(f"xh/{c}",))
                    if c == 1 and ql == 0 and hb == 0:
                        ckpt("D5")
                    if c == 3 and ql == 0 and hb == 0:
                        ckpt("D6")
                    if c == 7 and ql == 0 and hb == 0:
                        ckpt("D7")
                    if c == 15 and ql == 0 and hb == 0:
                        ckpt("D8")
                if ql == 0 and hb == 0:
                    ckpt("E0a")
                butterflies(xh, 16, slots, spare, "xh", 256)

                E("V", lambda g, T0=T0: g.tensor_tensor(
                    out=qr2[0:1, 512:768], in0=P["mrow"][0:1, T0:T0 + 256],
                    in1=P["sc2048"][0:1, 0:1].broadcast_to([1, 256]), op=ALU.mult),
                  reads=("mrow", "sc2048"), writes=("qr2/corr",))
                E("V", lambda g, s0=slots[0], xh=xh: g.tensor_sub(
                    out=xh[0:1, s0, :], in0=xh[0:1, s0, :], in1=qr2[0:1, 512:768]),
                  reads=(f"xh/{slots[0]}", "qr2/corr"), writes=(f"xh/{slots[0]}",))
                if ql == 0 and hb == 0:
                    ckpt("E0b")

                def out_q(c, sl, ql=ql):
                    eng = "V"
                    sp = spare[0]
                    E(eng, lambda g, sl=sl, sp=sp: g.tensor_tensor(
                        out=xh[:, sp, :].bitcast(I32), in0=xh[:, sl, :], in1=sxdq[:, 0, :],
                        op=ALU.mult),
                      reads=(f"xh/{sl}", "sxdq/0"), writes=(f"xh/{sp}",))
                    E(eng, lambda g, c=c, sp=sp, ql=ql: g.tensor_tensor(
                        out=xq[:, c, ql * 256:(ql + 1) * 256], in0=xh[:, sp, :].bitcast(I32),
                        in1=sxdq[:, 1, :], op=ALU.mult),
                      reads=(f"xh/{sp}", "sxdq/1"), writes=(f"xq/{c}/{ql}",))

                quant_tile(xh, 16, slots, spare, "xh", 256, ps_tp, "ps_tp", T0, out_q, dbg=(ql == 0 and hb == 0))
                if ql == 0 and hb == 0:
                    ckpt("E1")

            if hb == 0:
                ckpt("EA")
            if "xq" in tap_out and hb == 0:
                E(Q, lambda g: g.dma_start(out=tap_out["xq"][:, :, :], in_=xq[:, :, :]), reads=xqkeys, qsem="qout")

            f_tp = mem.pop("p", "ps_tp")
            f_x1 = mem.pop("p", "ps_x1")
            f_xh = mem.pop("sr", "xh") + mem.pop("sr", "ub2")

            wst3 = mem.push("sr", "wst3", [128, NCH * 128], F32, fence_marks=f_xh)
            wtb = mem.push("sr", "wtb", [128, 2, NCH * 128], BF16, fence_marks=f_xh)
            sgn = mem.push("sr", "sgn", [128, NCH * 128], BF16, fence_marks=f_xh)
            rsc2 = mem.push("sr", "rsc2", [128, 2, 512], F32, fence_marks=f_xh)
            ps_up = mem.push("p", "ps_up", [128, 2, 512], F32, fence_marks=f_x1 + f_tp)

            for oc in range(UPCH):
                wb = oc % 2
                if hb == 0:
                    E(Q, lambda g, oc=oc: g.dma_start(out=wst3[:, :], in_=wup[oc, :, :]), writes=("wst3",), qsem="qw")
                    E("V", lambda g: g.tensor_scalar(out=sgn[:, :], in0=wst3[:, :],
                                                     scalar1=P["nthrup"][:, 0:1], scalar2=None,
                                                     op0=ALU.is_lt),
                      reads=("wst3", "nthrup"), writes=("sgn",))
                    E("V", lambda g, wb=wb: g.tensor_scalar(out=wtb[:, wb, :], in0=wst3[:, :],
                                                            scalar1=P["thrup"][:, 0:1], scalar2=None,
                                                            op0=ALU.is_gt),
                      reads=("wst3", "thrup"), writes=(f"wtb/{wb}",))
                    E("V", lambda g, wb=wb: g.tensor_sub(out=wtb[:, wb, :], in0=wtb[:, wb, :],
                                                         in1=sgn[:, :]),
                      reads=("sgn", f"wtb/{wb}"), writes=(f"wtb/{wb}",))
                    E(Q, lambda g, oc=oc, wb=wb: g.dma_start(out=wup_cache[oc, :, :], in_=wtb[:, wb, :]),
                      reads=(f"wtb/{wb}",), writes=(f"wupc/{oc}",), qsem="qc")
                else:
                    E(Q, lambda g, oc=oc, wb=wb: g.dma_start(out=wtb[:, wb, :], in_=wup_cache[oc, :, :]),
                      reads=(f"wupc/{oc}",), writes=(f"wtb/{wb}",), qsem=f"qt{wb}")
                pp = oc % 2
                for ql in range(2):
                    for kc in range(NCH):
                        E("T", lambda g, wb=wb, kc=kc, pp=pp, ql=ql: g.matmul(
                            ps_up[:, pp, ql * 256:(ql + 1) * 256],
                            wtb[:, wb, kc * 128:(kc + 1) * 128],
                            xq[:, kc, ql * 256:(ql + 1) * 256],
                            start=(kc == 0), stop=(kc == NCH - 1)),
                          reads=(f"wtb/{wb}", f"xq/{kc}/{ql}"), writes=(f"ps_up/{pp}",))
                E("S", lambda g, pp=pp: g.activation(out=rsc2[:, pp, :], in_=ps_up[:, pp, :],
                                                     func=AF.Relu, bias=0.0, scale=1.0),
                  reads=(f"ps_up/{pp}",), writes=(f"rsc2/{pp}",))
                E("V", lambda g, oc=oc, pp=pp: g.tensor_mul(out=hbf[:, oc, :], in0=rsc2[:, pp, :],
                                                            in1=rsc2[:, pp, :]),
                  reads=(f"rsc2/{pp}",), writes=(f"hbf/{oc}",))

            if hb == 0:
                trk.settle("qc", tuple(f"wupc/{oc}" for oc in range(UPCH)))
                ckpt("EB")
            if "h" in tap_out and hb == 0:
                E(Q, lambda g: g.dma_start(out=tap_out["h"][:, :, :], in_=hbf[:, :, :]), reads=hbkeys, qsem="qout")

            f_up = mem.pop("p", "ps_up")
            f_s2 = mem.pop("sr", "rsc2") + mem.pop("sr", "sgn") + mem.pop("sr", "wtb") + \
                mem.pop("sr", "wst3")

            xh2 = mem.push("sr", "xh2", [128, 34, 256], F32, fence_marks=f_s2)
            ps_x2 = mem.push("p", "ps_x2", [128, 2, 512], F32, fence_marks=f_up)
            ps_t2 = mem.push("p", "ps_t2", [128, 4, 512], F32, fence_marks=f_up)

            for ql in range(2):
                QO = ql * 256
                slots2 = list(range(32))
                spare2 = [32, 33]
                for c in range(UPCH):
                    bi = c % 2
                    g8 = (c // 8) * 8
                    for j2 in range(8):
                        neg8 = bin((c % 8) & j2).count("1") & 1
                        wt = "h128nbf" if neg8 else "h128bf"
                        E("T", lambda g, bi=bi, QO=QO, wt=wt, j2=j2, src=g8 + j2: g.matmul(
                            ps_x2[:, bi, 0:256], P[wt][:], hbf[:, src, QO:QO + 256],
                            start=(j2 == 0), stop=(j2 == 7)),
                          reads=(f"hbf/{src}", wt), writes=(f"ps_x2/{bi}",))
                    E("S", lambda g, c=c, bi=bi: g.activation(out=xh2[:, c, :], in_=ps_x2[:, bi, 0:256],
                                                              func=AF.Copy, bias=0.0, scale=0.015625),
                      reads=(f"ps_x2/{bi}",), writes=(f"xh2/{c}",))

                butterflies(xh2, 32, slots2, spare2, "xh2", 256, s0=8)

                def out_h(c, sl, ql=ql):
                    eng = "V"
                    sp = spare2[0]
                    E(eng, lambda g, sl=sl, sp=sp: g.tensor_tensor(
                        out=xh2[:, sp, :].bitcast(I32), in0=xh2[:, sl, :], in1=sxdq[:, 0, :],
                        op=ALU.mult),
                      reads=(f"xh2/{sl}", "sxdq/0"), writes=(f"xh2/{sp}",))
                    E(eng, lambda g, c=c, sp=sp, ql=ql: g.tensor_tensor(
                        out=hq[:, c, ql * 256:(ql + 1) * 256], in0=xh2[:, sp, :].bitcast(I32),
                        in1=sxdq[:, 1, :], op=ALU.mult),
                      reads=(f"xh2/{sp}", "sxdq/1"), writes=(f"hq/{c}/{ql * 256}",))

                quant_tile(xh2, 32, slots2, spare2, "xh2", 256, ps_t2, "ps_t2", None, out_h)
                if ql == 0 and hb == 0:
                    ckpt("E2")

            if hb == 0:
                ckpt("EC")
            if "hq" in tap_out and hb == 0:
                E(Q, lambda g: g.dma_start(out=tap_out["hq"][:, :, :], in_=hq[:, :, :]), reads=hqkeys, qsem="qout")

            f_t2 = mem.pop("p", "ps_t2")
            f_x2 = mem.pop("p", "ps_x2")
            f_xh2 = mem.pop("sr", "xh2")

            wst4 = mem.push("sr", "wst4", [128, UPCH * 128], F32, fence_marks=f_xh2)
            wtb2 = mem.push("sr", "wtb2", [128, 2, UPCH * 128], BF16, fence_marks=f_xh2)
            sgn2 = mem.push("sr", "sgn2", [128, UPCH * 128], BF16, fence_marks=f_xh2)
            ps_dn = mem.push("p", "ps_dn", [128, 2, 512], F32, fence_marks=f_x2 + f_t2)

            for oc in range(NCH):
                wb = oc % 2
                if hb == 0:
                    E(Q, lambda g, oc=oc: g.dma_start(out=wst4[:, :], in_=wdn[oc, :, :]), writes=("wst4",), qsem="qw")
                    E("V", lambda g: g.tensor_scalar(out=sgn2[:, :], in0=wst4[:, :],
                                                     scalar1=P["nthrdn"][:, 0:1], scalar2=None,
                                                     op0=ALU.is_lt),
                      reads=("wst4", "nthrdn"), writes=("sgn2",))
                    E("V", lambda g, wb=wb: g.tensor_scalar(out=wtb2[:, wb, :], in0=wst4[:, :],
                                                            scalar1=P["thrdn"][:, 0:1], scalar2=None,
                                                            op0=ALU.is_gt),
                      reads=("wst4", "thrdn"), writes=(f"wtb2/{wb}",))
                    E("V", lambda g, wb=wb: g.tensor_sub(out=wtb2[:, wb, :], in0=wtb2[:, wb, :],
                                                         in1=sgn2[:, :]),
                      reads=("sgn2", f"wtb2/{wb}"), writes=(f"wtb2/{wb}",))
                    E(Q, lambda g, oc=oc, wb=wb: g.dma_start(out=wdn_cache[oc, :, :], in_=wtb2[:, wb, :]),
                      reads=(f"wtb2/{wb}",), writes=(f"wdnc/{oc}",), qsem="qc")
                else:
                    E(Q, lambda g, oc=oc, wb=wb: g.dma_start(out=wtb2[:, wb, :], in_=wdn_cache[oc, :, :]),
                      reads=(f"wdnc/{oc}",), writes=(f"wtb2/{wb}",), qsem=f"qt{wb}")
                pp = oc % 2
                for ql in range(2):
                    for kc in range(UPCH):
                        E("T", lambda g, wb=wb, kc=kc, pp=pp, ql=ql: g.matmul(
                            ps_dn[:, pp, ql * 256:(ql + 1) * 256],
                            wtb2[:, wb, kc * 128:(kc + 1) * 128],
                            hq[:, kc, ql * 256:(ql + 1) * 256],
                            start=(kc == 0), stop=(kc == UPCH - 1)),
                          reads=(f"wtb2/{wb}", f"hq/{kc}/{ql * 256}"), writes=(f"ps_dn/{pp}",))
                E("V", lambda g, oc=oc, pp=pp, hb=hb: g.tensor_add(
                    out=acc[:, oc, hb * 512:(hb + 1) * 512], in0=acc[:, oc, hb * 512:(hb + 1) * 512],
                    in1=ps_dn[:, pp, :]),
                  reads=(f"ps_dn/{pp}", f"acc/{oc}"), writes=(f"acc/{oc}", f"accf/{oc}/{hb}"))

            if hb == 0:
                trk.settle("qc", tuple(f"wdnc/{oc}" for oc in range(NCH)))
                ckpt("ED")
            last_dn_ps = mem.pop("p", "ps_dn")
            last_dn_r = mem.pop("sr", "sgn2") + mem.pop("sr", "wtb2") + mem.pop("sr", "wst4")
            return last_dn_ps, last_dn_r

        for hb in range(2):
            last_dn_ps, last_dn_r = emit_ffn_half(hb, last_dn_ps, last_dn_r)

        for c in range(NCH):
            E(Q, lambda g, c=c: g.dma_start(out=out_T[c * 128:(c + 1) * 128, :], in_=acc[:, c, :]),
              reads=(f"acc/{c}", f"accf/{c}/0", f"accf/{c}/1"), qsem="qout")


    except _Stop:
        pass


    es = contextlib.ExitStack()
    for e in ENGS:
        trk.sems[e] = es.enter_context(nc.semaphore(f"sem{e}"))
    for qn in trk.qcnt:
        trk.sems[qn] = es.enter_context(nc.semaphore(f"sem_{qn}"))
    qout_total = trk.qcnt.get("qout", 0)
    with es, nc.Block() as block:
        @block.sync
        def _(eng):
            for f in trk.prog["Q"]:
                f(eng)
            if qout_total:
                eng.wait_ge(trk.sems["qout"], qout_total)

        @block.tensor
        def _(eng):
            for f in trk.prog["T"]:
                f(eng)

        @block.scalar
        def _(eng):
            for f in trk.prog["S"]:
                f(eng)

        @block.vector
        def _(eng):
            for f in trk.prog["V"]:
                f(eng)

        @block.gpsimd
        def _(eng):
            for f in trk.prog["P"]:
                f(eng)

    return nc, tap_out


_CACHE = {}


def _get_program(taps=()):
    key = tuple(sorted(taps))
    if key not in _CACHE:
        _CACHE[key] = build_program(taps)
    return _CACHE[key]


def host_inputs(L0_attn_u, L0_attn_Wqkv, L0_attn_bqkv, L0_attn_Wo, L0_attn_bo,
                L0_attn_sinks, L0_ffn_w_up, L0_ffn_w_down):
    wqkv_s = _strips(np.asarray(L0_attn_Wqkv, np.float32), QKV_OC, NCH)
    wo_s = _strips(np.asarray(L0_attn_Wo, np.float32), NCH, NCH)
    wup_s = _strips(np.asarray(L0_ffn_w_up, np.float32), UPCH, NCH)
    wdn_s = _strips(np.asarray(L0_ffn_w_down, np.float32), NCH, UPCH)
    bq_t = np.ascontiguousarray(np.asarray(L0_attn_bqkv, np.float32).reshape(QKV_OC, 128).T)
    bo_t = np.ascontiguousarray(np.asarray(L0_attn_bo, np.float32).reshape(NCH, 128).T)
    sinks = np.asarray(L0_attn_sinks, np.float32)
    s4 = sinks.reshape(16, 2, NS, DH)
    sinkT = np.ascontiguousarray(s4.transpose(1, 3, 0, 2).reshape(128, 16, NS))
    sinkV = np.ascontiguousarray(sinks.transpose(1, 0, 2).reshape(NS, NH * DH))

    h128 = _hadamard128()
    consts = np.zeros((128, 8), np.float32)
    for j, v in enumerate([0.0, 1.0, 1e-5, -0.5, 2048.0, 0.015625]):
        consts[:, j] = v

    shared = dict(
        wqkv=wqkv_s, bqkv_t=bq_t, wo=wo_s, bo_t=bo_t, wup=wup_s, wdn=wdn_s,
        sinkT=sinkT, sinkV=sinkV,
        h128bf=h128.astype(ml_dtypes.bfloat16), h128f=h128,
        idenbf=np.eye(128).astype(ml_dtypes.bfloat16), idenf=np.eye(128, dtype=np.float32),
        pswap=_pswap().astype(ml_dtypes.bfloat16),
        consts=consts, onesf=np.ones((1, 128), np.float32), onesc=np.ones((128, 1), np.float32),
    )

    u = np.asarray(L0_attn_u, np.float32)
    in_maps = []
    for core in range(8):
        b, half = core // 2, core % 2
        qstart = half * 1024
        u_Tc = np.zeros((D, TKV), np.float32)
        lo, hi = qstart - 64, qstart - 64 + TKV
        slo, shi = max(lo, 0), min(hi, L)
        u_Tc[:, slo - lo:shi - lo] = u[b, slo:shi, :].T
        cq, sq, ck, sk = _trig_tables(qstart)
        m0, m1 = _masks(qstart)
        m = dict(shared)
        m.update(u_T=np.ascontiguousarray(u_Tc), cq=cq, sq=sq, ck=ck, sk=sk,
                 mask0=m0, mask1=m1)
        in_maps.append(m)
    return in_maps


def kernel(**inputs):
    nc, _ = _get_program()
    in_maps = host_inputs(**inputs)
    res = run_bass_kernel_spmd(nc, in_maps, core_ids=list(range(8)))
    out = np.empty((B, L, D), np.float32)
    for core in range(8):
        b, half = core // 2, core % 2
        out[b, half * 1024:(half + 1) * 1024, :] = np.asarray(res.results[core]["out_T"]).T
    return out


# version 08
# speedup vs baseline: 1.7889x; 1.0012x over previous
import contextlib
import numpy as np
import ml_dtypes
import concourse.bass as bass
import concourse.mybir as mybir
from concourse.bass_utils import run_bass_kernel_spmd

F32 = mybir.dt.float32
BF16 = mybir.dt.bfloat16
I32 = mybir.dt.int32
AF = mybir.ActivationFunctionType
ALU = mybir.AluOpType
AX = mybir.AxisListType

B, L, D = 4, 2048, 2048
NCH = D // 128
TKV = 1152
TQ = 1024
QOFF = 64
NH, DH, NS = 32, 64, 16
UP = 4096
UPCH = UP // 128
QKV_OC = 3 * NCH
NEG = -1.0e30
NT_KV = ((0, 512), (512, 512), (1024, 128))
NT_Q = ((64, 512), (576, 512))

ENGS = ("T", "S", "V", "P", "Q")


class Trk:
    def __init__(self):
        self.prog = {e: [] for e in ENGS}
        self.cnt = {e: 0 for e in ENGS}
        self.qcnt = {}
        self.writer = {}
        self.readers = {}
        self.waited = {e: {} for e in ENGS}
        self.pfence = {}
        self.sems = {}
        import os
        self.log = open("/tmp/trk.log", "w") if os.environ.get("TRKLOG") else None

    def _add_wait(self, e, oe, val):
        if val <= 0:
            return
        if self.waited[e].get(oe, 0) >= val:
            return
        self.waited[e][oe] = val
        sems = self.sems
        self.prog[e].append(lambda eng, oe=oe, val=val: eng.wait_ge(sems[oe], val))

    def emit(self, e, fn, reads=(), writes=(), tag="", qsem=None):
        need = {}
        for r in set(reads) | set(writes):
            p = r.split("/", 1)[0]
            marks = self.pfence.get(p)
            if marks:
                for (oe, c) in marks:
                    if oe != e:
                        need[oe] = max(need.get(oe, 0),
                                       self.qcnt[oe] if oe in self.qcnt else c)
        skipT = (e == "T")

        def mval(oe, c):
            return self.qcnt[oe] if oe in self.qcnt else c

        for r in reads:
            w = self.writer.get(r)
            if w is not None and not (skipT and w[0] == "T"):
                need[w[0]] = max(need.get(w[0], 0), mval(*w))
        for r in writes:
            w = self.writer.get(r)
            if w is not None and not (skipT and w[0] == "T"):
                need[w[0]] = max(need.get(w[0], 0), mval(*w))
            for oe, c in self.readers.get(r, {}).items():
                if not (skipT and oe == "T"):
                    need[oe] = max(need.get(oe, 0), mval(oe, c))
        for oe, v in need.items():
            self._add_wait(e, oe, v)
        if e == "Q":
            assert qsem is not None
            self.qcnt[qsem] = self.qcnt.get(qsem, 0) + 16
            me = self.qcnt[qsem]
            mark = (qsem, me)
            sems = self.sems
            self.prog["Q"].append(lambda eng, fn=fn, qsem=qsem: fn(eng).then_inc(sems[qsem], 16))
        else:
            self.cnt[e] += 1
            me = self.cnt[e]
            mark = (e, me)
            sems = self.sems
            self.prog[e].append(lambda eng, fn=fn, e=e: fn(eng).then_inc(sems[e], 1))
        if self.log is not None:
            self.log.write(f"{e}/{qsem} cnt={me} waits={need} reads={reads} writes={writes} {tag}\n")
        for r in writes:
            self.writer[r] = mark
            self.readers[r] = {}
        for r in reads:
            self.readers.setdefault(r, {})[mark[0]] = me
        return mark

    def settle(self, qsem, regions):
        total = self.qcnt.get(qsem, 0)
        for r in regions:
            w = self.writer.get(r)
            if w is not None and w[0] == qsem:
                self.writer[r] = (qsem, total)

    def marks_for(self, prefix):
        out = []
        for r, w in list(self.writer.items()):
            if r == prefix or r.startswith(prefix + "/"):
                if w is not None:
                    out.append(w)
                for oe, c in self.readers.get(r, {}).items():
                    out.append((oe, c))
        return out

    def clear_prefix(self, prefix):
        self.pfence.pop(prefix, None)
        for r in list(self.writer.keys()):
            if r == prefix or r.startswith(prefix + "/"):
                del self.writer[r]
                self.readers.pop(r, None)

    def fence(self, region, marks):
        self.pfence[region] = list(marks)


class Mem:
    def __init__(self, nc, trk):
        self.nc = nc
        self.trk = trk
        self.stack = {"sl": [], "sr": [], "p": []}
        self.serial = 0

    def push(self, which, name, shape, dtype, fence_marks=None):
        self.serial += 1
        tname = f"{name}__{self.serial}"
        if which == "p":
            g = self.nc.psum_tensor(tname, shape, dtype)
        else:
            g = self.nc.sbuf_tensor(tname, shape, dtype,
                                    side=("left" if which == "sl" else "right"))
        h = g.__enter__()
        self.stack[which].append((name, g))
        if fence_marks:
            self.trk.fence(name, fence_marks)
        return h

    def pop(self, which, name):
        top, g = self.stack[which].pop()
        assert top == name, f"LIFO violation: pop {name} but top is {top}"
        marks = self.trk.marks_for(name)
        self.trk.clear_prefix(name)
        g.__exit__(None, None, None)
        return marks


def _hadamard128():
    i = np.arange(128)
    pc = np.bitwise_count(i[:, None] & i[None, :]) if hasattr(np, "bitwise_count") else \
        np.array([bin(a & b).count("1") for a in i for b in i]).reshape(128, 128)
    return np.where(pc % 2 == 0, np.float32(1.0), np.float32(-1.0)).astype(np.float32)


def _pswap():
    p = np.zeros((128, 128), np.float32)
    for m in range(128):
        k = m + 32 if (m % 64) < 32 else m - 32
        p[k, m] = 1.0
    return p


def _trig_tables(qstart):
    import jax
    import jax.numpy as jnp
    cpu = jax.devices("cpu")[0]
    with jax.default_device(cpu):
        half = 32
        inv_freq = 1.0 / 10000.0 ** (jnp.arange(half, dtype=jnp.float32) / half)

        def tables(p0, n, scale):
            pos = jnp.arange(p0, p0 + n, dtype=jnp.float32)
            ang = pos[None, :] * inv_freq[:, None]
            c = np.asarray(jnp.cos(ang), np.float32)
            s = np.asarray(jnp.sin(ang), np.float32)
            C = np.tile(c, (4, 1)) * scale
            S = np.tile(s, (4, 1)) * scale
            sign = np.where((np.arange(128) % 64) < 32, np.float32(-1.0), np.float32(1.0))
            return np.ascontiguousarray(C), np.ascontiguousarray(S * sign[:, None])

        ck, sk = tables(qstart - 64, TKV, np.float32(1.0))
        cq, sq = tables(qstart, TQ, np.float32(0.125))
    return cq, sq, ck, sk


def _masks(qstart):
    out = []
    for g in range(2):
        m = np.zeros((128, 4, 272), np.float32)
        for i in range(4):
            qb = g * 4 + i
            q = np.arange(128)[:, None]
            jw = np.arange(256)[None, :]
            valid = (jw >= q) & (jw <= q + 128)
            kpos = qstart - 64 + qb * 128 + jw
            valid = valid & (kpos >= 0) & (kpos < L)
            m[:, i, 16:] = np.where(valid, np.float32(0.0), np.float32(NEG))
        out.append(np.ascontiguousarray(m.reshape(128, 4 * 272)).astype(ml_dtypes.bfloat16))
    return out


def _strips(w, n_oc, n_kc):
    return np.ascontiguousarray(
        w.reshape(n_oc, 128, n_kc, 128).transpose(0, 3, 2, 1).reshape(n_oc, 128, n_kc * 128))


class _Stop(Exception):
    pass


def build_program(taps=(), stop_after=None):
    nc = bass.Bass("TRN2", target_bir_lowering=False)
    trk = Trk()
    mem = Mem(nc, trk)
    E = trk.emit
    Q = "Q"

    dI = lambda n, s, dt=F32: nc.dram_tensor(n, s, dt, kind="ExternalInput").ap()
    dO = lambda n, s, dt=F32: nc.dram_tensor(n, s, dt, kind="ExternalOutput").ap()

    u_T = dI("u_T", [D, TKV])
    wqkv = dI("wqkv", [QKV_OC, 128, NCH * 128])
    bqkv = dI("bqkv_t", [128, QKV_OC])
    wo = dI("wo", [NCH, 128, NCH * 128])
    bo_in = dI("bo_t", [128, NCH])
    wup = dI("wup", [UPCH, 128, NCH * 128])
    wdn = dI("wdn", [NCH, 128, UPCH * 128])
    sinkT_in = dI("sinkT", [128, NCH, NS])
    sinkV_in = dI("sinkV", [NS, NH * DH])
    cq_in = dI("cq", [128, TQ])
    sq_in = dI("sq", [128, TQ])
    ck_in = dI("ck", [128, TKV])
    sk_in = dI("sk", [128, TKV])
    mask0_in = dI("mask0", [128, 4 * 272], BF16)
    mask1_in = dI("mask1", [128, 4 * 272], BF16)
    h128_in = dI("h128bf", [128, 128], BF16)
    h128f_in = dI("h128f", [128, 128])
    iden_in = dI("idenbf", [128, 128], BF16)
    idenf_in = dI("idenf", [128, 128])
    pswap_in = dI("pswap", [128, 128], BF16)
    consts_in = dI("consts", [128, 8])
    onesf_in = dI("onesf", [1, 128])
    onesc_in = dI("onesc", [128, 1])

    out_T = dO("out_T", [D, TQ])

    wup_cache = nc.dram_tensor("wup_cache", [UPCH, 128, NCH * 128], BF16, kind="Internal").ap()
    wdn_cache = nc.dram_tensor("wdn_cache", [NCH, 128, UPCH * 128], BF16, kind="Internal").ap()

    tapset = set(taps)
    tap_out = {}
    for nm, shape, dt in [("xn", [128, NCH, TKV], BF16), ("q", [128, NCH, TQ], BF16),
                          ("k", [128, NCH, TKV], BF16), ("v", [128, 9, NH * DH], BF16),
                          ("y", [128, NCH, TQ], BF16), ("accwo", [128, NCH, TQ], F32),
                          ("xq", [128, NCH, 512], BF16), ("h", [128, UPCH, 512], BF16),
                          ("hq", [128, UPCH, 512], BF16), ("sw", [128, 2], F32),
                          ("pe", [128, 4, 272], F32), ("pe2", [128, 4, 272], F32),
                          ("pb", [128, 4, 272], BF16), ("pt", [128, 4, 384], BF16),
                          ("ysu", [64, 512], BF16),
                          ("sr0", [128, 16], F32), ("mj", [128, 8], F32),
                          ("rows", [1, TKV * 2], F32)]:
        if nm in tapset:
            tap_out[nm] = dO("tap_" + nm, shape, dt)

    def ckpt(ph):
        if stop_after == ph:
            raise _Stop()

    P = {}
    for nm, shape, dt in [
        ("consts", [128, 8], F32), ("onesc", [128, 1], F32),
        ("onesbf", [128, 1], BF16), ("idenbf", [128, 128], BF16), ("idenf", [128, 128], F32),
        ("pswap", [128, 128], BF16), ("h128bf", [128, 128], BF16), ("h128f", [128, 128], F32),
        ("h128nbf", [128, 128], BF16),
        ("bq", [128, QKV_OC], F32), ("bo", [128, NCH], F32),
        ("rsc2048", [128, 1], F32),
        ("swup", [128, 1], F32), ("thrup", [128, 1], F32), ("nthrup", [128, 1], F32),
        ("swdn", [128, 1], F32), ("thrdn", [128, 1], F32), ("nthrdn", [128, 1], F32),
        ("wacc", [128, 2], F32), ("wsums", [128, 64], F32),
        ("onesf", [1, 128], F32), ("mrow", [1, TKV], F32), ("rrow", [1, TKV], F32),
        ("sc2048", [1, 1], F32), ("ln2048", [1, 1], F32),
    ]:
        P[nm] = mem.push("sl", "sb_" + nm, shape, dt)

    for val, col in [(0.0, 0), (1.0, 1), (1e-5, 2), (-0.5, 3), (2048.0, 4), (0.015625, 5), (0.5, 6)]:
        nc.const_aps.register(F32, val, P["consts"][:, col:col + 1])

    try:
        for nm, src in [("consts", consts_in), ("onesf", onesf_in), ("onesc", onesc_in),
                        ("idenbf", iden_in), ("idenf", idenf_in), ("pswap", pswap_in),
                        ("h128bf", h128_in), ("h128f", h128f_in), ("bq", bqkv), ("bo", bo_in)]:
            E(Q, lambda g, d=P[nm], s=src: g.dma_start(out=d[:], in_=s[:, :]), writes=(nm,), qsem="qx")
        trk.settle("qx", ("consts", "onesf", "onesc", "idenbf", "idenf", "pswap",
                          "h128bf", "h128f", "bq", "bo"))
        E("V", lambda g: g.tensor_copy(out=P["onesbf"][:], in_=P["onesc"][:]),
          reads=("onesc",), writes=("onesbf",))
        E("V", lambda g: g.tensor_scalar(out=P["h128nbf"][:], in0=P["h128bf"][:],
                                         scalar1=-1.0, scalar2=None, op0=ALU.mult),
          reads=("h128bf",), writes=("h128nbf",))
        E("S", lambda g: g.activation(out=P["ln2048"][:], in_=P["consts"][0:1, 4:5], func=AF.Ln,
                                      bias=0.0, scale=1.0), reads=("consts",), writes=("ln2048",))
        E("S", lambda g: g.activation(out=P["sc2048"][:], in_=P["ln2048"][:], func=AF.Exp,
                                      bias=0.0, scale=0.5), reads=("ln2048",), writes=("sc2048",))

        xn = mem.push("sl", "xn", [128, NCH, TKV], BF16)

        ubuf = mem.push("sr", "ubuf", [128, 2, TKV], F32)
        sqb = mem.push("sr", "sqb", [128, 2, TKV], BF16)
        ubfb = mem.push("sr", "ubfb", [128, 2, TKV], BF16)
        arow = mem.push("sr", "arow", [1, 4 * TKV], F32)
        RB = mem.push("sr", "RB", [128, 2, TKV], F32)

        ps_st1 = mem.push("p", "ps_st1", [1, TKV], F32)
        ps_st2 = mem.push("p", "ps_st2", [1, TKV], F32)

        for c in range(NCH):
            bi = c % 2
            E(Q, lambda g, c=c, bi=bi: g.dma_start(out=ubuf[:, bi, :], in_=u_T[c * 128:(c + 1) * 128, :]),
              writes=(f"ubuf/{bi}",), qsem=f"qu{bi}")
            E("V", lambda g, bi=bi: g.tensor_copy(out=ubfb[:, bi, :], in_=ubuf[:, bi, :]),
              reads=(f"ubuf/{bi}",), writes=(f"ubfb/{bi}",))
            E("V", lambda g, bi=bi: g.tensor_mul(out=sqb[:, bi, :], in0=ubfb[:, bi, :], in1=ubfb[:, bi, :]),
              reads=(f"ubfb/{bi}",), writes=(f"sqb/{bi}",))
            for (o, n) in NT_KV:
                E("T", lambda g, c=c, bi=bi, o=o, n=n: g.matmul(
                    ps_st1[:, o:o + n], P["onesbf"][:], ubfb[:, bi, o:o + n],
                    start=(c == 0), stop=(c == NCH - 1)),
                  reads=(f"ubfb/{bi}", "onesbf"), writes=(f"ps_st1/{o}",))
                E("T", lambda g, c=c, bi=bi, o=o, n=n: g.matmul(
                    ps_st2[:, o:o + n], P["onesbf"][:], sqb[:, bi, o:o + n],
                    start=(c == 0), stop=(c == NCH - 1)),
                  reads=(f"sqb/{bi}", "onesbf"), writes=(f"ps_st2/{o}",))

        mr, rr = P["mrow"], P["rrow"]
        e2 = arow[0:1, 0:TKV]
        var = arow[0:1, TKV:2 * TKV]
        lnv = arow[0:1, 2 * TKV:3 * TKV]
        brow = arow[0:1, 3 * TKV:4 * TKV]
        st1keys = tuple(f"ps_st1/{o}" for (o, n) in NT_KV)
        st2keys = tuple(f"ps_st2/{o}" for (o, n) in NT_KV)
        E("V", lambda g: g.tensor_scalar(out=mr[:], in0=ps_st1[:, :], scalar1=1.0 / 2048.0,
                                         scalar2=None, op0=ALU.mult), reads=st1keys, writes=("mrow",))
        E("V", lambda g: g.tensor_scalar(out=e2, in0=ps_st2[:, :], scalar1=1.0 / 2048.0,
                                         scalar2=None, op0=ALU.mult), reads=st2keys, writes=("arow/e2",))
        E("V", lambda g: g.tensor_mul(out=var, in0=mr[:], in1=mr[:]), reads=("mrow",), writes=("arow/var",))
        E("V", lambda g: g.tensor_sub(out=var, in0=e2, in1=var),
          reads=("arow/e2", "arow/var"), writes=("arow/var",))
        E("S", lambda g: g.activation(out=lnv, in_=var, func=AF.Ln, bias=1e-5, scale=1.0),
          reads=("arow/var",), writes=("arow/lnv",))
        E("S", lambda g: g.activation(out=rr[:], in_=lnv, func=AF.Exp, bias=0.0, scale=-0.5),
          reads=("arow/lnv",), writes=("rrow",))
        E("V", lambda g: g.tensor_mul(out=brow, in0=mr[:], in1=rr[:]),
          reads=("mrow", "rrow"), writes=("arow/brow",))
        E("V", lambda g: g.tensor_scalar(out=brow, in0=brow, scalar1=-1.0, scalar2=None, op0=ALU.mult),
          reads=("arow/brow",), writes=("arow/brow",))
        if "rows" in tap_out:
            E(Q, lambda g: g.dma_start(out=tap_out["rows"][0:1, 0:TKV], in_=mr[:]), reads=("mrow",), qsem="qout")
            E(Q, lambda g: g.dma_start(out=tap_out["rows"][0:1, TKV:2 * TKV], in_=rr[:]), reads=("rrow",), qsem="qout")
        E("S", lambda g: g.activation(out=arow[0:1, 0:1], in_=P["ln2048"][:], func=AF.Exp,
                                      bias=0.0, scale=-0.5),
          reads=("ln2048",), writes=("arow/rsc",))

        f_st2 = mem.pop("p", "ps_st2")
        f_st1 = mem.pop("p", "ps_st1")
        ps_rb = mem.push("p", "ps_rb", [128, 2, 1536], F32, fence_marks=f_st1 + f_st2)
        for (o, n) in NT_KV:
            E("T", lambda g, o=o, n=n: g.matmul(ps_rb[:, 0, o:o + n], P["onesf"][:], rr[0:1, o:o + n],
                                                start=True, stop=True),
              reads=("rrow", "onesf"), writes=(f"ps_rb/0/{o}",))
            E("T", lambda g, o=o, n=n: g.matmul(ps_rb[:, 1, o:o + n], P["onesf"][:],
                                                arow[0:1, 3 * TKV + o:3 * TKV + o + n],
                                                start=True, stop=True),
              reads=("arow/brow", "onesf"), writes=(f"ps_rb/1/{o}",))
        E("T", lambda g: g.matmul(ps_rb[:, 0, TKV:TKV + 1], P["onesf"][:], arow[0:1, 0:1],
                                  start=True, stop=True),
          reads=("arow/rsc", "onesf"), writes=("ps_rb/rsc",))
        E("V", lambda g: g.tensor_copy(out=P["rsc2048"][:], in_=ps_rb[:, 0, TKV:TKV + 1]),
          reads=("ps_rb/rsc",), writes=("rsc2048",))
        rbk0 = tuple(f"ps_rb/0/{o}" for (o, n) in NT_KV)
        rbk1 = tuple(f"ps_rb/1/{o}" for (o, n) in NT_KV)
        E("V", lambda g: g.tensor_copy(out=RB[:, 0, :], in_=ps_rb[:, 0, 0:TKV]), reads=rbk0, writes=("RB/0",))
        E("V", lambda g: g.tensor_copy(out=RB[:, 1, :], in_=ps_rb[:, 1, 0:TKV]), reads=rbk1, writes=("RB/1",))

        for c in range(NCH):
            bi = c % 2
            E(Q, lambda g, c=c, bi=bi: g.dma_start(out=ubuf[:, bi, :], in_=u_T[c * 128:(c + 1) * 128, :]),
              writes=(f"ubuf/{bi}",), qsem=f"qu{bi}")
            eng = "V"
            E(eng, lambda g, bi=bi: g.tensor_mul(out=ubuf[:, bi, :], in0=ubuf[:, bi, :], in1=RB[:, 0, :]),
              reads=(f"ubuf/{bi}", "RB/0"), writes=(f"ubuf/{bi}",))
            E(eng, lambda g, c=c, bi=bi: g.tensor_add(out=xn[:, c, :], in0=ubuf[:, bi, :], in1=RB[:, 1, :]),
              reads=(f"ubuf/{bi}", "RB/1"), writes=(f"xn/{c}",))

        xnkeys = tuple(f"xn/{c}" for c in range(NCH))
        if "xn" in tap_out:
            E(Q, lambda g: g.dma_start(out=tap_out["xn"][:, :, :], in_=xn[:, :, :]), reads=xnkeys, qsem="qout")
        ckpt("A")

        f_rb = mem.pop("p", "ps_rb")
        fA = []
        for nm in ("RB", "arow", "ubfb", "sqb", "ubuf"):
            fA += mem.pop("sr", nm)

        q_sb = mem.push("sr", "q", [128, NCH, TQ], BF16, fence_marks=fA)
        k_sb = mem.push("sr", "k", [128, NCH, TKV], BF16, fence_marks=fA)
        v_tm = mem.push("sr", "V", [128, 9, NH * DH], BF16, fence_marks=fA)
        wst = mem.push("sr", "wst", [128, NCH * 128], F32, fence_marks=fA)
        wbf = mem.push("sr", "wbf", [128, 2, NCH * 128], BF16, fence_marks=fA)
        vbf = mem.push("sr", "vbf", [128, 2, 512], BF16, fence_marks=fA)

        trig = mem.push("sl", "trig", [128, 2, TKV], F32)
        rsc = mem.push("sl", "rsc", [128, 2, 512], F32)

        ps_qkv = mem.push("p", "ps_qkv", [128, 2, 3, 512], F32, fence_marks=f_rb)
        ps_sw = mem.push("p", "ps_sw", [128, 512], F32, fence_marks=f_rb)
        ps_vt = mem.push("p", "ps_vt", [128, 1024], BF16, fence_marks=f_rb)

        def cvt(e, dst, src, rkey, wkey):
            if e == "S":
                E(e, lambda g: g.activation(out=dst, in_=src, func=AF.Copy, bias=0.0, scale=1.0),
                  reads=(rkey,) if isinstance(rkey, str) else rkey,
                  writes=(wkey,) if isinstance(wkey, str) else wkey)
            else:
                E(e, lambda g: g.tensor_copy(out=dst, in_=src),
                  reads=(rkey,) if isinstance(rkey, str) else rkey,
                  writes=(wkey,) if isinstance(wkey, str) else wkey)

        E(Q, lambda g: g.dma_start(out=trig[:, 0, 0:TQ], in_=cq_in[:, :]), writes=("trig/c",), qsem="qtr")
        E(Q, lambda g: g.dma_start(out=trig[:, 1, 0:TQ], in_=sq_in[:, :]), writes=("trig/s",), qsem="qtr")
        trk.settle("qtr", ("trig/c", "trig/s"))

        def rope_chunk(c, kind):
            tens = q_sb if kind == "q" else k_sb
            tiles = NT_Q if kind == "q" else NT_KV
            toff = QOFF if kind == "q" else 0
            for (o, n) in tiles:
                co = o - toff
                E("T", lambda g, c=c, o=o, n=n, co=co, tens=tens: g.matmul(
                    ps_sw[:, 0:n], P["pswap"][:], tens[:, c, co:co + n], start=True, stop=True),
                  reads=(f"{kind}/n/{c}", "pswap"), writes=("ps_sw",))
                E("V", lambda g, n=n, co=co: g.tensor_tensor(
                    out=rsc[:, 0, 0:n], in0=ps_sw[:, 0:n], in1=trig[:, 1, co:co + n], op=ALU.mult),
                  reads=("ps_sw", "trig/s"), writes=("rsc/m",))
                E("V", lambda g, c=c, n=n, co=co, tens=tens: g.tensor_tensor(
                    out=rsc[:, 1, 0:n], in0=tens[:, c, co:co + n], in1=trig[:, 0, co:co + n], op=ALU.mult),
                  reads=(f"{kind}/n/{c}", "trig/c"), writes=("rsc/t1",))
                E("V", lambda g, c=c, n=n, co=co, tens=tens: g.tensor_add(
                    out=tens[:, c, co:co + n], in0=rsc[:, 1, 0:n], in1=rsc[:, 0, 0:n]),
                  reads=("rsc/t1", "rsc/m"), writes=(f"{kind}/{c}",))

        conv_eng = ("S", "V", "S")
        for oc in range(QKV_OC):
            wb = oc % 2
            E(Q, lambda g, oc=oc: g.dma_start(out=wst[:, :], in_=wqkv[oc, :, :]), writes=("wst",), qsem="qw")
            cvt(conv_eng[oc % 3], wbf[:, wb, :], wst[:, :], "wst", f"wbf/{wb}")
            kind = "q" if oc < NCH else ("k" if oc < 2 * NCH else "v")
            c = oc % NCH
            tiles = NT_Q if kind == "q" else NT_KV
            pp = oc % 2
            for ni, (o, n) in enumerate(tiles):
                for kc in range(NCH):
                    E("T", lambda g, wb=wb, kc=kc, pp=pp, ni=ni, o=o, n=n: g.matmul(
                        ps_qkv[:, pp, ni, 0:n], wbf[:, wb, kc * 128:(kc + 1) * 128],
                        xn[:, kc, o:o + n], start=(kc == 0), stop=(kc == NCH - 1)),
                      reads=(f"wbf/{wb}", f"xn/{kc}"), writes=(f"ps_qkv/{pp}/{ni}",))
            if kind in ("q", "k"):
                tens = q_sb if kind == "q" else k_sb
                toff = QOFF if kind == "q" else 0
                for ni, (o, n) in enumerate(tiles):
                    E("S", lambda g, oc=oc, c=c, pp=pp, ni=ni, o=o, n=n, toff=toff, tens=tens: g.activation(
                        out=tens[:, c, o - toff:o - toff + n], in_=ps_qkv[:, pp, ni, 0:n],
                        func=AF.Identity, bias=P["bq"][:, oc:oc + 1], scale=1.0),
                      reads=(f"ps_qkv/{pp}/{ni}", "bq"), writes=(f"{kind}/n/{c}",))
                rope_chunk(c, kind)
                if kind == "q" and c == NCH - 1:
                    E(Q, lambda g: g.dma_start(out=trig[:, 0, :], in_=ck_in[:, :]), writes=("trig/c",), qsem="qtr")
                    E(Q, lambda g: g.dma_start(out=trig[:, 1, :], in_=sk_in[:, :]), writes=("trig/s",), qsem="qtr")
                    trk.settle("qtr", ("trig/c", "trig/s"))
            else:
                vc = oc - 2 * NCH
                for ni, (o, n) in enumerate(tiles):
                    vb = ni % 2
                    E("S", lambda g, oc=oc, pp=pp, ni=ni, n=n, vb=vb: g.activation(
                        out=vbf[:, vb, 0:n], in_=ps_qkv[:, pp, ni, 0:n],
                        func=AF.Identity, bias=P["bq"][:, oc:oc + 1], scale=1.0),
                      reads=(f"ps_qkv/{pp}/{ni}", "bq"), writes=(f"vbf/{vb}",))
                    ntp = n // 128
                    for j in range(ntp):
                        E("T", lambda g, vb=vb, j=j: g.transpose(
                            ps_vt[:, j * 128:(j + 1) * 128], vbf[:, vb, j * 128:(j + 1) * 128],
                            P["idenbf"][:]),
                          reads=(f"vbf/{vb}", "idenbf"), writes=(f"ps_vt/{j}",))
                    jt0 = o // 128
                    E("V", lambda g, vc=vc, jt0=jt0, n=n, ntp=ntp: g.tensor_copy(
                        out=v_tm[:, jt0:jt0 + ntp, vc * 128:(vc + 1) * 128],
                        in_=ps_vt[:, 0:n].rearrange("p (j f) -> p j f", f=128)),
                      reads=tuple(f"ps_vt/{j}" for j in range(ntp)), writes=(f"V/{vc}",))

        qkeys = tuple(f"q/{c}" for c in range(NCH))
        kkeys = tuple(f"k/{c}" for c in range(NCH))
        vkeys = tuple(f"V/{vc}" for vc in range(NCH))
        if "q" in tap_out:
            E(Q, lambda g: g.dma_start(out=tap_out["q"][:, :, :], in_=q_sb[:, :, :]), reads=qkeys, qsem="qout")
        if "k" in tap_out:
            E(Q, lambda g: g.dma_start(out=tap_out["k"][:, :, :], in_=k_sb[:, :, :]), reads=kkeys, qsem="qout")
        if "v" in tap_out:
            E(Q, lambda g: g.dma_start(out=tap_out["v"][:, :, :], in_=v_tm[:, :, :]), reads=vkeys, qsem="qout")
        ckpt("B")

        fB_ps = []
        for nm in ("ps_vt", "ps_sw", "ps_qkv"):
            fB_ps += mem.pop("p", nm)
        fB_l = mem.pop("sl", "rsc") + mem.pop("sl", "trig")
        fB_l += mem.pop("sl", "xn")
        y_sb = mem.push("sl", "y", [128, NCH, TQ], BF16, fence_marks=fB_l)
        fB_r = []
        for nm in ("vbf", "wbf", "wst"):
            fB_r += mem.pop("sr", nm)

        masks = mem.push("sr", "masks", [128, 2, 4 * 272], BF16, fence_marks=fB_r)
        sbf = mem.push("sr", "sbf", [128, NCH, NS], BF16, fence_marks=fB_r)
        svbf = mem.push("sr", "svbf", [NS, NH * DH], BF16, fence_marks=fB_r)
        stf = mem.push("sr", "stf", [128, NCH, NS], F32, fence_marks=fB_r)
        pe_sc = mem.push("sr", "pe", [128, 2, 4, 272], F32, fence_marks=fB_r)
        pe2_sc = mem.push("sr", "pe2", [128, 2, 4, 272], F32, fence_marks=fB_r)
        pb_sc = mem.push("sr", "pb", [128, 2, 4, 272], BF16, fence_marks=fB_r)
        pt_sb = mem.push("sr", "pt", [128, 2, 4, 384], BF16, fence_marks=fB_r)
        srow = mem.push("sr", "srow", [128, 2, 16], F32, fence_marks=fB_r)
        mst = mem.push("sr", "mst", [128, 2, 2048], F32, fence_marks=fB_r)

        E(Q, lambda g: g.dma_start(out=masks[:, 0, :], in_=mask0_in[:, :]), writes=("masks/0",), qsem="qx2")
        E(Q, lambda g: g.dma_start(out=masks[:, 1, :], in_=mask1_in[:, :]), writes=("masks/1",), qsem="qx2")
        E(Q, lambda g: g.dma_start(out=stf[:, :, :], in_=sinkT_in[:, :, :]), writes=("stf",), qsem="qx2")
        E(Q, lambda g: g.dma_start(out=mst[0:NS, 0, :], in_=sinkV_in[:, 0:2048]), writes=("mst/0",), qsem="qx2")
        trk.settle("qx2", ("masks/0", "masks/1", "stf", "mst/0"))
        E("V", lambda g: g.tensor_copy(out=sbf[:, :, :], in_=stf[:, :, :]), reads=("stf",), writes=("sbf",))
        E("V", lambda g: g.tensor_copy(out=svbf[:, :], in_=mst[0:NS, 0, :]), reads=("mst/0",), writes=("svbf",))

        ps_S = mem.push("p", "ps_S", [128, 4, 512], F32, fence_marks=fB_ps)
        ps_PT = mem.push("p", "ps_PT", [128, 4, 512], BF16, fence_marks=fB_ps)
        E("S", lambda g: g.memzero(ps_PT[:, :, :]),
          writes=tuple(f"ps_PT/{i}" for i in range(4)))
        ps_Y = mem.push("p", "ps_Y", [64, 2, 512], F32, fence_marks=fB_ps)

        mean_jobs = [("up", oc, 0) for oc in range(UPCH)] + \
                    [("dn", oc, off) for oc in range(NCH) for off in (0, 2048)]

        def emit_mean(idx):
            which, oc, off = mean_jobs[idx]
            bi = idx % 2
            src = wup[oc, :, :] if which == "up" else wdn[oc, :, off:off + 2048]
            wcol = 0 if which == "up" else 1
            E(Q, lambda g, src=src, bi=bi: g.dma_start(out=mst[:, bi, :], in_=src), writes=(f"mst/{bi}",), qsem=f"qm{bi}")
            E("V", lambda g, bi=bi, idx=idx: g.reduce_sum(out=P["wsums"][:, idx:idx + 1],
                                                          in_=mst[:, bi, :], axis=AX.X,
                                                          apply_absolute_value=True),
              reads=(f"mst/{bi}",), writes=(f"wsums/{idx}",))

        def su(h, g2):
            hp = (h % 2) * 64
            ch = h // 2
            vch = (h * DH) // 128
            pp = (h * 2 + g2) % 2
            for i in range(4):
                qb = g2 * 4 + i
                E("T", lambda g, hp=hp, ch=ch, qb=qb, i=i: g.matmul(
                    ps_S[:, i, 0:16], q_sb[hp:hp + 64, ch, qb * 128:(qb + 1) * 128],
                    sbf[hp:hp + 64, ch, :], start=True, stop=True),
                  reads=(f"q/{ch}", "sbf"), writes=(f"ps_S/{i}",))
                E("T", lambda g, hp=hp, ch=ch, qb=qb, i=i: g.matmul(
                    ps_S[:, i, 16:272], q_sb[hp:hp + 64, ch, qb * 128:(qb + 1) * 128],
                    k_sb[hp:hp + 64, ch, qb * 128:qb * 128 + 256], start=True, stop=True),
                  reads=(f"q/{ch}", f"k/{ch}"), writes=(f"ps_S/{i}",))
                E("V", lambda g, i=i, g2=g2, pp=pp: g.scalar_tensor_tensor(
                    out=pe_sc[:, pp, i, :], in0=ps_S[:, i, 0:272], scalar=1.0,
                    in1=masks[:, g2, i * 272:(i + 1) * 272], op0=ALU.mult, op1=ALU.add),
                  reads=(f"ps_S/{i}", f"masks/{g2}"), writes=(f"pe/{pp}/{i}",))
            E("V", lambda g, pp=pp: g.reduce_max(out=srow[:, pp, 0:4], in_=pe_sc[:, pp, :, :],
                                                 axis=AX.X, negate=True),
              reads=tuple(f"pe/{pp}/{i}" for i in range(4)), writes=(f"srow/{pp}/negmx",))
            for i in range(4):
                E("S", lambda g, pp=pp, i=i: g.activation(
                    out=pe_sc[:, pp, i, :], in_=pe_sc[:, pp, i, :], func=AF.Exp,
                    bias=srow[:, pp, i:i + 1], scale=1.0, accum_out=srow[:, pp, 4 + i:5 + i]),
                  reads=(f"pe/{pp}/{i}", f"srow/{pp}/negmx"),
                  writes=(f"pe/{pp}/{i}", f"srow/{pp}/sum",))
            E("V", lambda g, pp=pp: g.reciprocal(out=srow[:, pp, 8:12], in_=srow[:, pp, 4:8]),
              reads=(f"srow/{pp}/sum",), writes=(f"srow/{pp}/rec",))
            for i in range(4):
                eng = "S"
                if eng == "S":
                    E(eng, lambda g, pp=pp, i=i: g.activation(
                        out=pb_sc[:, pp, i, :], in_=pe_sc[:, pp, i, :], func=AF.Identity,
                        bias=0.0, scale=srow[:, pp, 8 + i:9 + i]),
                      reads=(f"pe/{pp}/{i}", f"srow/{pp}/rec"), writes=(f"pb/{pp}/{i}",))
                else:
                    E(eng, lambda g, pp=pp, i=i: g.tensor_scalar(
                        out=pb_sc[:, pp, i, :], in0=pe_sc[:, pp, i, :],
                        scalar1=srow[:, pp, 8 + i:9 + i], scalar2=None, op0=ALU.mult),
                      reads=(f"pe/{pp}/{i}", f"srow/{pp}/rec"), writes=(f"pb/{pp}/{i}",))
                E("T", lambda g, pp=pp, i=i: g.transpose(
                    ps_PT[:, i, 0:128], pb_sc[:, pp, i, 16:144], P["idenbf"][:]),
                  reads=(f"pb/{pp}/{i}", "idenbf"), writes=(f"ps_PT/{i}",))
                E("T", lambda g, pp=pp, i=i: g.transpose(
                    ps_PT[:, i, 128:256], pb_sc[:, pp, i, 144:272], P["idenbf"][:]),
                  reads=(f"pb/{pp}/{i}",), writes=(f"ps_PT/{i}",))
                E("T", lambda g, pp=pp, i=i: g.transpose(
                    ps_PT[0:16, i, 256:384], pb_sc[:, pp, i, 0:16], P["idenbf"][:]),
                  reads=(f"pb/{pp}/{i}",), writes=(f"ps_PT/{i}",))
            for half in range(2):
                E("V", lambda g, pp=pp, half=half: g.tensor_copy(
                    out=pt_sb[:, pp, 2 * half:2 * half + 2, :],
                    in_=ps_PT[:, 2 * half:2 * half + 2, 0:384]),
                  reads=(f"ps_PT/{2 * half}", f"ps_PT/{2 * half + 1}"),
                  writes=(f"pt/{pp}/{2 * half}", f"pt/{pp}/{2 * half + 1}"))
            yp = (h * 2 + g2) % 2
            for i in range(4):
                qb = g2 * 4 + i
                E("T", lambda g, pp=pp, i=i, h=h: g.matmul(
                    ps_Y[:, yp, i * 128:(i + 1) * 128], svbf[:, h * 64:(h + 1) * 64],
                    pt_sb[0:16, pp, i, 256:384], start=True, stop=False),
                  reads=(f"pt/{pp}/{i}", "svbf"), writes=(f"ps_Y/{yp}",))
                E("T", lambda g, pp=pp, i=i, h=h, qb=qb: g.matmul(
                    ps_Y[:, yp, i * 128:(i + 1) * 128], v_tm[:, qb, h * 64:(h + 1) * 64],
                    pt_sb[:, pp, i, 0:128], start=False, stop=False),
                  reads=(f"pt/{pp}/{i}", f"V/{vch}"), writes=(f"ps_Y/{yp}",))
                E("T", lambda g, pp=pp, i=i, h=h, qb=qb: g.matmul(
                    ps_Y[:, yp, i * 128:(i + 1) * 128], v_tm[:, qb + 1, h * 64:(h + 1) * 64],
                    pt_sb[:, pp, i, 128:256], start=False, stop=True),
                  reads=(f"pt/{pp}/{i}", f"V/{vch}"), writes=(f"ps_Y/{yp}",))
            E("S", lambda g, h=h, g2=g2, yp=yp, hp=hp, ch=ch: g.activation(
                out=y_sb[hp:hp + 64, ch, g2 * 512:(g2 + 1) * 512], in_=ps_Y[:, yp, :],
                func=AF.Copy, bias=0.0, scale=1.0),
              reads=(f"ps_Y/{yp}",), writes=(f"y/{h}/{g2}",))

        import os as _os
        _dsu = int(_os.environ.get("DSU", "0"))
        tap_wacc_final = None
        if "pe" in tap_out:
            su(_dsu, 0)
            E(Q, lambda g: g.dma_start(out=tap_out["pe"][:, :, :], in_=pe_sc[:, 0, :, :]),
              reads=tuple(f"pe/0/{i}" for i in range(4)), qsem="qout")
            E(Q, lambda g: g.dma_start(out=tap_out["pe2"][:, :, :], in_=pe2_sc[:, 0, :, :]),
              reads=tuple(f"pe2/0/{i}" for i in range(4)), qsem="qout")
            E(Q, lambda g: g.dma_start(out=tap_out["sr0"][:, :], in_=srow[:, 0, :]),
              reads=("srow/0/negmx", "srow/0/sum", "srow/0/rec"), qsem="qout")
            if "pb" in tap_out:
                E(Q, lambda g: g.dma_start(out=tap_out["pb"][:, :, :], in_=pb_sc[:, 0, :, :]),
                  reads=tuple(f"pb/0/{i}" for i in range(4)), qsem="qout")
                E(Q, lambda g: g.dma_start(out=tap_out["pt"][:, :, :], in_=pt_sb[:, 0, :, :]),
                  reads=tuple(f"pt/0/{i}" for i in range(4)), qsem="qout")
                E(Q, lambda g, hp=(_dsu % 2) * 64, ch=_dsu // 2: g.dma_start(
                    out=tap_out["ysu"][:, :], in_=y_sb[hp:hp + 64, ch, 0:512]),
                  reads=(f"y/{_dsu}/0",), qsem="qout")
        if "mj" in tap_out:
            _mjdone = [False]
            def _tap_wacc_final():
                E(Q, lambda g: g.dma_start(out=tap_out["mj"][:, 2:4], in_=P["wacc"][:, 0:2]),
                  reads=("wacc",), qsem="qout")
            tap_wacc_final = _tap_wacc_final
        else:
            tap_wacc_final = None
        _nosu = bool(_os.environ.get("DBG_NOSU"))
        mean_i = 0
        for h in range(NH):
            for g2 in range(2):
                if not _nosu and not ("pe" in tap_out and h == _dsu and g2 == 0):
                    su(h, g2)
                want = ((h * 2 + g2 + 1) * len(mean_jobs)) // 64
                while mean_i < min(want, len(mean_jobs)):
                    emit_mean(mean_i)
                    mean_i += 1
        while mean_i < len(mean_jobs):
            emit_mean(mean_i)
            mean_i += 1

        ykeys = tuple(f"y/{h}/{g2}" for h in range(NH) for g2 in range(2))
        if "y" in tap_out:
            E(Q, lambda g: g.dma_start(out=tap_out["y"][:, :, :], in_=y_sb[:, :, :]), reads=ykeys, qsem="qout")

        fC_ps = []
        for nm in ("ps_Y", "ps_PT", "ps_S"):
            fC_ps += mem.pop("p", nm)
        E("V", lambda g: g.reduce_sum(out=P["wacc"][:, 0:1], in_=P["wsums"][:, 0:32], axis=AX.X),
          reads=tuple(f"wsums/{i}" for i in range(32)), writes=("wacc",))
        E("V", lambda g: g.reduce_sum(out=P["wacc"][:, 1:2], in_=P["wsums"][:, 32:64], axis=AX.X),
          reads=tuple(f"wsums/{i}" for i in range(32, 64)), writes=("wacc",))
        ps_w = mem.push("p", "ps_w", [1, 2], F32, fence_marks=fC_ps)
        E("T", lambda g: g.matmul(ps_w[:, 0:1], P["wacc"][:, 0:1], P["onesc"][:], start=True, stop=True),
          reads=("wacc", "onesc"), writes=("ps_w/0",))
        E("T", lambda g: g.matmul(ps_w[:, 1:2], P["wacc"][:, 1:2], P["onesc"][:], start=True, stop=True),
          reads=("wacc", "onesc"), writes=("ps_w/1",))
        E("V", lambda g: g.tensor_copy(out=srow[0:1, 0, 0:2], in_=ps_w[:, 0:2]),
          reads=("ps_w/0", "ps_w/1"), writes=("srow/swr",))
        ps_wb = mem.push("p", "ps_wb", [128, 4], F32)
        E("T", lambda g: g.matmul(ps_wb[:, 0:2], P["onesf"][:], srow[0:1, 0, 0:2], start=True, stop=True),
          reads=("srow/swr", "onesf"), writes=("ps_wb",))
        for nm, col in (("swup", 0), ("swdn", 1)):
            E("V", lambda g, nm=nm, col=col: g.tensor_scalar(
                out=P[nm][:], in0=ps_wb[:, col:col + 1], scalar1=1.0 / (UP * D), scalar2=1e-5,
                op0=ALU.mult, op1=ALU.max),
              reads=("ps_wb",), writes=(nm,))
        for nm, sw, f in (("thrup", "swup", 0.5), ("thrdn", "swdn", 0.5),
                          ("nthrup", "swup", -0.5), ("nthrdn", "swdn", -0.5)):
            E("V", lambda g, nm=nm, sw=sw, f=f: g.tensor_scalar(out=P[nm][:], in0=P[sw][:], scalar1=f,
                                                                scalar2=None, op0=ALU.mult),
              reads=(sw,), writes=(nm,))
        if tap_wacc_final is not None:
            E(Q, lambda g: g.dma_start(out=tap_out["mj"][:, 0:2], in_=P["wsums"][:, 0:2]),
              reads=("wsums/0", "wsums/1"), qsem="qout")
            E(Q, lambda g: g.dma_start(out=tap_out["mj"][:, 2:4], in_=P["wacc"][:, 0:2]),
              reads=("wacc",), qsem="qout")
            E(Q, lambda g: g.dma_start(out=tap_out["mj"][0:1, 4:6], in_=srow[0:1, 0, 0:2]),
              reads=("srow/swr",), qsem="qout")
        if "sw" in tap_out:
            E("V", lambda g: g.tensor_copy(out=P["wacc"][:, 0:1], in_=P["swup"][:, :]),
              reads=("swup",), writes=("wacc",))
            E("V", lambda g: g.tensor_copy(out=P["wacc"][:, 1:2], in_=P["swdn"][:, :]),
              reads=("swdn",), writes=("wacc",))
            E(Q, lambda g: g.dma_start(out=tap_out["sw"][:, :], in_=P["wacc"][:, :]),
              reads=("wacc",), qsem="qout")
        fC_ps += mem.pop("p", "ps_wb")
        fC_ps += mem.pop("p", "ps_w")
        ckpt("C")

        fC = []
        for nm in ("mst", "srow", "pt", "pb", "pe2", "pe", "stf", "svbf", "sbf", "masks",
                   "V", "k", "q"):
            fC += mem.pop("sr", nm)

        acc = mem.push("sr", "acc", [128, NCH, TQ], F32, fence_marks=fC)
        wst2 = mem.push("sr", "wst2", [128, NCH * 128], F32, fence_marks=fC)
        wbf2 = mem.push("sr", "wbf2", [128, 2, NCH * 128], BF16, fence_marks=fC)
        ps_wo = mem.push("p", "ps_wo", [128, 2, 2, 512], F32, fence_marks=fC_ps)

        for c in range(NCH):
            E(Q, lambda g, c=c: g.dma_start(out=acc[:, c, :], in_=u_T[c * 128:(c + 1) * 128, QOFF:QOFF + TQ]),
              writes=(f"acc/{c}",), qsem="qa")
        trk.settle("qa", tuple(f"acc/{c}" for c in range(NCH)))

        for oc in range(NCH):
            wb = oc % 2
            E(Q, lambda g, oc=oc: g.dma_start(out=wst2[:, :], in_=wo[oc, :, :]), writes=("wst2",), qsem="qw")
            cvt(conv_eng[oc % 3], wbf2[:, wb, :], wst2[:, :], "wst2", f"wbf2/{wb}")
            pp = oc % 2
            for ni in range(2):
                for kc in range(NCH):
                    E("T", lambda g, wb=wb, kc=kc, ni=ni, pp=pp: g.matmul(
                        ps_wo[:, pp, ni, :], wbf2[:, wb, kc * 128:(kc + 1) * 128],
                        y_sb[:, kc, ni * 512:(ni + 1) * 512], start=(kc == 0), stop=(kc == NCH - 1)),
                      reads=(f"wbf2/{wb}",) + ykeys, writes=(f"ps_wo/{pp}/{ni}",))
                E("V", lambda g, oc=oc, ni=ni, pp=pp: g.scalar_tensor_tensor(
                    out=acc[:, oc, ni * 512:(ni + 1) * 512], in0=ps_wo[:, pp, ni, :],
                    scalar=P["bo"][:, oc:oc + 1], in1=acc[:, oc, ni * 512:(ni + 1) * 512],
                    op0=ALU.add, op1=ALU.add),
                  reads=(f"ps_wo/{pp}/{ni}", "bo", f"acc/{oc}"), writes=(f"acc/{oc}",))

        acckeys = tuple(f"acc/{c}" for c in range(NCH))
        if "accwo" in tap_out:
            E(Q, lambda g: g.dma_start(out=tap_out["accwo"][:, :, :], in_=acc[:, :, :]), reads=acckeys, qsem="qout")
        ckpt("D")

        fD_ps = mem.pop("p", "ps_wo")
        fD = mem.pop("sr", "wbf2") + mem.pop("sr", "wst2")
        fy = mem.pop("sl", "y")

        hq = mem.push("sl", "hq", [128, UPCH, 512], BF16, fence_marks=fy)
        hbf = mem.push("sl", "hbf", [128, UPCH, 512], BF16, fence_marks=fy)

        xq = mem.push("sr", "xq", [128, NCH, 512], BF16, fence_marks=fD)
        sxdq = mem.push("sr", "sxdq", [128, 6, 256], F32, fence_marks=fD)
        qrow = mem.push("sr", "qrow", [128, 16], F32, fence_marks=fD)
        qr2 = mem.push("sr", "qr2", [1, 1024], F32, fence_marks=fD)

        def quant_tile(xht, nch, slots, spare, key, width, psq, pk, dq_rstd_q0, out_fn, dbg=False):
            E("V", lambda g, s0=slots[0]: g.tensor_copy(out=sxdq[:, 2, 0:width], in_=xht[:, s0, :]),
              reads=(f"{key}/{slots[0]}",), writes=("sxdq/2",))
            E("V", lambda g, s0=slots[0]: g.tensor_copy(out=sxdq[:, 4, 0:width], in_=xht[:, s0, :]),
              reads=(f"{key}/{slots[0]}",), writes=("sxdq/4",))
            for c in range(1, nch):
                s_ = slots[c]
                E("V", lambda g, s_=s_: g.tensor_tensor(
                    out=sxdq[:, 2, 0:width], in0=sxdq[:, 2, 0:width], in1=xht[:, s_, :], op=ALU.max),
                  reads=(f"{key}/{s_}", "sxdq/2"), writes=("sxdq/2",))
                E("V", lambda g, s_=s_: g.tensor_tensor(
                    out=sxdq[:, 4, 0:width], in0=sxdq[:, 4, 0:width], in1=xht[:, s_, :], op=ALU.min),
                  reads=(f"{key}/{s_}", "sxdq/4"), writes=("sxdq/4",))
            E("V", lambda g: g.tensor_scalar(out=sxdq[:, 4, 0:width], in0=sxdq[:, 4, 0:width],
                                             scalar1=-1.0, scalar2=None, op0=ALU.mult),
              reads=("sxdq/4",), writes=("sxdq/4",))
            E("V", lambda g: g.tensor_tensor(out=sxdq[:, 2, 0:width], in0=sxdq[:, 2, 0:width],
                                             in1=sxdq[:, 4, 0:width], op=ALU.max),
              reads=("sxdq/2", "sxdq/4"), writes=("sxdq/2",))
            ntp = width // 128
            for j in range(ntp):
                E("T", lambda g, j=j: g.transpose(psq[:, 0, j * 128:(j + 1) * 128],
                                                  sxdq[:, 2, j * 128:(j + 1) * 128], P["idenf"][:]),
                  reads=("sxdq/2", "idenf"), writes=(f"{pk}/a{j}",))
            E("V", lambda g, ntp=ntp: g.reduce_max(
                out=qrow[:, 0:ntp], in_=psq[:, 0, 0:ntp * 128].rearrange("p (j t) -> p j t", j=ntp),
                axis=AX.X),
              reads=tuple(f"{pk}/a{j}" for j in range(ntp)), writes=("qrow/mx",))
            if dbg:
                ckpt("E0c")
            if dq_rstd_q0 is not None:
                for j in range(ntp):
                    E("T", lambda g, j=j, q0=dq_rstd_q0: g.transpose(
                        psq[:, 1, j * 128:j * 128 + 1],
                        P["rrow"][0:1, q0 + j * 128:q0 + (j + 1) * 128], P["idenf"][0:1, 0:1]),
                      reads=("rrow", "idenf"), writes=(f"{pk}/b{j}",))
                    E("V", lambda g, j=j: g.tensor_copy(out=qrow[:, 8 + j:9 + j],
                                                        in_=psq[:, 1, j * 128:j * 128 + 1]),
                      reads=(f"{pk}/b{j}",), writes=("qrow/rst",))
                E("V", lambda g, ntp=ntp: g.reciprocal(out=qrow[:, 12:12 + ntp], in_=qrow[:, 8:8 + ntp]),
                  reads=("qrow/rst",), writes=("qrow/irst",))
                E("V", lambda g, ntp=ntp: g.tensor_scalar(out=qrow[:, 12:12 + ntp],
                                                          in0=qrow[:, 12:12 + ntp], scalar1=1e-5,
                                                          scalar2=None, op0=ALU.mult),
                  reads=("qrow/irst",), writes=("qrow/irst",))
                E("V", lambda g, ntp=ntp: g.tensor_tensor(out=qrow[:, 0:ntp], in0=qrow[:, 0:ntp],
                                                          in1=qrow[:, 12:12 + ntp], op=ALU.max),
                  reads=("qrow/mx", "qrow/irst"), writes=("qrow/mx",))
            else:
                E("V", lambda g, ntp=ntp: g.tensor_scalar(out=qrow[:, 0:ntp], in0=qrow[:, 0:ntp],
                                                          scalar1=1e-5, scalar2=None, op0=ALU.max),
                  reads=("qrow/mx",), writes=("qrow/mx",))
            E("V", lambda g, ntp=ntp: g.reciprocal(out=qrow[:, 4:4 + ntp], in_=qrow[:, 0:ntp]),
              reads=("qrow/mx",), writes=("qrow/sx",))
            E("V", lambda g, ntp=ntp: g.tensor_tensor(out=qrow[:, 12:12 + ntp],
                                                      in0=qrow[:, 4:4 + ntp], in1=qrow[:, 0:ntp],
                                                      op=ALU.mult),
              reads=("qrow/sx", "qrow/mx"), writes=("qrow/nt",))
            E("V", lambda g, ntp=ntp: g.tensor_scalar(out=qrow[:, 12:12 + ntp],
                                                      in0=qrow[:, 12:12 + ntp], scalar1=-1.0,
                                                      scalar2=2.0, op0=ALU.mult, op1=ALU.add),
              reads=("qrow/nt",), writes=("qrow/nt",))
            E("V", lambda g, ntp=ntp: g.tensor_tensor(out=qrow[:, 4:4 + ntp],
                                                      in0=qrow[:, 4:4 + ntp],
                                                      in1=qrow[:, 12:12 + ntp], op=ALU.mult),
              reads=("qrow/sx", "qrow/nt"), writes=("qrow/sx",))
            E("V", lambda g, ntp=ntp: g.tensor_scalar(out=qrow[:, 4:4 + ntp], in0=qrow[:, 4:4 + ntp],
                                                      scalar1=127.0, scalar2=None, op0=ALU.mult),
              reads=("qrow/sx",), writes=("qrow/sx",))
            sw = P["swup"] if dq_rstd_q0 is not None else P["swdn"]
            swkey = "swup" if dq_rstd_q0 is not None else "swdn"
            E("V", lambda g, ntp=ntp, sw=sw: g.tensor_scalar(out=qrow[:, 12:12 + ntp],
                                                             in0=qrow[:, 0:ntp],
                                                             scalar1=1.0 / 127.0,
                                                             scalar2=sw[:, 0:1],
                                                             op0=ALU.mult, op1=ALU.mult),
              reads=("qrow/mx", swkey), writes=("qrow/dq",))
            if dq_rstd_q0 is not None:
                E("V", lambda g, ntp=ntp: g.tensor_tensor(out=qrow[:, 12:12 + ntp],
                                                          in0=qrow[:, 12:12 + ntp],
                                                          in1=qrow[:, 8:8 + ntp], op=ALU.mult),
                  reads=("qrow/dq", "qrow/rst"), writes=("qrow/dq",))
            for j in range(ntp):
                E("T", lambda g, j=j: g.transpose(psq[0:1, 2, j * 128:(j + 1) * 128],
                                                  qrow[:, 4 + j:5 + j], P["idenf"][:]),
                  reads=("qrow/sx", "idenf"), writes=(f"{pk}/s{j}",))
                E("T", lambda g, j=j: g.transpose(psq[0:1, 3, j * 128:(j + 1) * 128],
                                                  qrow[:, 12 + j:13 + j], P["idenf"][:]),
                  reads=("qrow/dq", "idenf"), writes=(f"{pk}/d{j}",))
            E("V", lambda g: g.tensor_copy(out=qr2[0:1, 0:width], in_=psq[0:1, 2, 0:width]),
              reads=tuple(f"{pk}/s{j}" for j in range(ntp)), writes=("qr2/sx",))
            E("V", lambda g: g.tensor_copy(out=qr2[0:1, 512:512 + width], in_=psq[0:1, 3, 0:width]),
              reads=tuple(f"{pk}/d{j}" for j in range(ntp)), writes=("qr2/dq",))
            for j in range(ntp):
                E("T", lambda g, j=j: g.matmul(psq[:, 0, j * 128:(j + 1) * 128], P["onesf"][:],
                                               qr2[0:1, j * 128:(j + 1) * 128], start=True, stop=True),
                  reads=("qr2/sx", "onesf"), writes=(f"{pk}/a{j}",))
                E("T", lambda g, j=j: g.matmul(psq[:, 1, 128 + j * 128:128 + (j + 1) * 128],
                                               P["onesf"][:],
                                               qr2[0:1, 512 + j * 128:512 + (j + 1) * 128],
                                               start=True, stop=True),
                  reads=("qr2/dq", "onesf"), writes=(f"{pk}/e{j}",))
            E("V", lambda g: g.tensor_copy(out=sxdq[:, 0, 0:width], in_=psq[:, 0, 0:width]),
              reads=tuple(f"{pk}/a{j}" for j in range(ntp)), writes=("sxdq/0",))
            E("V", lambda g: g.tensor_copy(out=sxdq[:, 1, 0:width], in_=psq[:, 1, 128:128 + width]),
              reads=tuple(f"{pk}/e{j}" for j in range(ntp)), writes=("sxdq/1",))
            if dbg:
                ckpt("E0d")
            for c in range(nch):
                out_fn(c, slots[c])

        def butterflies(xht, nch, slots, spare, key, width, s0=1):
            s = s0
            while s < nch:
                pairs = [c for c in range(nch) if not (c & s)]
                for pi, c in enumerate(pairs):
                    c2 = c + s
                    eng = "V"
                    spi = 0 if eng == "V" else 1
                    sp = spare[spi]
                    a, b = slots[c], slots[c2]
                    E(eng, lambda g, a=a, b=b, sp=sp: g.tensor_add(
                        out=xht[:, sp, :], in0=xht[:, a, :], in1=xht[:, b, :]),
                      reads=(f"{key}/{a}", f"{key}/{b}"), writes=(f"{key}/{sp}",))
                    E(eng, lambda g, a=a, b=b: g.tensor_sub(
                        out=xht[:, b, :], in0=xht[:, a, :], in1=xht[:, b, :]),
                      reads=(f"{key}/{a}", f"{key}/{b}"), writes=(f"{key}/{b}",))
                    spare[spi] = a
                    slots[c] = sp
                s *= 2

        last_dn_ps = fD_ps
        last_dn_r = []
        xqkeys = tuple(f"xq/{c}/{ql}" for c in range(NCH) for ql in range(2))
        hqkeys = tuple(f"hq/{c}/{qo}" for c in range(UPCH) for qo in (0, 256))
        hbkeys = tuple(f"hbf/{c}" for c in range(UPCH))

        def emit_ffn_half(hb, last_dn_ps, last_dn_r):
            CH0 = QOFF + hb * 512

            ub2 = mem.push("sr", "ub2", [128, 2, 256], F32, fence_marks=last_dn_r)
            xh = mem.push("sr", "xh", [128, 18, 256], F32, fence_marks=last_dn_r)
            ps_x1 = mem.push("p", "ps_x1", [128, 2, 512], F32, fence_marks=last_dn_ps)
            ps_tp = mem.push("p", "ps_tp", [128, 4, 512], F32, fence_marks=last_dn_ps)

            for ql in range(2):
                T0 = CH0 + ql * 256
                slots = list(range(16))
                spare = [16, 17]
                if ql == 0 and hb == 0:
                    ckpt("D2")
                for c in range(NCH):
                    bi = c % 2
                    E(Q, lambda g, c=c, bi=bi, T0=T0: g.dma_start(
                        out=ub2[:, bi, :], in_=u_T[c * 128:(c + 1) * 128, T0:T0 + 256]),
                      writes=(f"ub2/{bi}",), qsem=f"qu{bi}")
                    if c == 0 and ql == 0 and hb == 0:
                        ckpt("D3")
                    E("T", lambda g, bi=bi: g.matmul(ps_x1[:, bi, 0:256], P["h128f"][:], ub2[:, bi, :],
                                                     start=True, stop=True),
                      reads=(f"ub2/{bi}", "h128f"), writes=(f"ps_x1/{bi}",))
                    if c == 0 and ql == 0 and hb == 0:
                        ckpt("D4")
                    E("S", lambda g, c=c, bi=bi: g.activation(out=xh[:, c, :], in_=ps_x1[:, bi, 0:256],
                                                              func=AF.Identity, bias=0.0,
                                                              scale=P["rsc2048"][:, 0:1]),
                      reads=(f"ps_x1/{bi}", "rsc2048"), writes=(f"xh/{c}",))
                    if c == 1 and ql == 0 and hb == 0:
                        ckpt("D5")
                    if c == 3 and ql == 0 and hb == 0:
                        ckpt("D6")
                    if c == 7 and ql == 0 and hb == 0:
                        ckpt("D7")
                    if c == 15 and ql == 0 and hb == 0:
                        ckpt("D8")
                if ql == 0 and hb == 0:
                    ckpt("E0a")
                butterflies(xh, 16, slots, spare, "xh", 256)

                E("V", lambda g, T0=T0: g.tensor_tensor(
                    out=qr2[0:1, 512:768], in0=P["mrow"][0:1, T0:T0 + 256],
                    in1=P["sc2048"][0:1, 0:1].broadcast_to([1, 256]), op=ALU.mult),
                  reads=("mrow", "sc2048"), writes=("qr2/corr",))
                E("V", lambda g, s0=slots[0], xh=xh: g.tensor_sub(
                    out=xh[0:1, s0, :], in0=xh[0:1, s0, :], in1=qr2[0:1, 512:768]),
                  reads=(f"xh/{slots[0]}", "qr2/corr"), writes=(f"xh/{slots[0]}",))
                if ql == 0 and hb == 0:
                    ckpt("E0b")

                def out_q(c, sl, ql=ql):
                    eng = "V"
                    sp = spare[0]
                    E(eng, lambda g, sl=sl, sp=sp: g.tensor_tensor(
                        out=xh[:, sp, :].bitcast(I32), in0=xh[:, sl, :], in1=sxdq[:, 0, :],
                        op=ALU.mult),
                      reads=(f"xh/{sl}", "sxdq/0"), writes=(f"xh/{sp}",))
                    E(eng, lambda g, c=c, sp=sp, ql=ql: g.tensor_tensor(
                        out=xq[:, c, ql * 256:(ql + 1) * 256], in0=xh[:, sp, :].bitcast(I32),
                        in1=sxdq[:, 1, :], op=ALU.mult),
                      reads=(f"xh/{sp}", "sxdq/1"), writes=(f"xq/{c}/{ql}",))

                quant_tile(xh, 16, slots, spare, "xh", 256, ps_tp, "ps_tp", T0, out_q, dbg=(ql == 0 and hb == 0))
                if ql == 0 and hb == 0:
                    ckpt("E1")

            if hb == 0:
                ckpt("EA")
            if "xq" in tap_out and hb == 0:
                E(Q, lambda g: g.dma_start(out=tap_out["xq"][:, :, :], in_=xq[:, :, :]), reads=xqkeys, qsem="qout")

            f_tp = mem.pop("p", "ps_tp")
            f_x1 = mem.pop("p", "ps_x1")
            f_xh = mem.pop("sr", "xh") + mem.pop("sr", "ub2")

            wst3 = mem.push("sr", "wst3", [128, NCH * 128], F32, fence_marks=f_xh)
            wtb = mem.push("sr", "wtb", [128, 2, NCH * 128], BF16, fence_marks=f_xh)
            sgn = mem.push("sr", "sgn", [128, NCH * 128], BF16, fence_marks=f_xh)
            rsc2 = mem.push("sr", "rsc2", [128, 2, 512], F32, fence_marks=f_xh)
            ps_up = mem.push("p", "ps_up", [128, 2, 512], F32, fence_marks=f_x1 + f_tp)

            for oc in range(UPCH):
                wb = oc % 2
                if hb == 0:
                    E(Q, lambda g, oc=oc: g.dma_start(out=wst3[:, :], in_=wup[oc, :, :]), writes=("wst3",), qsem="qw")
                    E("V", lambda g: g.tensor_scalar(out=sgn[:, :], in0=wst3[:, :],
                                                     scalar1=P["nthrup"][:, 0:1], scalar2=None,
                                                     op0=ALU.is_lt),
                      reads=("wst3", "nthrup"), writes=("sgn",))
                    E("V", lambda g, wb=wb: g.tensor_scalar(out=wtb[:, wb, :], in0=wst3[:, :],
                                                            scalar1=P["thrup"][:, 0:1], scalar2=None,
                                                            op0=ALU.is_gt),
                      reads=("wst3", "thrup"), writes=(f"wtb/{wb}",))
                    E("V", lambda g, wb=wb: g.tensor_sub(out=wtb[:, wb, :], in0=wtb[:, wb, :],
                                                         in1=sgn[:, :]),
                      reads=("sgn", f"wtb/{wb}"), writes=(f"wtb/{wb}",))
                    E(Q, lambda g, oc=oc, wb=wb: g.dma_start(out=wup_cache[oc, :, :], in_=wtb[:, wb, :]),
                      reads=(f"wtb/{wb}",), writes=(f"wupc/{oc}",), qsem="qc")
                else:
                    E(Q, lambda g, oc=oc, wb=wb: g.dma_start(out=wtb[:, wb, :], in_=wup_cache[oc, :, :]),
                      reads=(f"wupc/{oc}",), writes=(f"wtb/{wb}",), qsem=f"qt{wb}")
                pp = oc % 2
                for ql in range(2):
                    for kc in range(NCH):
                        E("T", lambda g, wb=wb, kc=kc, pp=pp, ql=ql: g.matmul(
                            ps_up[:, pp, ql * 256:(ql + 1) * 256],
                            wtb[:, wb, kc * 128:(kc + 1) * 128],
                            xq[:, kc, ql * 256:(ql + 1) * 256],
                            start=(kc == 0), stop=(kc == NCH - 1)),
                          reads=(f"wtb/{wb}", f"xq/{kc}/{ql}"), writes=(f"ps_up/{pp}",))
                E("S", lambda g, pp=pp: g.activation(out=rsc2[:, pp, :], in_=ps_up[:, pp, :],
                                                     func=AF.Relu, bias=0.0, scale=1.0),
                  reads=(f"ps_up/{pp}",), writes=(f"rsc2/{pp}",))
                E("V", lambda g, oc=oc, pp=pp: g.tensor_mul(out=hbf[:, oc, :], in0=rsc2[:, pp, :],
                                                            in1=rsc2[:, pp, :]),
                  reads=(f"rsc2/{pp}",), writes=(f"hbf/{oc}",))

            if hb == 0:
                trk.settle("qc", tuple(f"wupc/{oc}" for oc in range(UPCH)))
                ckpt("EB")
            if "h" in tap_out and hb == 0:
                E(Q, lambda g: g.dma_start(out=tap_out["h"][:, :, :], in_=hbf[:, :, :]), reads=hbkeys, qsem="qout")

            f_up = mem.pop("p", "ps_up")
            f_s2 = mem.pop("sr", "rsc2") + mem.pop("sr", "sgn") + mem.pop("sr", "wtb") + \
                mem.pop("sr", "wst3")

            xh2 = mem.push("sr", "xh2", [128, 34, 256], F32, fence_marks=f_s2)
            ps_x2 = mem.push("p", "ps_x2", [128, 2, 512], F32, fence_marks=f_up)
            ps_t2 = mem.push("p", "ps_t2", [128, 4, 512], F32, fence_marks=f_up)

            for ql in range(2):
                QO = ql * 256
                slots2 = list(range(32))
                spare2 = [32, 33]
                for c in range(UPCH):
                    bi = c % 2
                    g16 = (c // 16) * 16
                    for j2 in range(16):
                        neg16 = bin((c % 16) & j2).count("1") & 1
                        wt = "h128nbf" if neg16 else "h128bf"
                        E("T", lambda g, bi=bi, QO=QO, wt=wt, j2=j2, src=g16 + j2: g.matmul(
                            ps_x2[:, bi, 0:256], P[wt][:], hbf[:, src, QO:QO + 256],
                            start=(j2 == 0), stop=(j2 == 15)),
                          reads=(f"hbf/{src}", wt), writes=(f"ps_x2/{bi}",))
                    E("S", lambda g, c=c, bi=bi: g.activation(out=xh2[:, c, :], in_=ps_x2[:, bi, 0:256],
                                                              func=AF.Copy, bias=0.0, scale=0.015625),
                      reads=(f"ps_x2/{bi}",), writes=(f"xh2/{c}",))

                butterflies(xh2, 32, slots2, spare2, "xh2", 256, s0=16)

                def out_h(c, sl, ql=ql):
                    eng = "V"
                    sp = spare2[0]
                    E(eng, lambda g, sl=sl, sp=sp: g.tensor_tensor(
                        out=xh2[:, sp, :].bitcast(I32), in0=xh2[:, sl, :], in1=sxdq[:, 0, :],
                        op=ALU.mult),
                      reads=(f"xh2/{sl}", "sxdq/0"), writes=(f"xh2/{sp}",))
                    E(eng, lambda g, c=c, sp=sp, ql=ql: g.tensor_tensor(
                        out=hq[:, c, ql * 256:(ql + 1) * 256], in0=xh2[:, sp, :].bitcast(I32),
                        in1=sxdq[:, 1, :], op=ALU.mult),
                      reads=(f"xh2/{sp}", "sxdq/1"), writes=(f"hq/{c}/{ql * 256}",))

                quant_tile(xh2, 32, slots2, spare2, "xh2", 256, ps_t2, "ps_t2", None, out_h)
                if ql == 0 and hb == 0:
                    ckpt("E2")

            if hb == 0:
                ckpt("EC")
            if "hq" in tap_out and hb == 0:
                E(Q, lambda g: g.dma_start(out=tap_out["hq"][:, :, :], in_=hq[:, :, :]), reads=hqkeys, qsem="qout")

            f_t2 = mem.pop("p", "ps_t2")
            f_x2 = mem.pop("p", "ps_x2")
            f_xh2 = mem.pop("sr", "xh2")

            wst4 = mem.push("sr", "wst4", [128, UPCH * 128], F32, fence_marks=f_xh2)
            wtb2 = mem.push("sr", "wtb2", [128, 2, UPCH * 128], BF16, fence_marks=f_xh2)
            sgn2 = mem.push("sr", "sgn2", [128, UPCH * 128], BF16, fence_marks=f_xh2)
            ps_dn = mem.push("p", "ps_dn", [128, 2, 512], F32, fence_marks=f_x2 + f_t2)

            for oc in range(NCH):
                wb = oc % 2
                if hb == 0:
                    E(Q, lambda g, oc=oc: g.dma_start(out=wst4[:, :], in_=wdn[oc, :, :]), writes=("wst4",), qsem="qw")
                    E("V", lambda g: g.tensor_scalar(out=sgn2[:, :], in0=wst4[:, :],
                                                     scalar1=P["nthrdn"][:, 0:1], scalar2=None,
                                                     op0=ALU.is_lt),
                      reads=("wst4", "nthrdn"), writes=("sgn2",))
                    E("V", lambda g, wb=wb: g.tensor_scalar(out=wtb2[:, wb, :], in0=wst4[:, :],
                                                            scalar1=P["thrdn"][:, 0:1], scalar2=None,
                                                            op0=ALU.is_gt),
                      reads=("wst4", "thrdn"), writes=(f"wtb2/{wb}",))
                    E("V", lambda g, wb=wb: g.tensor_sub(out=wtb2[:, wb, :], in0=wtb2[:, wb, :],
                                                         in1=sgn2[:, :]),
                      reads=("sgn2", f"wtb2/{wb}"), writes=(f"wtb2/{wb}",))
                    E(Q, lambda g, oc=oc, wb=wb: g.dma_start(out=wdn_cache[oc, :, :], in_=wtb2[:, wb, :]),
                      reads=(f"wtb2/{wb}",), writes=(f"wdnc/{oc}",), qsem="qc")
                else:
                    E(Q, lambda g, oc=oc, wb=wb: g.dma_start(out=wtb2[:, wb, :], in_=wdn_cache[oc, :, :]),
                      reads=(f"wdnc/{oc}",), writes=(f"wtb2/{wb}",), qsem=f"qt{wb}")
                pp = oc % 2
                for ql in range(2):
                    for kc in range(UPCH):
                        E("T", lambda g, wb=wb, kc=kc, pp=pp, ql=ql: g.matmul(
                            ps_dn[:, pp, ql * 256:(ql + 1) * 256],
                            wtb2[:, wb, kc * 128:(kc + 1) * 128],
                            hq[:, kc, ql * 256:(ql + 1) * 256],
                            start=(kc == 0), stop=(kc == UPCH - 1)),
                          reads=(f"wtb2/{wb}", f"hq/{kc}/{ql * 256}"), writes=(f"ps_dn/{pp}",))
                E("V", lambda g, oc=oc, pp=pp, hb=hb: g.tensor_add(
                    out=acc[:, oc, hb * 512:(hb + 1) * 512], in0=acc[:, oc, hb * 512:(hb + 1) * 512],
                    in1=ps_dn[:, pp, :]),
                  reads=(f"ps_dn/{pp}", f"acc/{oc}"), writes=(f"acc/{oc}", f"accf/{oc}/{hb}"))

            if hb == 0:
                trk.settle("qc", tuple(f"wdnc/{oc}" for oc in range(NCH)))
                ckpt("ED")
            last_dn_ps = mem.pop("p", "ps_dn")
            last_dn_r = mem.pop("sr", "sgn2") + mem.pop("sr", "wtb2") + mem.pop("sr", "wst4")
            return last_dn_ps, last_dn_r

        for hb in range(2):
            last_dn_ps, last_dn_r = emit_ffn_half(hb, last_dn_ps, last_dn_r)

        for c in range(NCH):
            E(Q, lambda g, c=c: g.dma_start(out=out_T[c * 128:(c + 1) * 128, :], in_=acc[:, c, :]),
              reads=(f"acc/{c}", f"accf/{c}/0", f"accf/{c}/1"), qsem="qout")


    except _Stop:
        pass


    es = contextlib.ExitStack()
    for e in ENGS:
        trk.sems[e] = es.enter_context(nc.semaphore(f"sem{e}"))
    for qn in trk.qcnt:
        trk.sems[qn] = es.enter_context(nc.semaphore(f"sem_{qn}"))
    qout_total = trk.qcnt.get("qout", 0)
    with es, nc.Block() as block:
        @block.sync
        def _(eng):
            for f in trk.prog["Q"]:
                f(eng)
            if qout_total:
                eng.wait_ge(trk.sems["qout"], qout_total)

        @block.tensor
        def _(eng):
            for f in trk.prog["T"]:
                f(eng)

        @block.scalar
        def _(eng):
            for f in trk.prog["S"]:
                f(eng)

        @block.vector
        def _(eng):
            for f in trk.prog["V"]:
                f(eng)

        @block.gpsimd
        def _(eng):
            for f in trk.prog["P"]:
                f(eng)

    return nc, tap_out


_CACHE = {}


def _get_program(taps=()):
    key = tuple(sorted(taps))
    if key not in _CACHE:
        _CACHE[key] = build_program(taps)
    return _CACHE[key]


def host_inputs(L0_attn_u, L0_attn_Wqkv, L0_attn_bqkv, L0_attn_Wo, L0_attn_bo,
                L0_attn_sinks, L0_ffn_w_up, L0_ffn_w_down):
    wqkv_s = _strips(np.asarray(L0_attn_Wqkv, np.float32), QKV_OC, NCH)
    wo_s = _strips(np.asarray(L0_attn_Wo, np.float32), NCH, NCH)
    wup_s = _strips(np.asarray(L0_ffn_w_up, np.float32), UPCH, NCH)
    wdn_s = _strips(np.asarray(L0_ffn_w_down, np.float32), NCH, UPCH)
    bq_t = np.ascontiguousarray(np.asarray(L0_attn_bqkv, np.float32).reshape(QKV_OC, 128).T)
    bo_t = np.ascontiguousarray(np.asarray(L0_attn_bo, np.float32).reshape(NCH, 128).T)
    sinks = np.asarray(L0_attn_sinks, np.float32)
    s4 = sinks.reshape(16, 2, NS, DH)
    sinkT = np.ascontiguousarray(s4.transpose(1, 3, 0, 2).reshape(128, 16, NS))
    sinkV = np.ascontiguousarray(sinks.transpose(1, 0, 2).reshape(NS, NH * DH))

    h128 = _hadamard128()
    consts = np.zeros((128, 8), np.float32)
    for j, v in enumerate([0.0, 1.0, 1e-5, -0.5, 2048.0, 0.015625]):
        consts[:, j] = v

    shared = dict(
        wqkv=wqkv_s, bqkv_t=bq_t, wo=wo_s, bo_t=bo_t, wup=wup_s, wdn=wdn_s,
        sinkT=sinkT, sinkV=sinkV,
        h128bf=h128.astype(ml_dtypes.bfloat16), h128f=h128,
        idenbf=np.eye(128).astype(ml_dtypes.bfloat16), idenf=np.eye(128, dtype=np.float32),
        pswap=_pswap().astype(ml_dtypes.bfloat16),
        consts=consts, onesf=np.ones((1, 128), np.float32), onesc=np.ones((128, 1), np.float32),
    )

    u = np.asarray(L0_attn_u, np.float32)
    in_maps = []
    for core in range(8):
        b, half = core // 2, core % 2
        qstart = half * 1024
        u_Tc = np.zeros((D, TKV), np.float32)
        lo, hi = qstart - 64, qstart - 64 + TKV
        slo, shi = max(lo, 0), min(hi, L)
        u_Tc[:, slo - lo:shi - lo] = u[b, slo:shi, :].T
        cq, sq, ck, sk = _trig_tables(qstart)
        m0, m1 = _masks(qstart)
        m = dict(shared)
        m.update(u_T=np.ascontiguousarray(u_Tc), cq=cq, sq=sq, ck=ck, sk=sk,
                 mask0=m0, mask1=m1)
        in_maps.append(m)
    return in_maps


def kernel(**inputs):
    nc, _ = _get_program()
    in_maps = host_inputs(**inputs)
    res = run_bass_kernel_spmd(nc, in_maps, core_ids=list(range(8)))
    out = np.empty((B, L, D), np.float32)
    for core in range(8):
        b, half = core // 2, core % 2
        out[b, half * 1024:(half + 1) * 1024, :] = np.asarray(res.results[core]["out_T"]).T
    return out
